# version 23
# baseline (speedup 1.0000x reference)
import numpy as np
import ml_dtypes
from contextlib import ExitStack

import concourse.bass as bass
import concourse.mybir as mybir
from concourse.bass_utils import run_bass_kernel_spmd

F32 = mybir.dt.float32
BF16 = mybir.dt.bfloat16
AF = mybir.ActivationFunctionType
ALU = mybir.AluOpType
AX = mybir.AxisListType

NCORES = 8
B = 2
TB = 2048
CH = 8
P = 128
DM = 1024
FF = 4096
TT = 512
NT = TB // TT
NCK = TB // 128
JP = 8
MAGIC = 12582912.0
NEGBIG = -1.0e9
LN_EPS = 1e-5
QEPS = 1e-8
RG = [list(range(NCORES))]

DEBUG = False


class Chan:
    def __init__(self, es, nc, name, step=1):
        self.h = es.enter_context(nc.semaphore(name))
        self.n = 0
        self.step = step
        self.pending_epoch = False

    def inc(self, instr):
        self.n += self.step
        instr.then_inc(self.h, self.step)
        return self.n

    def tok(self):
        return (self, self.n)


def build_program():
    nc = bass.Bass("TRN2", target_bir_lowering=False)
    es = ExitStack()
    S = es.enter_context

    def din(name, shape, dtype=F32):
        return nc.dram_tensor(name, shape, dtype, kind="ExternalInput").ap()

    def dint(name, shape, dtype=F32):
        return nc.dram_tensor(name, shape, dtype, kind="Internal").ap()

    x_d = din("x", [B, CH, P, TB])
    wkT_d = din("wkT", [DM, DM])
    wvT_d = din("wvT", [DM, DM])
    wgT_d = din("wgT", [DM, DM])
    wpT_d = din("wpT", [DM, DM])
    wmod_d = din("wmod", [DM, 6 * DM])
    cT_d = din("cT", [P, CH, B])
    w1s_d = din("w1s", [512, DM])
    w2s_d = din("w2s", [P, FF])
    b1t_d = din("b1t", [P, 32])
    b2t_d = din("b2t", [P, CH])
    qpk_d = din("qpk", [P, JP, 64])
    cost_d = din("cost", [P, NCK * 64])
    sint_d = din("sint", [P, NCK * 64])
    mneg_d = din("mneg", [P, 64])
    oneblk_d = din("oneblk", [P, 2])
    blk2_d = din("blk2", [2, P])
    ident_d = din("ident", [P, P])
    onescol_d = din("onescol", [P, 1])
    onesrow_d = din("onesrow", [1, TT])
    kconst_d = din("kconst", [P, 4])

    y_d = nc.dram_tensor("y", [B, CH, P, TB], F32, kind="ExternalOutput").ap()

    h2_dram = dint("h2sp", [B, P, CH, TB])
    x2_dram = dint("x2sp", [B, P, CH, TB])
    u_dram = dint("usp", [8, 4, P, CH, TT])
    rowb_dram = dint("rowb", [2, TT])
    colb_dram = dint("colb", [P])
    qb2_dram = dint("qb2b", [CH, P])
    agi_qw1 = dint("agi_qw1", [512, DM], BF16)
    ago_qw1 = dint("ago_qw1", [FF, DM], BF16)
    agi_qw2 = dint("agi_qw2", [P, FF], BF16)
    ago_qw2 = dint("ago_qw2", [DM, FF], BF16)
    agi_sc = dint("agi_sc", [5, P])
    ago_sc = dint("ago_sc", [40, P])
    ar1_in = dint("ar1_in", [1, 1])
    ar1_out = dint("ar1_out", [1, 1])
    ar2_in = dint("ar2_in", [1, 1])
    ar2_out = dint("ar2_out", [1, 1])

    BIGA = S(nc.sbuf_tensor("BIGA", [P, 20480], F32))
    SMA = S(nc.sbuf_tensor("SMA", [P, 12288], F32))
    STR = S(nc.sbuf_tensor("STR", [P, 8192], F32))
    SCR = S(nc.sbuf_tensor("SCR", [P, 4096], F32))

    BIGAb = BIGA.ap().bitcast(BF16)
    SMAb = SMA.ap().bitcast(BF16)
    STRf = STR.ap()
    SCRb = SCR.ap().bitcast(BF16)

    vK = BIGAb[:, 0:8192].rearrange("p (j t) -> p j t", j=CH)
    vV = BIGAb[:, 8192:24576].rearrange("p (n f) -> p n f", n=NCK)
    vG = BIGAb[:, 24576:40960].rearrange("p (j t) -> p j t", j=CH)
    vW = SMAb[:, 0:8192].rearrange("p (j f) -> p j f", j=CH)
    vW2 = BIGAb[:, 32768:40960].rearrange("p (j f) -> p j f", j=CH)
    vH = SMAb[:, 8192:24576].rearrange("p (s j t) -> p s j t", s=NT, j=CH)
    vXTd = [SMA.ap()[:, 4096:8192].rearrange("p (j t) -> p j t", j=CH),
            SMA.ap()[:, 8192:12288].rearrange("p (j t) -> p j t", j=CH)]
    vX2d = [STRf[:, 0:4096].rearrange("p (j t) -> p j t", j=CH),
            STRf[:, 4096:8192].rearrange("p (j t) -> p j t", j=CH)]
    vXBF0 = SCRb[:, 0:4096].rearrange("p (j t) -> p j t", j=CH)
    vXBF1 = STR.ap().bitcast(BF16)[:, 0:4096].rearrange("p (j t) -> p j t", j=CH)
    vSM = SCR.ap()[:, 0:1024]
    vESC = SCRb[:, 2048:3072]
    vPRB = SCRb[:, 3072:4096]
    vRDN = SCR.ap()[0:2, 2048:3072]
    vRDS = SCRb[0:2, 6144:7168]
    vWT = SCRb[:, 0:4096].rearrange("p (j t) -> p j t", j=CH)

    vMOD = BIGA.ap()[0:2, 0:6144]
    vWQF = BIGA.ap()[:, 8192:12288].rearrange("p (f c) -> p f c", f=4)
    vWQT = BIGA.ap()[:, 12288:13312]
    vWMB0 = BIGA.ap()[:, 13312:15360]
    vWMB1 = BIGA.ap()[:, 18432:20480]
    vWQQ = BIGAb[:, 32768:36864].rearrange("p (f c) -> p f c", f=4)
    vW2F = vWQF.rearrange("p f c -> p (f c)")
    vW2Q = vWQQ.rearrange("p f c -> p (f c)")

    vQW1 = BIGAb[:, 0:32768].rearrange("p (j f) -> p j f", j=CH)
    vQW2 = BIGAb[:, 0:32768].rearrange("p (j f) -> p j f", j=32)
    vQ2a = BIGAb[:, 32768:40960].rearrange("p (j t) -> p j t", j=32)
    vQ2b = SMAb[:, 16384:24576].rearrange("p (j t) -> p j t", j=32)
    vH2T = SMA.ap()[:, 0:8192].rearrange("p (s j t) -> p s j t", s=2, j=CH)
    vQH = SMAb[:, 16384:24576].rearrange("p (s j t) -> p s j t", s=2, j=CH)
    vUST = STRf.rearrange("p (s j t) -> p s j t", s=2, j=CH)
    vUIN = SMA.ap()[:, 0:8192].rearrange("p (j t) -> p j t", j=32)
    vX2F = STRf[:, 0:4096].rearrange("p (s j t) -> p s j t", s=2, j=CH)
    vX3 = STRf[:, 4096:8192].rearrange("p (s j t) -> p s j t", s=2, j=CH)
    vQB2R = SCR.ap()[0:1, 0:1024]

    kraw = S(nc.sbuf_tensor("kraw", [P, 512], BF16))
    kswp = S(nc.sbuf_tensor("kswp", [P, 512], BF16))
    kt1 = S(nc.sbuf_tensor("kt1", [P, 512], F32))
    kt2 = S(nc.sbuf_tensor("kt2", [P, 512], F32))
    apt = S(nc.sbuf_tensor("apt", [P, 2 * TT], F32))
    apt2 = S(nc.sbuf_tensor("apt2", [P, 2 * TT], F32))
    r_m = SCR.ap()[0:1, 2048:2560]
    r_var = SCR.ap()[0:1, 2560:3072]
    r_rstd = SCR.ap()[0:1, 3072:3584]
    r_negmr = SCR.ap()[0:1, 3584:4096]
    egt = S(nc.sbuf_tensor("egt", [P, TT], BF16))
    rdgf = S(nc.sbuf_tensor("rdgf", [2, TT], F32))
    rdg = S(nc.sbuf_tensor("rdg", [2, TT], BF16))
    attn_sb = S(nc.sbuf_tensor("attn_sb", [P, JP, 64], BF16))
    sapm = S(nc.sbuf_tensor("sapm", [P, CH, B], F32))
    capm = S(nc.sbuf_tensor("capm", [P, CH, B], F32))
    gapm = S(nc.sbuf_tensor("gapm", [P, CH, B], F32))
    sbpm = S(nc.sbuf_tensor("sbpm", [P, CH, B], F32))
    cbpm = S(nc.sbuf_tensor("cbpm", [P, CH, B], F32))
    gbpm = S(nc.sbuf_tensor("gbpm", [P, CH, B], F32))
    am1_l = S(nc.sbuf_tensor("am1_l", [P, 4], F32))
    rw1_l = S(nc.sbuf_tensor("rw1_l", [P, 4], F32))
    am2_l = S(nc.sbuf_tensor("am2_l", [P, 1], F32))
    rw2_l = S(nc.sbuf_tensor("rw2_l", [P, 1], F32))
    sc5 = S(nc.sbuf_tensor("sc5", [5, P], F32))
    sc40 = S(nc.sbuf_tensor("sc40", [40, P], F32))
    scT = S(nc.sbuf_tensor("scT", [P, 40], F32))
    am1pm = S(nc.sbuf_tensor("am1pm", [P, 32], F32))
    am2pm = S(nc.sbuf_tensor("am2pm", [P, CH], F32))
    dq1pm = S(nc.sbuf_tensor("dq1pm", [P, 32], F32))
    qb1pm = S(nc.sbuf_tensor("qb1pm", [P, 32], F32))
    qbdq1 = S(nc.sbuf_tensor("qbdq1", [P, 32], F32))
    dq2pm = S(nc.sbuf_tensor("dq2pm", [P, CH], F32))
    qb2pm = S(nc.sbuf_tensor("qb2pm", [P, CH], F32))
    gbdq2 = S(nc.sbuf_tensor("gbdq2", [P, CH, B], F32))
    qb2hi = SCRb[0:1, 2048:3072]
    qb2lo = SCRb[0:1, 3072:4096]
    t32a = S(nc.sbuf_tensor("t32a", [P, 32], F32))
    t32b = S(nc.sbuf_tensor("t32b", [P, 32], F32))
    hmax = S(nc.sbuf_tensor("hmax", [P, 1], F32))
    umax = S(nc.sbuf_tensor("umax", [P, 1], F32))
    tmax = S(nc.sbuf_tensor("tmax", [P, 1], F32))
    row128 = S(nc.sbuf_tensor("row128", [1, P], F32))
    gm1 = S(nc.sbuf_tensor("gm1", [1, 1], F32))
    gm2 = S(nc.sbuf_tensor("gm2", [1, 1], F32))
    r1row = S(nc.sbuf_tensor("r1row", [1, 2], F32))
    r2row = S(nc.sbuf_tensor("r2row", [1, 2], F32))
    r1pm = S(nc.sbuf_tensor("r1pm", [P, 2], F32))
    r2pm = S(nc.sbuf_tensor("r2pm", [P, 2], F32))
    cb_qpk = S(nc.sbuf_tensor("cb_qpk", [P, JP, 64], BF16))
    cb_cos = S(nc.sbuf_tensor("cb_cos", [P, NCK * 64], BF16))
    cb_sin = S(nc.sbuf_tensor("cb_sin", [P, NCK * 64], BF16))
    cb_mneg = S(nc.sbuf_tensor("cb_mneg", [P, 64], BF16))
    cb_ob = S(nc.sbuf_tensor("cb_ob", [P, 2], BF16))
    cb_b2 = S(nc.sbuf_tensor("cb_b2", [2, P], BF16))
    cb_b2f = S(nc.sbuf_tensor("cb_b2f", [2, P], F32))
    cb_id = S(nc.sbuf_tensor("cb_id", [P, P], F32))
    cb_o1b = S(nc.sbuf_tensor("cb_o1b", [P, 1], BF16))
    cb_o1cf = S(nc.sbuf_tensor("cb_o1cf", [P, 1], F32))
    cb_o1f = S(nc.sbuf_tensor("cb_o1f", [1, P], F32))
    cb_orb = S(nc.sbuf_tensor("cb_orb", [1, 256], BF16))
    cb_kc = S(nc.sbuf_tensor("cb_kc", [P, 4], F32))
    cb_cT = S(nc.sbuf_tensor("cb_cT", [P, CH, B], F32))
    silc = S(nc.sbuf_tensor("silc", [P, CH * B], F32))
    cb_b1t = S(nc.sbuf_tensor("cb_b1t", [P, 32], F32))
    cb_b2t = S(nc.sbuf_tensor("cb_b2t", [P, CH], F32))

    pbA = S(nc.psum_tensor("pbA", [P, 1024], F32))
    pbB = S(nc.psum_tensor("pbB", [P, 1024], F32))
    pbC = S(nc.psum_tensor("pbC", [P, 512], F32))
    pbD = S(nc.psum_tensor("pbD", [P, 512], F32))
    pbE = S(nc.psum_tensor("pbE", [P, 512], F32))
    pbF = S(nc.psum_tensor("pbF", [P, 512], F32))

    chX = Chan(es, nc, "chX", 16)
    chSP = Chan(es, nc, "chSP", 16)
    chOUT = Chan(es, nc, "chOUT", 16)
    chBC = Chan(es, nc, "chBC", 16)
    chW = Chan(es, nc, "chW", 16)
    chUL = Chan(es, nc, "chUL", 16)
    chGC = Chan(es, nc, "chGC", 16)
    chVD = Chan(es, nc, "chVD", 16)
    chCC = Chan(es, nc, "chCC", 16)
    cCC = Chan(es, nc, "cCC", 1)
    cPE = Chan(es, nc, "cPE", 1)
    cDV = Chan(es, nc, "cDV", 1)
    cAC = Chan(es, nc, "cAC", 1)
    cGP = Chan(es, nc, "cGP", 1)

    nc.const_aps.register(F32, -MAGIC, cb_kc[:, 3:4])
    block = S(nc.Block())

    def on(eng, f):
        getattr(block, eng)(f)

    def W(e, *tokens):
        for t in tokens:
            if t is None:
                continue
            c, v = t
            if v > 0:
                e.wait_ge(c.h, v)

    def dma(ch, q, out, in_):
        if ch.pending_epoch:
            q.wait_ge(ch.h, ch.n)
            ch.pending_epoch = False
        return ch.inc(q.dma_start(out=out, in_=in_))

    def use(ch):
        ch.pending_epoch = True
        return (ch, ch.n)

    bankfree = {}

    def bw(e, *names):
        for nm in names:
            t = bankfree.get(nm)
            if t is not None:
                W(e, t)

    def a_loads(sy):
        for sb_t, d in [(cb_cos, None), (cb_sin, None)]:
            pass
        for sb_t, d in [(cb_id, ident_d), (cb_b2f, blk2_d),
                        (cb_o1f, onesrow_d[:, 0:P]), (cb_o1cf, onescol_d), (cb_kc, kconst_d),
                        (cb_cT, cT_d), (cb_b1t, b1t_d), (cb_b2t, b2t_d)]:
            dma(chX, sy, sb_t.ap(), d)
    on("sync", a_loads)
    tok_cst = use(chX)

    def a_casts(ge):
        for sb_t, d in [(cb_qpk, qpk_d), (cb_mneg, mneg_d), (cb_ob, oneblk_d), (cb_b2, blk2_d),
                        (cb_o1b, onescol_d), (cb_orb, onesrow_d[:, 0:256]),
                        (cb_cos, cost_d), (cb_sin, sint_d)]:
            dma(chGC, ge, sb_t.ap(), d)
    on("gpsimd", ge_f := a_casts)
    tok_cc = use(chGC)

    def a_w1s(seq):
        dma(chW, seq, vWQF, w1s_d.rearrange("(ft p) c -> p ft c", p=P))
    on("scalar", a_w1s)
    tok_w1s = use(chW)

    def b_am1(ve):
        W(ve, tok_w1s)
        cDV.inc(ve.tensor_reduce(out=am1_l.ap(), in_=vWQF, axis=AX.X,
                                 op=ALU.max, apply_absolute_value=True))
        ve.wait_ge(cDV.h, cDV.n)
        cDV.inc(ve.tensor_scalar_max(am1_l.ap(), am1_l.ap(), QEPS))
        ve.wait_ge(cDV.h, cDV.n)
        cDV.inc(ve.reciprocal(rw1_l.ap(), am1_l.ap()))
        ve.wait_ge(cDV.h, cDV.n)
        cDV.inc(ve.tensor_scalar_mul(rw1_l.ap(), rw1_l.ap(), 127.0))
    on("vector", b_am1)
    v_am1 = cDV.tok()

    v_qts = None
    for ft in range(4):
        def b_qa(se, ft=ft, prev=v_qts):
            W(se, v_am1, tok_cst, prev)
            cAC.inc(se.activation(vWQT, vWQF[:, ft, :], AF.Copy,
                                  bias=MAGIC, scale=rw1_l[:, ft:ft + 1]))
        on("scalar", b_qa)
        a_q = cAC.tok()

        def b_qt(ve, ft=ft):
            W(ve, a_q)
            cDV.inc(ve.tensor_scalar(out=vWQT, in0=vWQT,
                                     scalar1=MAGIC, scalar2=-128.0,
                                     op0=ALU.subtract, op1=ALU.max))
            ve.wait_ge(cDV.h, cDV.n)
            cDV.inc(ve.tensor_scalar_min(vWQQ[:, ft, :], vWQT, 127.0))
        on("vector", b_qt)
        v_qts = cDV.tok()
    v_qw1 = cDV.tok()

    def b_st1(ge):
        W(ge, v_qw1)
        dma(chCC, ge, agi_qw1.rearrange("(ft p) c -> p ft c", p=P), vWQQ)
    on("gpsimd", b_st1)
    tok_agi1 = use(chCC)

    def b_ag1(ge):
        W(ge, tok_agi1)
        cCC.inc(ge.collective_compute("AllGather", ALU.bypass, replica_groups=RG,
                                      ins=[agi_qw1], outs=[ago_qw1]))
    on("gpsimd", b_ag1)
    v_ag1 = cCC.tok()

    def b_w2l(seq):
        W(seq, v_qw1)
        dma(chW, seq, vW2F, w2s_d)
    on("scalar", b_w2l)
    tok_w2s = use(chW)

    def b_am2(ve):
        W(ve, tok_w2s)
        cDV.inc(ve.tensor_reduce(out=am2_l.ap(), in_=vW2F, axis=AX.X,
                                 op=ALU.max, apply_absolute_value=True))
        ve.wait_ge(cDV.h, cDV.n)
        cDV.inc(ve.tensor_scalar_max(am2_l.ap(), am2_l.ap(), QEPS))
        ve.wait_ge(cDV.h, cDV.n)
        cDV.inc(ve.reciprocal(rw2_l.ap(), am2_l.ap()))
        ve.wait_ge(cDV.h, cDV.n)
        cDV.inc(ve.tensor_scalar_mul(rw2_l.ap(), rw2_l.ap(), 127.0))
    on("vector", b_am2)
    v_am2 = cDV.tok()

    def b_qa2(se):
        W(se, v_am2, tok_agi1)
        cAC.inc(se.activation(vW2F, vW2F, AF.Copy, bias=MAGIC,
                              scale=rw2_l[:, 0:1]))
    on("scalar", b_qa2)
    a_q2 = cAC.tok()

    def b_qt2(ve):
        W(ve, a_q2)
        cDV.inc(ve.tensor_scalar(out=vW2F, in0=vW2F, scalar1=MAGIC,
                                 scalar2=-128.0, op0=ALU.subtract, op1=ALU.max))
        ve.wait_ge(cDV.h, cDV.n)
        cDV.inc(ve.tensor_scalar_min(vW2Q, vW2F, 127.0))
    on("vector", b_qt2)
    v_qw2 = cDV.tok()

    def b_st2(ge):
        W(ge, v_qw2)
        dma(chCC, ge, agi_qw2, vW2Q)
    on("gpsimd", b_st2)
    tok_agi2 = use(chCC)

    def b_ag2(ge):
        W(ge, tok_agi2)
        cCC.inc(ge.collective_compute("AllGather", ALU.bypass, replica_groups=RG,
                                      ins=[agi_qw2], outs=[ago_qw2]))
    on("gpsimd", b_ag2)
    v_ag2 = cCC.tok()

    def b_tr1(te):
        W(te, v_am1, v_am2, tok_cst)
        bw(te, "D")
        cPE.inc(te.transpose(pbD[0:4, 0:P], am1_l.ap(), cb_id.ap()))
    on("tensor", b_tr1)
    p_t1 = cPE.tok()

    def b_te1(ve):
        W(ve, p_t1)
        cDV.inc(ve.tensor_copy(sc5[0:4, :], pbD[0:4, 0:P]))
    on("vector", b_te1)
    bankfree["D"] = cDV.tok()

    def b_tr2(te):
        bw(te, "D")
        cPE.inc(te.transpose(pbD[0:1, 0:P], am2_l.ap(), cb_id.ap()))
    on("tensor", b_tr2)
    p_t2 = cPE.tok()

    def b_te2(ve):
        W(ve, p_t2)
        cDV.inc(ve.tensor_copy(row128.ap(), pbD[0:1, 0:P]))
    on("vector", b_te2)
    bankfree["D"] = cDV.tok()
    v_sc5 = cDV.tok()

    def b_te2b(ge):
        W(ge, v_sc5)
        dma(chVD, ge, sc5[4:5, :], row128.ap())
    on("gpsimd", b_te2b)
    tok_sc5r = use(chVD)

    def b_st3(ge):
        W(ge, v_sc5, tok_sc5r)
        dma(chCC, ge, agi_sc, sc5.ap())
    on("gpsimd", b_st3)
    tok_agisc = use(chCC)

    def b_ag3(ge):
        W(ge, tok_agisc)
        cCC.inc(ge.collective_compute("AllGather", ALU.bypass, replica_groups=RG,
                                      ins=[agi_sc], outs=[ago_sc]))
    on("gpsimd", b_ag3)
    v_ag3 = cCC.tok()

    def m_silu(se):
        W(se, tok_cst)
        cAC.inc(se.activation(silc.ap(), cb_cT.ap().rearrange("p j b -> p (j b)"),
                              AF.Silu))
    on("scalar", m_silu)
    a_silu = cAC.tok()

    def m_zero(ve):
        W(ve, tok_agi2)
        cDV.inc(ve.memset(vMOD, 0.0))
    on("vector", m_zero)
    v_mz = cDV.tok()

    wm_prev_pe = [None, None]
    for j in range(CH):
        for q in range(3):
            wb = vWMB0 if (j * 3 + q) % 2 == 0 else vWMB1
            def m_wl(ge, j=j, q=q, prev=wm_prev_pe[(j * 3 + q) % 2]):
                W(ge, prev)
                dma(chGC, ge, wb,
                    wmod_d[j * P:(j + 1) * P, q * 2048:(q + 1) * 2048])
            on("gpsimd", m_wl)
            tok_wm = use(chGC)

            for fi in range(4):
                ftile = q * 4 + fi
                def m_mm(te, j=j, fi=fi, ftile=ftile, wb=wb):
                    W(te, tok_wm, a_silu)
                    if j == 0 and ftile == 0:
                        W(te, v_mz)
                    bw(te, "C")
                    cPE.inc(te.matmul(pbC[0:2, :], silc[:, 2 * j:2 * j + 2],
                                      wb[:, fi * 512:(fi + 1) * 512],
                                      start=True, stop=True))
                on("tensor", m_mm)
                p_mm = cPE.tok()

                def m_acc(ve, ftile=ftile):
                    W(ve, p_mm)
                    sl = vMOD[:, ftile * 512:(ftile + 1) * 512]
                    cDV.inc(ve.tensor_tensor(out=sl, in0=sl, in1=pbC[0:2, :],
                                             op=ALU.add))
                on("vector", m_acc)
                bankfree["C"] = cDV.tok()
            wm_prev_pe[(j * 3 + q) % 2] = cPE.tok()
    v_mod = cDV.tok()
    p_modmm = cPE.tok()

    for v6, dest in [(0, sapm), (1, capm), (2, gapm), (3, sbpm), (4, cbpm),
                     (5, gbpm)]:
        for j in range(CH):
            def m_tr(te, v6=v6, j=j):
                W(te, v_mod)
                bw(te, "D")
                cPE.inc(te.transpose(
                    pbD[0:P, 0:2],
                    vMOD[:, v6 * DM + j * P: v6 * DM + (j + 1) * P],
                    cb_id[0:2, 0:2]))
            on("tensor", m_tr)
            p_t = cPE.tok()

            def m_ev(ve, dest=dest, j=j):
                W(ve, p_t)
                cDV.inc(ve.tensor_copy(dest[:, j, :], pbD[0:P, 0:2]))
            on("vector", m_ev)
            bankfree["D"] = cDV.tok()
    p_modtr = cPE.tok()

    def m_caps(ve):
        cDV.inc(ve.tensor_scalar_add(capm.ap(), capm.ap(), 1.0))
        cDV.inc(ve.tensor_scalar_add(cbpm.ap(), cbpm.ap(), 1.0))
    on("vector", m_caps)

    for g in (gapm, gbpm):
        gv = g.ap().rearrange("p j b -> p (j b)")
        v_pre = cDV.tok()

        def m_abs(se, gv=gv, v_pre=v_pre):
            W(se, v_pre)
            cAC.inc(se.activation(t32b[:, 0:CH * B], gv, AF.Abs))
        on("scalar", m_abs)
        a_abs = cAC.tok()

        def m_cl(ve, gv=gv, a_abs=a_abs):
            W(ve, a_abs)
            cDV.inc(ve.tensor_scalar(out=t32a[:, 0:CH * B], in0=gv,
                                     scalar1=0.0, scalar2=2.0,
                                     op0=ALU.is_ge, op1=ALU.mult))
            ve.wait_ge(cDV.h, cDV.n)
            cDV.inc(ve.tensor_scalar_sub(t32a[:, 0:CH * B], t32a[:, 0:CH * B], 1.0))
            cDV.inc(ve.tensor_scalar_max(t32b[:, 0:CH * B], t32b[:, 0:CH * B], 0.25))
            ve.wait_ge(cDV.h, cDV.n)
            cDV.inc(ve.tensor_tensor(out=gv, in0=t32a[:, 0:CH * B],
                                     in1=t32b[:, 0:CH * B], op=ALU.mult))
            ve.wait_ge(cDV.h, cDV.n)
        on("vector", m_cl)
    v_mpm = cDV.tok()

    early_pe = [p_modtr]
    early_cc = [use(chCC)]

    def c_init(ve):
        cDV.inc(ve.memset(hmax.ap(), -3.0e38))
        cDV.inc(ve.memset(umax.ap(), 0.0))
        cDV.inc(ve.memset(attn_sb.ap(), 0.0))
    on("vector", c_init)

    st = {
        "xbf_free": None,
        "h_free": [None] * NT,
        "k_free": None, "v_free": None, "g_free": None,
        "w_free": None,
        "xt_free": None, "x2_free": None,
        "xt_free2": [None, None], "x2_free2": [None, None],
        "w2_free": None,
        "c5_done": None,
        "wt_free": None,
        "scr_mid_free": None,
        "egt_free": None,
        "attn_free": None,
        "x2sp": [None, None],
        "h2rd": None,
    }
    first_k_evac = [True]
    first_v_evac = [True]
    first_g_evac = [True]

    for b in range(B):
        def c_wk(ge, b=b, wfree=st["w_free"]):
            W(ge, wfree)
            dma(chGC, ge, vW, wkT_d.rearrange("(j p) f -> p j f", p=P))
        on("gpsimd", c_wk)
        tok_wk = use(chGC)

        apply_toks = [None] * NT
        xbf_free = st.get("xbf_free2", [None, None])

        def emit_C1(jt, b=b):
            xb = vXBF0 if jt % 2 == 0 else vXBF1
            def c_xbf1(ge, jt=jt, xb=xb, prev=xbf_free[jt % 2]):
                W(ge, prev)
                if b == 1 and jt == 0:
                    W(ge, st["scr_free_b"])
                if b == 1 and jt == 1:
                    W(ge, *st["str_free_b"])
                dma(chGC, ge, xb,
                    x_d[b, :, :, jt * TT:(jt + 1) * TT].rearrange("j p t -> p j t"))
            on("gpsimd", c_xbf1)
            tok_x1 = use(chGC)

            def c_sum1(te, first=(b == 0 and jt == 0), xb=xb):
                W(te, tok_x1, tok_cc)
                if first:
                    W(te, p_modmm)
                bw(te, "C")
                for j in range(CH):
                    mm = te.matmul(pbC[0:1, :], cb_o1b.ap(), xb[:, j, :],
                                   start=(j == 0), stop=(j == CH - 1))
                    if j == CH - 1:
                        cPE.inc(mm)
            on("tensor", c_sum1)
            p_s1 = cPE.tok()

            def c_sq(ve, xb=xb):
                W(ve, p_s1)
                xf = xb.rearrange("p j t -> p (j t)")
                cDV.inc(ve.tensor_mul(xf, xf, xf))
            on("vector", c_sq)
            v_sq = cDV.tok()

            def c_sum2(te, xb=xb):
                W(te, v_sq)
                bw(te, "D")
                for j in range(CH):
                    mm = te.matmul(pbD[0:1, :], cb_o1b.ap(), xb[:, j, :],
                                   start=(j == 0), stop=(j == CH - 1))
                    if j == CH - 1:
                        cPE.inc(mm)
            on("tensor", c_sum2)
            p_s2 = cPE.tok()

            def c_xbf2(ge, b=b, jt=jt, xb=xb):
                W(ge, p_s2)
                dma(chGC, ge, xb,
                    x_d[b, :, :, jt * TT:(jt + 1) * TT].rearrange("j p t -> p j t"))
            on("gpsimd", c_xbf2)
            tok_x2b = use(chGC)

            def c_rows(ve):
                W(ve, p_s1, p_s2)
                cDV.inc(ve.tensor_scalar_mul(r_m, pbC[0:1, :], 1.0 / DM))
                ve.wait_ge(cDV.h, cDV.n)
                cDV.inc(ve.tensor_tensor(out=r_var, in0=r_m, in1=r_m,
                                         op=ALU.mult))
                ve.wait_ge(cDV.h, cDV.n)
                cDV.inc(ve.scalar_tensor_tensor(out=r_var, in0=pbD[0:1, :],
                                                scalar=1.0 / DM, in1=r_var,
                                                op0=ALU.mult, op1=ALU.subtract))
            on("vector", c_rows)
            v_var = cDV.tok()

            def c_rowa(se):
                W(se, v_var)
                cAC.inc(se.activation(r_var, r_var, AF.Ln, bias=cb_kc[0:1, 1:2]))
                se.wait_ge(cAC.h, cAC.n)
                cAC.inc(se.activation(r_rstd, r_var, AF.Exp, scale=-0.5))
            on("scalar", c_rowa)
            a_rstd = cAC.tok()

            def c_row2(ve):
                W(ve, a_rstd)
                cDV.inc(ve.scalar_tensor_tensor(out=r_negmr, in0=r_m,
                                                scalar=-1.0, in1=r_rstd,
                                                op0=ALU.mult, op1=ALU.mult))
            on("vector", c_row2)
            v_rows = cDV.tok()

            def c_bcmm(te):
                W(te, v_rows)
                cPE.inc(te.matmul(pbC.ap(), cb_o1f.ap(), r_rstd,
                                  start=True, stop=True))
                cPE.inc(te.matmul(pbD.ap(), cb_o1f.ap(), r_negmr,
                                  start=True, stop=True))
            on("tensor", c_bcmm)
            p_bc = cPE.tok()

            def c_apply(ve, b=b, jt=jt, xb=xb, hfree=st["h_free"][jt]):
                W(ve, p_bc, tok_x2b, v_mpm, hfree)
                for j in range(CH):
                    ap1 = apt[:, 0:512]
                    cDV.inc(ve.tensor_tensor(out=ap1, in0=xb[:, j, :],
                                             in1=pbC.ap(), op=ALU.mult))
                    ve.wait_ge(cDV.h, cDV.n)
                    cDV.inc(ve.tensor_tensor(out=ap1, in0=ap1,
                                             in1=pbD.ap(), op=ALU.add))
                    ve.wait_ge(cDV.h, cDV.n)
                    cDV.inc(ve.tensor_scalar(out=vH[:, jt, j, :], in0=ap1,
                                             scalar1=capm[:, j, b:b + 1],
                                             scalar2=sapm[:, j, b:b + 1],
                                             op0=ALU.mult, op1=ALU.add))
                    ve.wait_ge(cDV.h, cDV.n)
            on("vector", c_apply)
            apply_toks[jt] = cDV.tok()
            bankfree["C"] = cDV.tok()
            bankfree["D"] = cDV.tok()
            xbf_free[jt % 2] = cDV.tok()

        kslots = [pbE.ap(), pbF.ap()]
        kkeys = ["E", "F"]

        def emit_K(jt, b=b):
            for jp2 in range(4):
                si = (jt * 4 + jp2) % 2
                dst = kslots[si]
                def c_kmm(te, jt=jt, jp2=jp2, si=si, dst=dst):
                    W(te, apply_toks[jt], tok_wk)
                    bw(te, kkeys[si])
                    rhs = vH[:, jt].rearrange("p j (n c) -> p j n c", c=P)[:, :, :, 0:64]
                    for half in range(2):
                        jo = jp2 * 2 + half
                        for jk in range(CH):
                            mm = te.matmul(dst[:, half * 256:(half + 1) * 256],
                                           vW[:, jk, jo * P:(jo + 1) * P],
                                           rhs[:, jk], start=(jk == 0),
                                           stop=(jk == CH - 1))
                            if jk == CH - 1 and half == 1:
                                cPE.inc(mm)
                on("tensor", c_kmm)
                p_k = cPE.tok()

                def c_kev(se, dst=dst, kfree=st["k_free"]):
                    W(se, p_k, kfree)
                    cAC.inc(se.activation(kraw.ap(), dst, AF.Copy))
                on("scalar", c_kev)
                a_kraw = cAC.tok()
                bankfree[kkeys[si]] = a_kraw

                def c_ksw(ge):
                    W(ge, a_kraw)
                    for (d0, s0) in [(0, 32), (32, 0), (64, 96), (96, 64)]:
                        dma(chVD, ge, kswp[d0:d0 + 32, :], kraw[s0:s0 + 32, :])
                on("gpsimd", c_ksw)
                tok_ksw = use(chVD)

                def c_krope(ve, jt=jt, jp2=jp2, first=first_k_evac[0]):
                    W(ve, tok_ksw, tok_cc)
                    if first:
                        W(ve, p_modmm, tok_agi2)
                        W(ve, early_cc[0])
                    csl = slice(jt * 256, (jt + 1) * 256)
                    cosw = cb_cos[:, None, csl].broadcast_to([P, 2, 256])
                    sinw = cb_sin[:, None, csl].broadcast_to([P, 2, 256])
                    cDV.inc(ve.tensor_tensor(out=kt1.ap().rearrange("p (a t) -> p a t", a=2),
                                             in0=kraw.ap().rearrange("p (a t) -> p a t", a=2),
                                             in1=cosw, op=ALU.mult))
                    cDV.inc(ve.tensor_tensor(out=kt2.ap().rearrange("p (a t) -> p a t", a=2),
                                             in0=kswp.ap().rearrange("p (a t) -> p a t", a=2),
                                             in1=sinw, op=ALU.mult))
                    ve.wait_ge(cDV.h, cDV.n)
                    cDV.inc(ve.tensor_tensor(
                        out=vK[:, jp2 * 2:jp2 * 2 + 2, csl],
                        in0=kt1.ap().rearrange("p (a t) -> p a t", a=2),
                        in1=kt2.ap().rearrange("p (a t) -> p a t", a=2),
                        op=ALU.add))
                on("vector", c_krope)
                first_k_evac[0] = False
                st["k_free"] = cDV.tok()

        def c_wv(ge, b=b, wfree=st["w2_free"]):
            W(ge, wfree)
            if b == 0:
                W(ge, tok_agi2, p_modmm)
            dma(chGC, ge, vW2, wvT_d.rearrange("(j p) f -> p j f", p=P))
        on("gpsimd", c_wv)
        tok_wv = use(chGC)

        def emit_V(jt, b=b):
            for sub in range(4):
                nchunk = jt * 4 + sub
                vbank = pbA.ap() if nchunk % 2 == 0 else pbB.ap()
                vkey = "A01" if nchunk % 2 == 0 else "B23"
                def c_vmm(te, jt=jt, sub=sub, vbank=vbank, vkey=vkey,
                          first=(jt == 0 and sub < 2)):
                    W(te, apply_toks[jt], tok_wv)
                    bw(te, vkey)
                    if first:
                        bw(te, "B2", "B3")
                    for jk in range(CH):
                        for fh in range(2):
                            mm = te.matmul(vbank[:, fh * 512:(fh + 1) * 512],
                                           vH[:, jt, jk, sub * P:(sub + 1) * P],
                                           vW2[:, jk, fh * 512:(fh + 1) * 512],
                                           start=(jk == 0), stop=(jk == CH - 1))
                            if jk == CH - 1 and fh == 1:
                                cPE.inc(mm)
                on("tensor", c_vmm)
                p_v = cPE.tok()

                def c_vev(se, nchunk=nchunk, vbank=vbank, vfree=st["v_free"],
                          first=first_v_evac[0]):
                    W(se, p_v, vfree)
                    if first:
                        W(se, p_modmm, tok_agi2)
                    cAC.inc(se.activation(vV[0:64, nchunk, :], vbank[0:64, :],
                                          AF.Copy))
                on("scalar", c_vev)
                first_v_evac[0] = False
                a_vev = cAC.tok()
                bankfree[vkey] = a_vev
                if nchunk % 2 == 0:
                    bankfree["A0"] = a_vev
                    bankfree["A1"] = a_vev
                else:
                    bankfree["B2"] = a_vev
                    bankfree["B3"] = a_vev

                def c_vdup(ge, nchunk=nchunk):
                    W(ge, a_vev)
                    dma(chVD, ge, vV[64:128, nchunk, :], vV[0:64, nchunk, :])
                on("gpsimd", c_vdup)

        for jt in range(NT):
            emit_C1(jt)
            if jt > 0:
                emit_K(jt - 1)
            if jt > 1:
                emit_V(jt - 2)
        emit_K(NT - 1)
        emit_V(NT - 2)
        emit_V(NT - 1)
        st["xbf_free2"] = xbf_free
        k_done = cDV.tok()
        v_done = use(chVD)
        v_done_pe = cPE.tok()

        def c_wg(ge, b=b, kpe=k_done):
            W(ge, kpe)
            dma(chGC, ge, vW, wgT_d.rearrange("(j p) f -> p j f", p=P))
        on("gpsimd", c_wg)
        tok_wg = use(chGC)

        for jt in range(NT):
            for jo in range(CH):
                gbank = pbF.ap() if jo % 2 == 0 else pbE.ap()
                gkey = "F" if jo % 2 == 0 else "E"
                def c_gmm(te, jt=jt, jo=jo, gbank=gbank, gkey=gkey):
                    W(te, apply_toks[jt], tok_wg)
                    bw(te, gkey)
                    for jk in range(CH):
                        mm = te.matmul(gbank, vW[:, jk, jo * P:(jo + 1) * P],
                                       vH[:, jt, jk, :], start=(jk == 0),
                                       stop=(jk == CH - 1))
                        if jk == CH - 1:
                            cPE.inc(mm)
                on("tensor", c_gmm)
                p_g = cPE.tok()

                def c_gex(se, gbank=gbank, egfree=st["egt_free"]):
                    W(se, p_g, egfree)
                    cAC.inc(se.activation(egt.ap(), gbank, AF.Exp, scale=0.125))
                on("scalar", c_gex)
                a_eg = cAC.tok()
                bankfree[gkey] = a_eg

                def c_gden(te):
                    W(te, a_eg, tok_cc)
                    bw(te, "B3")
                    cPE.inc(te.matmul(pbB[0:2, 512:1024], cb_ob.ap(), egt.ap(),
                                      start=True, stop=True))
                on("tensor", c_gden)
                p_gd = cPE.tok()

                def c_grd(ve):
                    W(ve, p_gd)
                    cDV.inc(ve.reciprocal(rdgf.ap(), pbB[0:2, 512:1024]))
                on("vector", c_grd)
                v_rdg = cDV.tok()
                bankfree["B3"] = v_rdg

                def c_grdb(te):
                    W(te, v_rdg)
                    bw(te, "B2")
                    cPE.inc(te.matmul(pbB[:, 0:512], cb_b2f.ap(), rdgf.ap(),
                                      start=True, stop=True))
                on("tensor", c_grdb)
                p_grdb = cPE.tok()

                def c_gn(ve, jt=jt, jo=jo, first=first_g_evac[0]):
                    W(ve, p_grdb)
                    if first:
                        W(ve, p_modmm, tok_agi2)
                    if jo >= 4 and jt == 0:
                        W(ve, v_done_pe)
                    cDV.inc(ve.tensor_tensor(out=vG[:, jo, jt * TT:(jt + 1) * TT],
                                             in0=egt.ap(), in1=pbB[:, 0:512],
                                             op=ALU.mult))
                on("vector", c_gn)
                first_g_evac[0] = False
                st["egt_free"] = cDV.tok()
                bankfree["B2"] = cDV.tok()
        g_done = cDV.tok()
        g_done_pe = cPE.tok()

        for jp in range(JP):
            def c_sc(te, jp=jp):
                W(te, k_done, tok_cc)
                if jp == 0:
                    W(te, st["scr_mid_free"])
                bw(te, "A01")
                for n in range(NCK):
                    osl = slice(n * 64, (n + 1) * 64)
                    te.matmul(pbA[0:64, osl], vK[0:64, jp, osl],
                              cb_qpk[0:64, jp, :], start=True, stop=True,
                              tile_position=(0, 0))
                    mm2 = te.matmul(pbA[64:128, osl], vK[64:128, jp, osl],
                                    cb_qpk[64:128, jp, :], start=True, stop=True,
                                    tile_position=(64, 64))
                    if n == NCK - 1:
                        cPE.inc(mm2)
            on("tensor", c_sc)
            p_sc = cPE.tok()

            def c_smsk(ve, jp=jp):
                W(ve, p_sc, tok_cst)
                if jp == 0:
                    W(ve, st["scr_mid_free"], xbf_free[0])
                cDV.inc(ve.tensor_tensor(
                    out=vSM.rearrange("p (n l) -> p n l", n=NCK),
                    in0=pbA.ap().rearrange("p (n l) -> p n l", n=NCK),
                    in1=cb_mneg[:, None, :].broadcast_to([P, NCK, 64]),
                    op=ALU.add))
            on("vector", c_smsk)
            v_sm = cDV.tok()
            bankfree["A01"] = v_sm

            def c_sexp(se):
                W(se, v_sm)
                cAC.inc(se.activation(vESC, vSM, AF.Exp, scale=0.125))
            on("scalar", c_sexp)
            a_esc = cAC.tok()

            def c_sden(te):
                W(te, a_esc)
                bw(te, "C", "D")
                cPE.inc(te.matmul(pbC[0:2, :], cb_ob.ap(), vESC[:, 0:512],
                                  start=True, stop=True))
                cPE.inc(te.matmul(pbD[0:2, :], cb_ob.ap(), vESC[:, 512:1024],
                                  start=True, stop=True))
            on("tensor", c_sden)
            p_sd = cPE.tok()

            def c_srd(ve):
                W(ve, p_sd)
                cDV.inc(ve.reciprocal(vRDN[:, 0:512], pbC[0:2, :]))
                cDV.inc(ve.reciprocal(vRDN[:, 512:1024], pbD[0:2, :]))
            on("vector", c_srd)
            v_srd = cDV.tok()
            bankfree["C"] = v_srd
            bankfree["D"] = v_srd

            def c_srdb(te):
                W(te, v_srd)
                bw(te, "B2", "B3")
                cPE.inc(te.matmul(pbB[:, 0:512], cb_b2f.ap(), vRDN[:, 0:512],
                                  start=True, stop=True))
                cPE.inc(te.matmul(pbB[:, 512:1024], cb_b2f.ap(), vRDN[:, 512:1024],
                                  start=True, stop=True))
            on("tensor", c_srdb)
            p_srdb = cPE.tok()

            def c_pn(ve):
                W(ve, p_srdb)
                cDV.inc(ve.tensor_tensor(out=vPRB[:, 0:512], in0=vESC[:, 0:512],
                                         in1=pbB[:, 0:512], op=ALU.mult))
                cDV.inc(ve.tensor_tensor(out=vPRB[:, 512:1024],
                                         in0=vESC[:, 512:1024],
                                         in1=pbB[:, 512:1024], op=ALU.mult))
            on("vector", c_pn)
            v_prb = cDV.tok()
            bankfree["B2"] = v_prb
            bankfree["B3"] = v_prb

            def c_at(te, jp=jp):
                W(te, v_prb, v_done)
                bw(te, "E")
                for n in range(NCK):
                    osl = slice(n * 64, (n + 1) * 64)
                    te.matmul(pbE[0:64, 0:64], vPRB[0:64, osl],
                              vV[0:64, n, jp * P:jp * P + 64],
                              start=(n == 0), stop=(n == NCK - 1),
                              tile_position=(0, 0))
                    mm2 = te.matmul(pbE[64:128, 0:64], vPRB[64:128, osl],
                                    vV[64:128, n, jp * P + 64:(jp + 1) * P],
                                    start=(n == 0), stop=(n == NCK - 1),
                                    tile_position=(64, 64))
                    if n == NCK - 1:
                        cPE.inc(mm2)
            on("tensor", c_at)
            p_at = cPE.tok()

            def c_atev(se, jp=jp, afree=st["attn_free"]):
                W(se, p_at)
                if jp == 0:
                    W(se, afree)
                cAC.inc(se.activation(attn_sb[:, jp, :], pbE[:, 0:64], AF.Copy))
            on("scalar", c_atev)
            bankfree["E"] = cAC.tok()
        at_done = cAC.tok()
        mid_pe = cPE.tok()

        def c_wp(ge, b=b):
            W(ge, g_done_pe)
            dma(chGC, ge, vW, wpT_d.rearrange("(j p) f -> p j f", p=P))
        on("gpsimd", c_wp)
        tok_wp = use(chGC)

        if DEBUG and b == 1:
            dg_g = nc.dram_tensor("dbg_gates", [P, CH, TB], BF16,
                                  kind="ExternalOutput").ap()
            dg_w = nc.dram_tensor("dbg_wp", [P, CH, DM], BF16,
                                  kind="ExternalOutput").ap()
            def dbg_mid(sy):
                W(sy, g_done, tok_wp)
                dma(chOUT, sy, dg_g, vG)
                dma(chOUT, sy, dg_w, vW2)
            on("sync", dbg_mid)
            st["dbg_mid_tok"] = use(chOUT)

        resid_toks = [None] * NT
        x2sp_toks = [None] * NT

        def emit_A(jt, b=b):
            sl2 = jt % 2
            for jp in range(JP):
                def c_wt(te, jt=jt, jp=jp):
                    W(te, at_done, g_done)
                    if jp == 0 and jt == 0:
                        W(te, mid_pe)
                    bw(te, "F")
                    te.matmul(pbF[0:64, :], attn_sb[0:64, jp, :],
                              vG[0:64, jp, jt * TT:(jt + 1) * TT],
                              start=True, stop=True, tile_position=(0, 0))
                    mm2 = te.matmul(pbF[64:128, :], attn_sb[64:128, jp, :],
                                    vG[64:128, jp, jt * TT:(jt + 1) * TT],
                                    start=True, stop=True, tile_position=(64, 64))
                    cPE.inc(mm2)
                on("tensor", c_wt)
                p_wt = cPE.tok()

                def c_wtev(se, jp=jp, jt=jt, wtfree=st["wt_free"]):
                    W(se, p_wt)
                    if jp == 0:
                        W(se, wtfree)
                        if jt == 0:
                            W(se, mid_pe, apply_toks[NT - 1])
                    cAC.inc(se.activation(vWT[:, jp, :], pbF.ap(), AF.Copy))
                on("scalar", c_wtev)
                bankfree["F"] = cAC.tok()
            wt_done = cAC.tok()

            def c_xt(sy, b=b, jt=jt, sl2=sl2, xtfree=st["xt_free2"][sl2]):
                W(sy, xtfree, g_done_pe)
                if jt < 2:
                    W(sy, apply_toks[NT - 1])
                dma(chX, sy, vXTd[sl2],
                    x_d[b, :, :, jt * TT:(jt + 1) * TT].rearrange("j p t -> p j t"))
            on("sync", c_xt)
            tok_xt = use(chX)

            for jo in range(CH):
                bank = "E" if jo % 2 == 0 else "C"
                def c_pmm(te, jt=jt, jo=jo, bank=bank):
                    W(te, wt_done, tok_wp)
                    bw(te, bank)
                    dst = pbE.ap() if bank == "E" else pbC.ap()
                    for jk in range(CH):
                        mm = te.matmul(dst, vW[:, jk, jo * P:(jo + 1) * P],
                                       vWT[:, jk, :], start=(jk == 0),
                                       stop=(jk == CH - 1))
                        if jk == CH - 1:
                            cPE.inc(mm)
                on("tensor", c_pmm)
                p_p = cPE.tok()

                def c_pev(ve, b=b, jt=jt, jo=jo, bank=bank, sl2=sl2,
                          x2free=st["x2_free2"][sl2]):
                    W(ve, p_p, tok_xt)
                    if jo == 0:
                        W(ve, x2free)
                    src = pbE.ap() if bank == "E" else pbC.ap()
                    cDV.inc(ve.scalar_tensor_tensor(
                        out=vX2d[sl2][:, jo, :], in0=src,
                        scalar=gapm[:, jo, b:b + 1], in1=vXTd[sl2][:, jo, :],
                        op0=ALU.mult, op1=ALU.add))
                on("vector", c_pev)
                bankfree[bank] = cDV.tok()
            resid_toks[jt] = cDV.tok()
            st["wt_free"] = cPE.tok()

            def c_x2sp(sy, b=b, jt=jt, sl2=sl2):
                W(sy, resid_toks[jt])
                dma(chSP, sy, x2_dram[b][:, :, jt * TT:(jt + 1) * TT], vX2d[sl2])
            on("sync", c_x2sp)
            x2sp_toks[jt] = use(chSP)

        def emit_B(jt, b=b):
            sl2 = jt % 2
            x2v = vX2d[sl2]
            h2v_ = vXTd[sl2]
            def c2_sum1(te, jt=jt, x2v=x2v):
                W(te, resid_toks[jt])
                bw(te, "B2")
                for j in range(CH):
                    mm = te.matmul(pbB[0:1, 0:512], cb_o1cf.ap(), x2v[:, j, :],
                                   start=(j == 0), stop=(j == CH - 1))
                    if j == CH - 1:
                        cPE.inc(mm)
            on("tensor", c2_sum1)
            p2_s1 = cPE.tok()

            sq_toks = []
            mm_toks = []
            for grp in range(4):
                def c2_sqv(ve, grp=grp, x2v=x2v,
                           prev=(mm_toks[grp - 2] if grp >= 2 else None)):
                    W(ve, prev)
                    if grp < 2:
                        W(ve, resid_toks[jt])
                    buf = apt if grp % 2 == 0 else apt2
                    cDV.inc(ve.tensor_tensor(
                        out=buf.ap().rearrange("p (a t) -> p a t", a=2),
                        in0=x2v[:, grp * 2:grp * 2 + 2, :],
                        in1=x2v[:, grp * 2:grp * 2 + 2, :], op=ALU.mult))
                on("vector", c2_sqv)
                sq_toks.append(cDV.tok())

                def c2_sqm(te, grp=grp, tok=sq_toks[grp]):
                    W(te, tok)
                    if grp == 0:
                        bw(te, "B3")
                    buf = apt if grp % 2 == 0 else apt2
                    for h in range(2):
                        mm = te.matmul(pbB[0:1, 512:1024], cb_o1cf.ap(),
                                       buf[:, h * 512:(h + 1) * 512],
                                       start=(grp == 0 and h == 0),
                                       stop=(grp == 3 and h == 1))
                        if h == 1:
                            cPE.inc(mm)
                on("tensor", c2_sqm)
                mm_toks.append(cPE.tok())
            p2_s2 = cPE.tok()

            def c2_rows(ve):
                W(ve, p2_s1, p2_s2)
                cDV.inc(ve.tensor_scalar_mul(r_m, pbB[0:1, 0:512], 1.0 / DM))
                ve.wait_ge(cDV.h, cDV.n)
                cDV.inc(ve.tensor_tensor(out=r_var, in0=r_m, in1=r_m,
                                         op=ALU.mult))
                ve.wait_ge(cDV.h, cDV.n)
                cDV.inc(ve.scalar_tensor_tensor(out=r_var, in0=pbB[0:1, 512:1024],
                                                scalar=1.0 / DM, in1=r_var,
                                                op0=ALU.mult, op1=ALU.subtract))
            on("vector", c2_rows)
            v2_var = cDV.tok()

            def c2_rowa(se):
                W(se, v2_var)
                cAC.inc(se.activation(r_var, r_var, AF.Ln, bias=cb_kc[0:1, 1:2]))
                se.wait_ge(cAC.h, cAC.n)
                cAC.inc(se.activation(r_rstd, r_var, AF.Exp, scale=-0.5))
            on("scalar", c2_rowa)
            a2_rstd = cAC.tok()

            def c2_row2(ve):
                W(ve, a2_rstd)
                cDV.inc(ve.scalar_tensor_tensor(out=r_negmr, in0=r_m,
                                                scalar=-1.0, in1=r_rstd,
                                                op0=ALU.mult, op1=ALU.mult))
            on("vector", c2_row2)
            v2_rows = cDV.tok()

            def c2_bcmm(te):
                W(te, v2_rows)
                cPE.inc(te.matmul(pbB[:, 0:512], cb_o1f.ap(), r_rstd,
                                  start=True, stop=True))
                cPE.inc(te.matmul(pbB[:, 512:1024], cb_o1f.ap(), r_negmr,
                                  start=True, stop=True))
            on("tensor", c2_bcmm)
            p2_bc = cPE.tok()

            def c2_apply(ve, b=b, jt=jt, x2v=x2v, h2v_=h2v_):
                W(ve, p2_bc)
                for j in range(CH):
                    ap1 = apt[:, 0:512]
                    cDV.inc(ve.tensor_tensor(out=ap1, in0=x2v[:, j, :],
                                             in1=pbB[:, 0:512], op=ALU.mult))
                    ve.wait_ge(cDV.h, cDV.n)
                    cDV.inc(ve.tensor_tensor(out=ap1, in0=ap1,
                                             in1=pbB[:, 512:1024], op=ALU.add))
                    ve.wait_ge(cDV.h, cDV.n)
                    cDV.inc(ve.tensor_scalar(out=h2v_[:, j, :], in0=ap1,
                                             scalar1=cbpm[:, j, b:b + 1],
                                             scalar2=sbpm[:, j, b:b + 1],
                                             op0=ALU.mult, op1=ALU.add))
                    ve.wait_ge(cDV.h, cDV.n)
                cDV.inc(ve.tensor_reduce(out=tmax.ap(), in_=h2v_, axis=AX.XY,
                                         op=ALU.max))
                ve.wait_ge(cDV.h, cDV.n)
                cDV.inc(ve.tensor_tensor(out=hmax.ap(), in0=hmax.ap(),
                                         in1=tmax.ap(), op=ALU.max))
                ve.wait_ge(cDV.h, cDV.n)
            on("vector", c2_apply)
            v2_h2 = cDV.tok()
            bankfree["B2"] = v2_h2
            bankfree["B3"] = v2_h2
            st["x2_free2"][sl2] = v2_h2

            def c2_h2sp(sy, b=b, jt=jt, sl2=sl2):
                W(sy, v2_h2, x2sp_toks[jt])
                dma(chSP, sy, h2_dram[b][:, :, jt * TT:(jt + 1) * TT], vXTd[sl2])
            on("sync", c2_h2sp)
            tok_h2sp = use(chSP)
            st.setdefault("h2sp_tok", {})[(b, jt)] = tok_h2sp
            st["xt_free2"][sl2] = tok_h2sp
            st["scr_mid_free"] = cPE.tok()

        for jt in range(NT):
            emit_A(jt)
            if jt > 0:
                emit_B(jt - 1)
        emit_B(NT - 1)

        for jt in range(NT):
            st["h_free"][jt] = g_done_pe
        st["k_free"] = mid_pe
        st["v_free"] = mid_pe
        st["kv_free"] = mid_pe
        st["g_free"] = cPE.tok()
        st["w_free"] = cPE.tok()
        st["w2_free"] = cPE.tok()
        bankfree["E"] = cPE.tok()
        bankfree["F"] = cPE.tok()
        st["attn_free"] = cPE.tok()
        st["c5_done"] = cDV.tok()
        st["xt_free"] = st["xt_free2"][0]
        st["x2_free"] = st["x2_free2"][1]
        st["scr_free_b"] = st["wt_free"]
        st["str_free_b"] = (st["x2_free2"][0], x2sp_toks[2])

    h2_all_sp = use(chSP)

    def b_ld3(ge):
        W(ge, v_ag3)
        dma(chCC, ge, sc40.ap(), ago_sc)
    on("gpsimd", b_ld3)
    tok_sc40 = use(chCC)

    def b_tr3(te):
        W(te, tok_sc40)
        bw(te, "D")
        cPE.inc(te.transpose(pbD[0:P, 0:40], sc40.ap(), cb_id[0:40, 0:40]))
    on("tensor", b_tr3)
    p_t3 = cPE.tok()

    def b_te3(ve):
        W(ve, p_t3)
        cDV.inc(ve.tensor_copy(scT.ap(), pbD[0:P, 0:40]))
        ve.wait_ge(cDV.h, cDV.n)
        sv = scT.ap().rearrange("p (r k) -> p r k", r=NCORES)
        cDV.inc(ve.tensor_copy(am1pm.ap().rearrange("p (r k) -> p r k", r=NCORES),
                               sv[:, :, 0:4]))
        cDV.inc(ve.tensor_copy(am2pm.ap(), sv[:, :, 4]))
    on("vector", b_te3)
    bankfree["D"] = cDV.tok()
    v_ampm = cDV.tok()



    def g1_tr(te):
        W(te, cDV.tok())
        bw(te, "D")
        cPE.inc(te.transpose(pbD[0:1, 0:P], hmax.ap(), cb_id.ap()))
    on("tensor", g1_tr)
    p_g1t = cPE.tok()

    def g1_r(ve):
        W(ve, p_g1t)
        cDV.inc(ve.tensor_reduce(out=gm1.ap(), in_=pbD[0:1, 0:P], axis=AX.X,
                                 op=ALU.max))
        ve.wait_ge(cDV.h, cDV.n)
        cDV.inc(ve.tensor_scalar_max(gm1.ap(), gm1.ap(), QEPS))
    on("vector", g1_r)
    bankfree["D"] = cDV.tok()
    v_g1 = cDV.tok()

    def g1_ar(ge):
        W(ge, v_g1)
        dma(chCC, ge, ar1_in, gm1.ap())
        ge.wait_ge(chCC.h, chCC.n)
        chCC.pending_epoch = False
        cCC.inc(ge.collective_compute("AllReduce", ALU.max, replica_groups=RG,
                                      ins=[ar1_in], outs=[ar1_out]))
        ge.wait_ge(cCC.h, cCC.n)
        dma(chCC, ge, gm1.ap(), ar1_out)
    on("gpsimd", g1_ar)
    tok_gm1 = use(chCC)

    def g1_rr(ve):
        W(ve, tok_gm1)
        cDV.inc(ve.reciprocal(r1row[0:1, 0:1], gm1.ap()))
        ve.wait_ge(cDV.h, cDV.n)
        cDV.inc(ve.tensor_scalar_mul(r1row[0:1, 0:1], r1row[0:1, 0:1], 255.0))
        cDV.inc(ve.tensor_scalar_mul(r1row[0:1, 1:2], gm1.ap(), 1.0 / 255.0))
    on("vector", g1_rr)
    v_r1r = cDV.tok()

    def g1_mm(te):
        W(te, v_r1r)
        bw(te, "D")
        cPE.inc(te.matmul(pbD[:, 0:2], cb_o1f.ap(), r1row.ap(),
                          start=True, stop=True))
    on("tensor", g1_mm)
    p_r1 = cPE.tok()

    def g1_c(ve):
        W(ve, p_r1, v_ampm)
        cDV.inc(ve.tensor_copy(r1pm.ap(), pbD[:, 0:2]))
        ve.wait_ge(cDV.h, cDV.n)
        cDV.inc(ve.tensor_scalar(out=dq1pm.ap(), in0=am1pm.ap(),
                                 scalar1=r1pm[:, 1:2], scalar2=1.0 / 127.0,
                                 op0=ALU.mult, op1=ALU.mult))
        cDV.inc(ve.reciprocal(t32a.ap(), am1pm.ap()))
        ve.wait_ge(cDV.h, cDV.n)
        cDV.inc(ve.tensor_scalar(out=t32b.ap(), in0=cb_b1t.ap(),
                                 scalar1=r1pm[:, 0:1], scalar2=127.0,
                                 op0=ALU.mult, op1=ALU.mult))
        ve.wait_ge(cDV.h, cDV.n)
        cDV.inc(ve.tensor_tensor(out=qb1pm.ap(), in0=t32b.ap(), in1=t32a.ap(),
                                 op=ALU.mult))
        ve.wait_ge(cDV.h, cDV.n)
        cDV.inc(ve.tensor_scalar(out=qb1pm.ap(), in0=qb1pm.ap(),
                                 scalar1=MAGIC, scalar2=MAGIC,
                                 op0=ALU.add, op1=ALU.subtract))
        ve.wait_ge(cDV.h, cDV.n)
        cDV.inc(ve.tensor_tensor(out=qbdq1.ap(), in0=qb1pm.ap(),
                                 in1=dq1pm.ap(), op=ALU.mult))
        ve.wait_ge(cDV.h, cDV.n)
    on("vector", g1_c)
    bankfree["D"] = cDV.tok()
    v_q1c = cDV.tok()

    def f_qw1l(seq):
        W(seq, v_ag1)
        W(seq, st["kv_free"])
        if DEBUG:
            W(seq, st.get("dbg_mid_tok"))
        for j in range(6):
            chW.inc(seq.dma_start_transpose(vQW1[:, j, :],
                                            ago_qw1[:, j * P:(j + 1) * P]))
        W(seq, st["w_free"])
        for j in range(6, CH):
            chW.inc(seq.dma_start_transpose(vQW1[:, j, :],
                                            ago_qw1[:, j * P:(j + 1) * P]))
    on("scalar", f_qw1l)
    tok_qw1 = use(chW)

    fst = {"h2_free": [None, None], "qh_free": [None, None],
           "ust_free": [None, None]}
    prep = {}

    def emit_prep(tg):
        b, jt = tg // 4, tg % 4
        sl2 = tg % 2
        h2b = vH2T[:, sl2]

        def f1_h2l(seq, b=b, jt=jt, h2b=h2b, prev=fst["h2_free"][sl2],
                   first=(tg == 0), sp_tok=st["h2sp_tok"][(tg // 4, tg % 4)]):
            W(seq, sp_tok, prev)
            if first:
                W(seq, st["w_free"], st["xt_free2"][0], st["xt_free2"][1],
                  st["c5_done"])
            dma(chUL, seq, h2b, h2_dram[b][:, :, jt * TT:(jt + 1) * TT])
        on("scalar", f1_h2l)
        tok_h2 = use(chUL)

        def f1_rd(se, h2b=h2b):
            W(se, tok_h2, v_q1c)
            hb = h2b.rearrange("p j t -> p (j t)")
            cAC.inc(se.activation(hb, hb, AF.Copy, bias=MAGIC,
                                  scale=r1pm[:, 0:1]))
        on("scalar", f1_rd)
        prep[tg] = cAC.tok()

    emit_prep(0)
    for tg in range(8):
        b, jt = tg // 4, tg % 4
        sl2 = tg % 2
        h2b = vH2T[:, sl2]
        qh = vQH[:, sl2]
        a_rd = prep[tg]

        def f1_cl(ve, h2b=h2b, qh=qh, a_rd=a_rd, prev=fst["qh_free"][sl2],
                  first=(tg == 0)):
            W(ve, a_rd, prev)
            if first:
                W(ve, st["w_free"], st["xt_free2"][1])
            hb = h2b.rearrange("p j t -> p (j t)")
            cDV.inc(ve.tensor_scalar(out=hb, in0=hb, scalar1=MAGIC, scalar2=0.0,
                                     op0=ALU.subtract, op1=ALU.max))
            ve.wait_ge(cDV.h, cDV.n)
            cDV.inc(ve.tensor_scalar_min(qh.rearrange("p j t -> p (j t)"),
                                         hb, 255.0))
        on("vector", f1_cl)
        v_qh = cDV.tok()
        fst["h2_free"][sl2] = v_qh
        for _f in fst.get("pending_um", []):
            _f()
        fst["pending_um"] = []

        for g in range(4):
            ub = vUST[:, g % 2]
            for fi in range(CH):
                ft = g * CH + fi
                bank = "E" if ft % 2 == 0 else "F"
                def f1_mm(te, ft=ft, qh=qh, bank=bank, first=(tg == 0 and ft == 0)):
                    W(te, v_qh, tok_qw1)
                    bw(te, bank)
                    dst = pbE.ap() if bank == "E" else pbF.ap()
                    for jk in range(CH):
                        mm = te.matmul(dst, vQW1[:, jk, ft * P:(ft + 1) * P],
                                       qh[:, jk, :], start=(jk == 0),
                                       stop=(jk == CH - 1))
                        if jk == CH - 1:
                            cPE.inc(mm)
                on("tensor", f1_mm)
                p_f1 = cPE.tok()

                if ft % 4 != 3:
                    def f1_ev(se, ft=ft, fi=fi, ub=ub, bank=bank,
                              prev=fst["ust_free"][g % 2] if fi == 0 else None,
                              first=(tg == 0 and g < 2 and fi == 0)):
                        if prev is not None:
                            ptok, pum = prev
                            W(se, p_f1, v_q1c, ptok, pum[0])
                        else:
                            W(se, p_f1, v_q1c)
                        if first:
                            W(se, st["x2_free2"][0], st["x2_free2"][1],
                              st["c5_done"])
                        src = pbE.ap() if bank == "E" else pbF.ap()
                        cAC.inc(se.activation(ub[:, fi, :], src, AF.Relu,
                                              bias=qbdq1[:, ft:ft + 1],
                                              scale=dq1pm[:, ft:ft + 1]))
                    on("scalar", f1_ev)
                    bankfree[bank] = cAC.tok()
                else:
                    def f1_evv(ve, ft=ft, fi=fi, ub=ub, bank=bank):
                        W(ve, p_f1, v_q1c)
                        src = pbE.ap() if bank == "E" else pbF.ap()
                        cDV.inc(ve.tensor_scalar(out=apt[:, 0:512], in0=src,
                                                 scalar1=qb1pm[:, ft:ft + 1],
                                                 scalar2=0.0,
                                                 op0=ALU.add, op1=ALU.max))
                        ve.wait_ge(cDV.h, cDV.n)
                        cDV.inc(ve.tensor_scalar_mul(ub[:, fi, :], apt[:, 0:512],
                                                     dq1pm[:, ft:ft + 1]))
                    on("vector", f1_evv)
                    bankfree[bank] = cDV.tok()
            a_ug = (cAC.tok(), cDV.tok())

            um_cell = [None]

            def emit_um(a_ug=a_ug, ub=ub, um_cell=um_cell):
                def f1_um(ve):
                    W(ve, *a_ug)
                    cDV.inc(ve.tensor_reduce(out=tmax.ap(), in_=ub, axis=AX.XY,
                                             op=ALU.max))
                    ve.wait_ge(cDV.h, cDV.n)
                    cDV.inc(ve.tensor_tensor(out=umax.ap(), in0=umax.ap(),
                                             in1=tmax.ap(), op=ALU.max))
                    ve.wait_ge(cDV.h, cDV.n)
                on("vector", f1_um)
                um_cell[0] = cDV.tok()
            if g < 2:
                emit_um()
            else:
                fst.setdefault("pending_um", []).append(emit_um)

            def f1_sp(sy, tg=tg, g=g, ub=ub):
                W(sy, *a_ug)
                dma(chSP, sy, u_dram[tg, g], ub)
            on("sync", f1_sp)
            tok_us = use(chSP)
            fst["ust_free"][g % 2] = (tok_us, um_cell)
            if g == 1 and tg < 7:
                emit_prep(tg + 1)
        fst["qh_free"][sl2] = cPE.tok()
    for _f in fst.get("pending_um", []):
        _f()
    fst["pending_um"] = []
    f1_sp_all = use(chSP)
    f1_pe_all = cPE.tok()

    def g2_tr(te):
        W(te, cDV.tok())
        bw(te, "D")
        cPE.inc(te.transpose(pbD[0:1, 0:P], umax.ap(), cb_id.ap()))
    on("tensor", g2_tr)
    p_g2t = cPE.tok()

    def g2_r(ve):
        W(ve, p_g2t)
        cDV.inc(ve.tensor_reduce(out=gm2.ap(), in_=pbD[0:1, 0:P], axis=AX.X,
                                 op=ALU.max))
        ve.wait_ge(cDV.h, cDV.n)
        cDV.inc(ve.tensor_scalar_max(gm2.ap(), gm2.ap(), QEPS))
    on("vector", g2_r)
    bankfree["D"] = cDV.tok()
    v_g2 = cDV.tok()

    def g2_ar(ge):
        W(ge, v_g2)
        dma(chCC, ge, ar2_in, gm2.ap())
        ge.wait_ge(chCC.h, chCC.n)
        chCC.pending_epoch = False
        cCC.inc(ge.collective_compute("AllReduce", ALU.max, replica_groups=RG,
                                      ins=[ar2_in], outs=[ar2_out]))
        ge.wait_ge(cCC.h, cCC.n)
        dma(chCC, ge, gm2.ap(), ar2_out)
    on("gpsimd", g2_ar)
    tok_gm2 = use(chCC)

    def g2_rr(ve):
        W(ve, tok_gm2)
        cDV.inc(ve.reciprocal(r2row[0:1, 0:1], gm2.ap()))
        ve.wait_ge(cDV.h, cDV.n)
        cDV.inc(ve.tensor_scalar_mul(r2row[0:1, 0:1], r2row[0:1, 0:1], 255.0))
        cDV.inc(ve.tensor_scalar_mul(r2row[0:1, 1:2], gm2.ap(), 1.0 / 255.0))
    on("vector", g2_rr)
    v_r2r = cDV.tok()

    def g2_mm(te):
        W(te, v_r2r)
        bw(te, "D")
        cPE.inc(te.matmul(pbD[:, 0:2], cb_o1f.ap(), r2row.ap(),
                          start=True, stop=True))
    on("tensor", g2_mm)
    p_r2 = cPE.tok()

    def g2_c(ve):
        W(ve, p_r2, v_ampm, v_mpm)
        cDV.inc(ve.tensor_copy(r2pm.ap(), pbD[:, 0:2]))
        ve.wait_ge(cDV.h, cDV.n)
        cDV.inc(ve.tensor_scalar(out=dq2pm.ap(), in0=am2pm.ap(),
                                 scalar1=r2pm[:, 1:2], scalar2=1.0 / 127.0,
                                 op0=ALU.mult, op1=ALU.mult))
        cDV.inc(ve.reciprocal(t32a[:, 0:CH], am2pm.ap()))
        ve.wait_ge(cDV.h, cDV.n)
        cDV.inc(ve.tensor_scalar(out=t32b[:, 0:CH], in0=cb_b2t.ap(),
                                 scalar1=r2pm[:, 0:1], scalar2=127.0,
                                 op0=ALU.mult, op1=ALU.mult))
        ve.wait_ge(cDV.h, cDV.n)
        cDV.inc(ve.tensor_tensor(out=qb2pm.ap(), in0=t32b[:, 0:CH],
                                 in1=t32a[:, 0:CH], op=ALU.mult))
        ve.wait_ge(cDV.h, cDV.n)
        cDV.inc(ve.tensor_scalar(out=qb2pm.ap(), in0=qb2pm.ap(),
                                 scalar1=MAGIC, scalar2=MAGIC,
                                 op0=ALU.add, op1=ALU.subtract))
        ve.wait_ge(cDV.h, cDV.n)
        cDV.inc(ve.tensor_tensor(out=gbdq2.ap(), in0=gbpm.ap(),
                                 in1=dq2pm.ap()[:, :, None].broadcast_to([P, CH, B]),
                                 op=ALU.mult))
        ve.wait_ge(cDV.h, cDV.n)
    on("vector", g2_c)
    bankfree["D"] = cDV.tok()
    v_q2c = cDV.tok()

    def q2_tr(te):
        W(te, v_q2c)
        bw(te, "D")
        cPE.inc(te.transpose(pbD[0:CH, 0:P], qb2pm.ap(), cb_id.ap()))
    on("tensor", q2_tr)
    p_q2t = cPE.tok()

    def q2_ev(ve):
        W(ve, p_q2t)
        cDV.inc(ve.tensor_copy(sc40[0:CH, :], pbD[0:CH, 0:P]))
    on("vector", q2_ev)
    bankfree["D"] = cDV.tok()
    v_q2rows = cDV.tok()

    def q2_bb(sy):
        W(sy, v_q2rows, f1_pe_all)
        dma(chBC, sy, qb2_dram, sc40[0:CH, :])
        sy.wait_ge(chBC.h, chBC.n)
        chBC.pending_epoch = False
        dma(chBC, sy, vQB2R, qb2_dram.rearrange("r c -> (r c)")[None, :])
    on("sync", q2_bb)
    tok_q2r = use(chBC)

    def q2_hl(ve):
        W(ve, tok_q2r)
        cDV.inc(ve.tensor_copy(qb2hi, vQB2R))
        ve.wait_ge(cDV.h, cDV.n)
        cDV.inc(ve.tensor_tensor(out=vQB2R, in0=vQB2R, in1=qb2hi,
                                 op=ALU.subtract))
        ve.wait_ge(cDV.h, cDV.n)
        cDV.inc(ve.tensor_copy(qb2lo, vQB2R))
    on("vector", q2_hl)
    v_hl = cDV.tok()

    def f_qw2l(seq):
        W(seq, v_ag2, f1_pe_all)
        for j in range(32):
            chW.inc(seq.dma_start_transpose(vQW2[:, j, :],
                                            ago_qw2[:, j * P:(j + 1) * P]))
    on("scalar", f_qw2l)
    tok_qw2 = use(chW)

    f2 = {"uin_free": None, "q2_free": [None, None], "x2_free": [None, None],
          "x3_free": [None, None]}
    for ti in range(16):
        tg, th = ti // 2, ti % 2
        b, jt = tg // 4, tg % 4
        t0 = jt * TT + th * 256
        q2 = vQ2a if ti % 2 == 0 else vQ2b
        x2b = vX2F[:, ti % 2]
        x3b = vX3[:, ti % 2]

        def f2_ul(seq, tg=tg, th=th, prev=f2["uin_free"], first=(ti == 0)):
            W(seq, f1_sp_all, prev)
            if first:
                W(seq, fst["qh_free"][0], fst["qh_free"][1])
                W(seq, cDV.tok())
            for g in range(4):
                dma(chUL, seq, vUIN[:, g * CH:(g + 1) * CH, :],
                    u_dram[tg, g][:, :, th * 256:(th + 1) * 256])
        on("scalar", f2_ul)
        tok_ui = use(chUL)

        def f2_x2l(seq, b=b, t0=t0, x2b=x2b, prev=f2["x2_free"][ti % 2],
                   first=(ti < 2)):
            W(seq, prev)
            if first:
                W(seq, f1_sp_all)
            dma(chUL, seq, x2b, x2_dram[b][:, :, t0:t0 + 256])
        on("scalar", f2_x2l)
        tok_x2 = use(chUL)

        def f2_rd(se):
            W(se, tok_ui, v_q2c)
            uf = vUIN.rearrange("p j t -> p (j t)")
            cAC.inc(se.activation(uf, uf, AF.Copy, bias=MAGIC,
                                  scale=r2pm[:, 0:1]))
            se.wait_ge(cAC.h, cAC.n)
            cAC.inc(se.activation(uf, uf, AF.Relu, bias=-MAGIC))
        on("scalar", f2_rd)
        a_f2r = cAC.tok()

        def f2_cl(ve, q2=q2, prev=f2["q2_free"][ti % 2], first=(ti < 2)):
            W(ve, a_f2r, prev)
            if first:
                W(ve, f1_pe_all, v_hl)
            uf = vUIN.rearrange("p j t -> p (j t)")
            cDV.inc(ve.tensor_scalar_min(q2.rearrange("p j t -> p (j t)"),
                                         uf, 255.0))
        on("vector", f2_cl)
        v_q2t = cDV.tok()
        f2["uin_free"] = v_q2t

        for dc in range(CH):
            bank = "E" if dc % 2 == 0 else "F"
            def f2_mm(te, dc=dc, q2=q2, bank=bank):
                W(te, v_q2t, tok_qw2, v_hl, tok_cc)
                bw(te, bank)
                dst = (pbE.ap() if bank == "E" else pbF.ap())[:, 0:256]
                for fj in range(32):
                    te.matmul(dst, vQW2[:, fj, dc * P:(dc + 1) * P],
                              q2[:, fj, :], start=(fj == 0), stop=False)
                te.matmul(dst, qb2hi[0:1, dc * P:(dc + 1) * P], cb_orb.ap(),
                          start=False, stop=False)
                mm = te.matmul(dst, qb2lo[0:1, dc * P:(dc + 1) * P], cb_orb.ap(),
                               start=False, stop=True)
                cPE.inc(mm)
            on("tensor", f2_mm)
            p_f2 = cPE.tok()

            def f2_ev(ve, dc=dc, b=b, x2b=x2b, x3b=x3b, bank=bank,
                      prev=f2["x3_free"][ti % 2]):
                W(ve, p_f2, tok_x2, v_q2c)
                if dc == 0:
                    W(ve, prev)
                src = (pbE.ap() if bank == "E" else pbF.ap())[:, 0:256]
                cDV.inc(ve.scalar_tensor_tensor(
                    out=x3b[:, dc, :], in0=src,
                    scalar=gbdq2[:, dc, b:b + 1], in1=x2b[:, dc, :],
                    op0=ALU.mult, op1=ALU.add))
            on("vector", f2_ev)
            bankfree[bank] = cDV.tok()
        v_x3 = cDV.tok()
        f2["x2_free"][ti % 2] = v_x3
        f2["q2_free"][ti % 2] = cPE.tok()

        def f2_out(sy, b=b, t0=t0, x3b=x3b):
            W(sy, v_x3)
            dma(chOUT, sy, y_d[b].rearrange("j p t -> p j t")[:, :, t0:t0 + 256],
                x3b)
        on("sync", f2_out)
        f2["x3_free"][ti % 2] = use(chOUT)

    if DEBUG:
        dbg_outs = {}
        def dmk(name, shape, dtype=F32):
            t = nc.dram_tensor("dbg_" + name, list(shape), dtype,
                               kind="ExternalOutput").ap()
            dbg_outs[name] = t
            return t
        d_x2 = dmk("x2", [B, P, CH, TB])
        d_h2 = dmk("h2", [B, P, CH, TB])
        d_u0 = dmk("u0", [4, P, CH, TT])
        smalls = [("gapm", gapm.ap()), ("gbpm", gbpm.ap()), ("capm", capm.ap()),
                  ("sapm", sapm.ap()), ("cbpm", cbpm.ap()), ("sbpm", sbpm.ap()),
                  ("am1pm", am1pm.ap()), ("dq1pm", dq1pm.ap()),
                  ("qb1pm", qb1pm.ap()), ("qbdq1", qbdq1.ap()),
                  ("dq2pm", dq2pm.ap()), ("qb2pm", qb2pm.ap()),
                  ("gbdq2", gbdq2.ap()), ("r1pm", r1pm.ap()), ("r2pm", r2pm.ap()),
                  ("gm1", gm1.ap()), ("gm2", gm2.ap()), ("hmax", hmax.ap()),
                  ("umax", umax.ap()), ("attn1", attn_sb.ap())]
        for nm, ap in smalls:
            dmk(nm, ap.shape, ap.dtype)

        def dbg_dump(sy):
            sy.wait_ge(chOUT.h, chOUT.n)
            dma(chOUT, sy, d_x2, x2_dram)
            dma(chOUT, sy, d_h2, h2_dram)
            dma(chOUT, sy, d_u0, u_dram[0])
            for nm, ap in smalls:
                dma(chOUT, sy, dbg_outs[nm], ap)
        on("sync", dbg_dump)

    def fin(sy):
        sy.wait_ge(chOUT.h, chOUT.n)
    on("sync", fin)

    es.close()
    return nc


def _host_prepare(inputs):
    x = np.asarray(inputs["L0_attn_x"])
    lq = np.asarray(inputs["L0_attn_latent_queries"])
    Wk = np.asarray(inputs["L0_attn_W_k"])
    Wv = np.asarray(inputs["L0_attn_W_v"])
    Wg = np.asarray(inputs["L0_attn_W_g"])
    Wp = np.asarray(inputs["L0_attn_W_p"])
    w1 = np.asarray(inputs["L0_ffn_w1"])
    b1 = np.asarray(inputs["L0_ffn_b1"])
    w2 = np.asarray(inputs["L0_ffn_w2"])
    b2 = np.asarray(inputs["L0_ffn_b2"])
    c = np.asarray(inputs["L0_c"])
    wmod = np.asarray(inputs["L0_wmod"])

    g64 = np.concatenate([np.arange(0, 64, 2), np.arange(1, 64, 2)])
    gidx = (np.arange(16)[:, None] * 64 + g64[None, :]).reshape(-1)

    wkT = np.ascontiguousarray(Wk[gidx, :].T)
    wvT = np.ascontiguousarray(Wv.T)
    wgT = np.ascontiguousarray(Wg.T)
    wpT = np.ascontiguousarray(Wp.T)

    q = lq[0]
    qpk = np.empty((P, JP, 64), np.float32)
    for jp in range(JP):
        qpk[0:64, jp, :] = q[:, 2 * jp, :][:, g64].T
        qpk[64:128, jp, :] = q[:, 2 * jp + 1, :][:, g64].T

    inv = 1.0 / (10000.0 ** (np.arange(0, 64, 2, dtype=np.float64) / 64.0))
    tpos = (np.arange(NCK)[:, None] * 128 + np.arange(64)[None, :]).reshape(-1)
    pfreq = inv[np.arange(P) % 32]
    ang = tpos[None, :] * pfreq[:, None]
    cost = np.cos(ang).astype(np.float32)
    sint = np.sin(ang).astype(np.float32)
    sgn = np.where((np.arange(P) % 64) < 32, -1.0, 1.0).astype(np.float32)
    sint = sint * sgn[:, None]

    mneg = np.where((np.arange(P) % 64)[:, None] <= np.arange(64)[None, :],
                    0.0, NEGBIG).astype(np.float32)
    oneblk = np.zeros((P, 2), np.float32)
    oneblk[0:64, 0] = 1.0
    oneblk[64:128, 1] = 1.0
    blk2 = np.ascontiguousarray(oneblk.T)
    ident = np.eye(P, dtype=np.float32)
    onescol = np.ones((P, 1), np.float32)
    onesrow = np.ones((1, TT), np.float32)
    kconst = np.zeros((P, 4), np.float32)
    kconst[:, 0] = MAGIC
    kconst[:, 1] = LN_EPS
    kconst[:, 2] = QEPS
    kconst[:, 3] = -MAGIC

    b1t = np.ascontiguousarray(b1.reshape(32, P).T)
    b2t = np.ascontiguousarray(b2.reshape(CH, P).T)

    in_maps = []
    for cid in range(NCORES):
        xb = x[cid * B:(cid + 1) * B]
        x_lay = np.ascontiguousarray(xb.transpose(0, 2, 1).reshape(B, CH, P, TB))
        cb = c[cid * B:(cid + 1) * B]
        cT = np.ascontiguousarray(cb.T.reshape(CH, P, B).transpose(1, 0, 2))
        in_maps.append({
            "x": x_lay,
            "wkT": wkT, "wvT": wvT, "wgT": wgT, "wpT": wpT,
            "wmod": wmod, "cT": cT,
            "w1s": np.ascontiguousarray(w1[cid * 512:(cid + 1) * 512]),
            "w2s": np.ascontiguousarray(w2[cid * P:(cid + 1) * P]),
            "b1t": b1t, "b2t": b2t,
            "qpk": qpk, "cost": cost, "sint": sint, "mneg": mneg,
            "oneblk": oneblk, "blk2": blk2, "ident": ident,
            "onescol": onescol, "onesrow": onesrow, "kconst": kconst,
        })
    return in_maps


def kernel(**inputs):
    in_maps = _host_prepare(inputs)
    nc = build_program()
    res = run_bass_kernel_spmd(nc, in_maps, core_ids=list(range(NCORES)))
    outs = []
    for cid in range(NCORES):
        y = np.asarray(res.results[cid]["y"])
        outs.append(y.transpose(0, 3, 1, 2).reshape(B, TB, DM))
    return np.concatenate(outs, axis=0).astype(np.float32)


if __name__ == "__main__":
    nc = build_program()
    print("instructions:", nc.n_instructions())
    print("sbuf remaining:", nc.sbuf_bytes_remaining)


# version 24
# speedup vs baseline: 1.0222x; 1.0222x over previous
import numpy as np
import ml_dtypes
from contextlib import ExitStack

import concourse.bass as bass
import concourse.mybir as mybir
from concourse.bass_utils import run_bass_kernel_spmd

F32 = mybir.dt.float32
BF16 = mybir.dt.bfloat16
AF = mybir.ActivationFunctionType
ALU = mybir.AluOpType
AX = mybir.AxisListType

NCORES = 8
B = 2
TB = 2048
CH = 8
P = 128
DM = 1024
FF = 4096
TT = 512
NT = TB // TT
NCK = TB // 128
JP = 8
MAGIC = 12582912.0
NEGBIG = -1.0e9
LN_EPS = 1e-5
QEPS = 1e-8
RG = [list(range(NCORES))]

DEBUG = False


class Chan:
    def __init__(self, es, nc, name, step=1):
        self.h = es.enter_context(nc.semaphore(name))
        self.n = 0
        self.step = step
        self.pending_epoch = False

    def inc(self, instr):
        self.n += self.step
        instr.then_inc(self.h, self.step)
        return self.n

    def tok(self):
        return (self, self.n)


def build_program():
    nc = bass.Bass("TRN2", target_bir_lowering=False)
    es = ExitStack()
    S = es.enter_context

    def din(name, shape, dtype=F32):
        return nc.dram_tensor(name, shape, dtype, kind="ExternalInput").ap()

    def dint(name, shape, dtype=F32):
        return nc.dram_tensor(name, shape, dtype, kind="Internal").ap()

    x_d = din("x", [B, CH, P, TB])
    wkT_d = din("wkT", [DM, DM])
    wvT_d = din("wvT", [DM, DM])
    wgT_d = din("wgT", [DM, DM])
    wpT_d = din("wpT", [DM, DM])
    wmod_d = din("wmod", [DM, 6 * DM])
    cT_d = din("cT", [P, CH, B])
    w1s_d = din("w1s", [512, DM])
    w2s_d = din("w2s", [P, FF])
    b1t_d = din("b1t", [P, 32])
    b2t_d = din("b2t", [P, CH])
    qpk_d = din("qpk", [P, JP, 64])
    cost_d = din("cost", [P, NCK * 64])
    sint_d = din("sint", [P, NCK * 64])
    mneg_d = din("mneg", [P, 64])
    oneblk_d = din("oneblk", [P, 2])
    blk2_d = din("blk2", [2, P])
    ident_d = din("ident", [P, P])
    onescol_d = din("onescol", [P, 1])
    onesrow_d = din("onesrow", [1, TT])
    kconst_d = din("kconst", [P, 4])

    y_d = nc.dram_tensor("y", [B, CH, P, TB], F32, kind="ExternalOutput").ap()

    h2_dram = dint("h2sp", [B, P, CH, TB])
    x2_dram = dint("x2sp", [B, P, CH, TB])
    u_dram = dint("usp", [8, 4, P, CH, TT])
    rowb_dram = dint("rowb", [2, TT])
    colb_dram = dint("colb", [P])
    qb2_dram = dint("qb2b", [CH, P])
    agi_qw1 = dint("agi_qw1", [512, DM], BF16)
    ago_qw1 = dint("ago_qw1", [FF, DM], BF16)
    agi_qw2 = dint("agi_qw2", [P, FF], BF16)
    ago_qw2 = dint("ago_qw2", [DM, FF], BF16)
    agi_sc = dint("agi_sc", [5, P])
    ago_sc = dint("ago_sc", [40, P])
    ar1_in = dint("ar1_in", [1, 1])
    ar1_out = dint("ar1_out", [1, 1])
    ar2_in = dint("ar2_in", [1, 1])
    ar2_out = dint("ar2_out", [1, 1])

    BIGA = S(nc.sbuf_tensor("BIGA", [P, 20480], F32))
    SMA = S(nc.sbuf_tensor("SMA", [P, 12288], F32))
    STR = S(nc.sbuf_tensor("STR", [P, 8192], F32))
    SCR = S(nc.sbuf_tensor("SCR", [P, 4096], F32))

    BIGAb = BIGA.ap().bitcast(BF16)
    SMAb = SMA.ap().bitcast(BF16)
    STRf = STR.ap()
    SCRb = SCR.ap().bitcast(BF16)

    vK = BIGAb[:, 0:8192].rearrange("p (j t) -> p j t", j=CH)
    vV = BIGAb[:, 8192:24576].rearrange("p (n f) -> p n f", n=NCK)
    vG = BIGAb[:, 24576:40960].rearrange("p (j t) -> p j t", j=CH)
    vW = SMAb[:, 0:8192].rearrange("p (j f) -> p j f", j=CH)
    vW2 = BIGAb[:, 32768:40960].rearrange("p (j f) -> p j f", j=CH)
    vH = SMAb[:, 8192:24576].rearrange("p (s j t) -> p s j t", s=NT, j=CH)
    vXTd = [SMA.ap()[:, 4096:8192].rearrange("p (j t) -> p j t", j=CH),
            SMA.ap()[:, 8192:12288].rearrange("p (j t) -> p j t", j=CH)]
    vX2d = [STRf[:, 0:4096].rearrange("p (j t) -> p j t", j=CH),
            STRf[:, 4096:8192].rearrange("p (j t) -> p j t", j=CH)]
    vXBF0 = SCRb[:, 0:4096].rearrange("p (j t) -> p j t", j=CH)
    vXBF1 = STR.ap().bitcast(BF16)[:, 0:4096].rearrange("p (j t) -> p j t", j=CH)
    vSM = SCR.ap()[:, 0:1024]
    vESC = SCRb[:, 2048:3072]
    vPRB = SCRb[:, 3072:4096]
    vRDN = SCR.ap()[0:2, 2048:3072]
    vRDS = SCRb[0:2, 6144:7168]
    vWT = SCRb[:, 0:4096].rearrange("p (j t) -> p j t", j=CH)

    vMOD = BIGA.ap()[0:2, 0:6144]
    vWQF = BIGA.ap()[:, 8192:12288].rearrange("p (f c) -> p f c", f=4)
    vWQT = BIGA.ap()[:, 12288:13312]
    vWMB0 = BIGA.ap()[:, 13312:15360]
    vWMB1 = BIGA.ap()[:, 18432:20480]
    vWQQ = BIGAb[:, 32768:36864].rearrange("p (f c) -> p f c", f=4)
    vW2F = vWQF.rearrange("p f c -> p (f c)")
    vW2Q = vWQQ.rearrange("p f c -> p (f c)")

    vQW1 = BIGAb[:, 0:32768].rearrange("p (j f) -> p j f", j=CH)
    vQW2 = BIGAb[:, 0:32768].rearrange("p (j f) -> p j f", j=32)
    vQ2a = BIGAb[:, 32768:40960].rearrange("p (j t) -> p j t", j=32)
    vQ2b = SMAb[:, 16384:24576].rearrange("p (j t) -> p j t", j=32)
    vH2T = SMA.ap()[:, 0:8192].rearrange("p (s j t) -> p s j t", s=2, j=CH)
    vQH = SMAb[:, 16384:24576].rearrange("p (s j t) -> p s j t", s=2, j=CH)
    vUST = STRf.rearrange("p (s j t) -> p s j t", s=2, j=CH)
    vUIN = SMA.ap()[:, 0:8192].rearrange("p (j t) -> p j t", j=32)
    vX2F = STRf[:, 0:4096].rearrange("p (s j t) -> p s j t", s=2, j=CH)
    vX3 = STRf[:, 4096:8192].rearrange("p (s j t) -> p s j t", s=2, j=CH)
    vQB2R = SCR.ap()[0:1, 0:1024]

    kraw = S(nc.sbuf_tensor("kraw", [P, 512], BF16))
    kswp = S(nc.sbuf_tensor("kswp", [P, 512], BF16))
    kt1 = S(nc.sbuf_tensor("kt1", [P, 512], F32))
    kt2 = S(nc.sbuf_tensor("kt2", [P, 512], F32))
    apt = S(nc.sbuf_tensor("apt", [P, 2 * TT], F32))
    apt2 = S(nc.sbuf_tensor("apt2", [P, 2 * TT], F32))
    r_m = SCR.ap()[0:1, 2048:2560]
    r_var = SCR.ap()[0:1, 2560:3072]
    r_rstd = SCR.ap()[0:1, 3072:3584]
    r_negmr = SCR.ap()[0:1, 3584:4096]
    egt = S(nc.sbuf_tensor("egt", [P, TT], BF16))
    rdgf = S(nc.sbuf_tensor("rdgf", [2, TT], F32))
    rdg = S(nc.sbuf_tensor("rdg", [2, TT], BF16))
    attn_sb = S(nc.sbuf_tensor("attn_sb", [P, JP, 64], BF16))
    sapm = S(nc.sbuf_tensor("sapm", [P, CH, B], F32))
    capm = S(nc.sbuf_tensor("capm", [P, CH, B], F32))
    gapm = S(nc.sbuf_tensor("gapm", [P, CH, B], F32))
    sbpm = S(nc.sbuf_tensor("sbpm", [P, CH, B], F32))
    cbpm = S(nc.sbuf_tensor("cbpm", [P, CH, B], F32))
    gbpm = S(nc.sbuf_tensor("gbpm", [P, CH, B], F32))
    am1_l = S(nc.sbuf_tensor("am1_l", [P, 4], F32))
    rw1_l = S(nc.sbuf_tensor("rw1_l", [P, 4], F32))
    am2_l = S(nc.sbuf_tensor("am2_l", [P, 1], F32))
    rw2_l = S(nc.sbuf_tensor("rw2_l", [P, 1], F32))
    sc5 = S(nc.sbuf_tensor("sc5", [5, P], F32))
    sc40 = S(nc.sbuf_tensor("sc40", [40, P], F32))
    scT = S(nc.sbuf_tensor("scT", [P, 40], F32))
    am1pm = S(nc.sbuf_tensor("am1pm", [P, 32], F32))
    am2pm = S(nc.sbuf_tensor("am2pm", [P, CH], F32))
    dq1pm = S(nc.sbuf_tensor("dq1pm", [P, 32], F32))
    qb1pm = S(nc.sbuf_tensor("qb1pm", [P, 32], F32))
    qbdq1 = S(nc.sbuf_tensor("qbdq1", [P, 32], F32))
    dq2pm = S(nc.sbuf_tensor("dq2pm", [P, CH], F32))
    qb2pm = S(nc.sbuf_tensor("qb2pm", [P, CH], F32))
    gbdq2 = S(nc.sbuf_tensor("gbdq2", [P, CH, B], F32))
    qb2hi = SCRb[0:1, 2048:3072]
    qb2lo = SCRb[0:1, 3072:4096]
    t32a = S(nc.sbuf_tensor("t32a", [P, 32], F32))
    t32b = S(nc.sbuf_tensor("t32b", [P, 32], F32))
    hmax = S(nc.sbuf_tensor("hmax", [P, 1], F32))
    umax = S(nc.sbuf_tensor("umax", [P, 1], F32))
    tmax = S(nc.sbuf_tensor("tmax", [P, 1], F32))
    row128 = S(nc.sbuf_tensor("row128", [1, P], F32))
    gm1 = S(nc.sbuf_tensor("gm1", [1, 1], F32))
    gm2 = S(nc.sbuf_tensor("gm2", [1, 1], F32))
    r1row = S(nc.sbuf_tensor("r1row", [1, 2], F32))
    r2row = S(nc.sbuf_tensor("r2row", [1, 2], F32))
    r1pm = S(nc.sbuf_tensor("r1pm", [P, 2], F32))
    r2pm = S(nc.sbuf_tensor("r2pm", [P, 2], F32))
    cb_qpk = S(nc.sbuf_tensor("cb_qpk", [P, JP, 64], BF16))
    cb_cos = S(nc.sbuf_tensor("cb_cos", [P, NCK * 64], BF16))
    cb_sin = S(nc.sbuf_tensor("cb_sin", [P, NCK * 64], BF16))
    cb_mneg = S(nc.sbuf_tensor("cb_mneg", [P, 64], BF16))
    cb_ob = S(nc.sbuf_tensor("cb_ob", [P, 2], BF16))
    cb_b2 = S(nc.sbuf_tensor("cb_b2", [2, P], BF16))
    cb_b2f = S(nc.sbuf_tensor("cb_b2f", [2, P], F32))
    cb_id = S(nc.sbuf_tensor("cb_id", [P, P], F32))
    cb_o1b = S(nc.sbuf_tensor("cb_o1b", [P, 1], BF16))
    cb_o1cf = S(nc.sbuf_tensor("cb_o1cf", [P, 1], F32))
    cb_o1f = S(nc.sbuf_tensor("cb_o1f", [1, P], F32))
    cb_orb = S(nc.sbuf_tensor("cb_orb", [1, 256], BF16))
    cb_kc = S(nc.sbuf_tensor("cb_kc", [P, 4], F32))
    cb_cT = S(nc.sbuf_tensor("cb_cT", [P, CH, B], F32))
    silc = S(nc.sbuf_tensor("silc", [P, CH * B], F32))
    cb_b1t = S(nc.sbuf_tensor("cb_b1t", [P, 32], F32))
    cb_b2t = S(nc.sbuf_tensor("cb_b2t", [P, CH], F32))

    pbA = S(nc.psum_tensor("pbA", [P, 1024], F32))
    pbB = S(nc.psum_tensor("pbB", [P, 1024], F32))
    pbC = S(nc.psum_tensor("pbC", [P, 512], F32))
    pbD = S(nc.psum_tensor("pbD", [P, 512], F32))
    pbE = S(nc.psum_tensor("pbE", [P, 512], F32))
    pbF = S(nc.psum_tensor("pbF", [P, 512], F32))

    chX = Chan(es, nc, "chX", 16)
    chSP = Chan(es, nc, "chSP", 16)
    chOUT = Chan(es, nc, "chOUT", 16)
    chBC = Chan(es, nc, "chBC", 16)
    chW = Chan(es, nc, "chW", 16)
    chUL = Chan(es, nc, "chUL", 16)
    chGC = Chan(es, nc, "chGC", 16)
    chVD = Chan(es, nc, "chVD", 16)
    chCC = Chan(es, nc, "chCC", 16)
    cCC = Chan(es, nc, "cCC", 1)
    cPE = Chan(es, nc, "cPE", 1)
    cDV = Chan(es, nc, "cDV", 1)
    cAC = Chan(es, nc, "cAC", 1)
    cGP = Chan(es, nc, "cGP", 1)

    nc.const_aps.register(F32, -MAGIC, cb_kc[:, 3:4])
    block = S(nc.Block())

    def on(eng, f):
        getattr(block, eng)(f)

    def W(e, *tokens):
        for t in tokens:
            if t is None:
                continue
            c, v = t
            if v > 0:
                e.wait_ge(c.h, v)

    def dma(ch, q, out, in_):
        if ch.pending_epoch:
            q.wait_ge(ch.h, ch.n)
            ch.pending_epoch = False
        return ch.inc(q.dma_start(out=out, in_=in_))

    def use(ch):
        ch.pending_epoch = True
        return (ch, ch.n)

    bankfree = {}

    def bw(e, *names):
        for nm in names:
            t = bankfree.get(nm)
            if t is not None:
                W(e, t)

    def a_loads(sy):
        for sb_t, d in [(cb_cos, None), (cb_sin, None)]:
            pass
        for sb_t, d in [(cb_id, ident_d), (cb_b2f, blk2_d),
                        (cb_o1f, onesrow_d[:, 0:P]), (cb_o1cf, onescol_d), (cb_kc, kconst_d),
                        (cb_cT, cT_d), (cb_b1t, b1t_d), (cb_b2t, b2t_d)]:
            dma(chX, sy, sb_t.ap(), d)
    on("sync", a_loads)
    tok_cst = use(chX)

    def a_casts(ge):
        for sb_t, d in [(cb_qpk, qpk_d), (cb_mneg, mneg_d), (cb_ob, oneblk_d), (cb_b2, blk2_d),
                        (cb_o1b, onescol_d), (cb_orb, onesrow_d[:, 0:256]),
                        (cb_cos, cost_d), (cb_sin, sint_d)]:
            dma(chGC, ge, sb_t.ap(), d)
    on("gpsimd", ge_f := a_casts)
    tok_cc = use(chGC)

    def a_w1s(seq):
        dma(chW, seq, vWQF, w1s_d.rearrange("(ft p) c -> p ft c", p=P))
    on("scalar", a_w1s)
    tok_w1s = use(chW)

    def b_am1(ve):
        W(ve, tok_w1s)
        cDV.inc(ve.tensor_reduce(out=am1_l.ap(), in_=vWQF, axis=AX.X,
                                 op=ALU.max, apply_absolute_value=True))
        ve.wait_ge(cDV.h, cDV.n)
        cDV.inc(ve.tensor_scalar_max(am1_l.ap(), am1_l.ap(), QEPS))
        ve.wait_ge(cDV.h, cDV.n)
        cDV.inc(ve.reciprocal(rw1_l.ap(), am1_l.ap()))
        ve.wait_ge(cDV.h, cDV.n)
        cDV.inc(ve.tensor_scalar_mul(rw1_l.ap(), rw1_l.ap(), 127.0))
    on("vector", b_am1)
    v_am1 = cDV.tok()

    v_qts = None
    for ft in range(4):
        def b_qa(se, ft=ft, prev=v_qts):
            W(se, v_am1, tok_cst, prev)
            cAC.inc(se.activation(vWQT, vWQF[:, ft, :], AF.Copy,
                                  bias=MAGIC, scale=rw1_l[:, ft:ft + 1]))
        on("scalar", b_qa)
        a_q = cAC.tok()

        def b_qt(ve, ft=ft):
            W(ve, a_q)
            cDV.inc(ve.tensor_scalar(out=vWQT, in0=vWQT,
                                     scalar1=MAGIC, scalar2=-128.0,
                                     op0=ALU.subtract, op1=ALU.max))
            ve.wait_ge(cDV.h, cDV.n)
            cDV.inc(ve.tensor_scalar_min(vWQQ[:, ft, :], vWQT, 127.0))
        on("vector", b_qt)
        v_qts = cDV.tok()
    v_qw1 = cDV.tok()

    def b_st1(ge):
        W(ge, v_qw1)
        dma(chCC, ge, agi_qw1.rearrange("(ft p) c -> p ft c", p=P), vWQQ)
    on("gpsimd", b_st1)
    tok_agi1 = use(chCC)

    def b_ag1(ge):
        W(ge, tok_agi1)
        cCC.inc(ge.collective_compute("AllGather", ALU.bypass, replica_groups=RG,
                                      ins=[agi_qw1], outs=[ago_qw1]))
    on("gpsimd", b_ag1)
    v_ag1 = cCC.tok()

    def b_w2l(seq):
        W(seq, v_qw1)
        dma(chW, seq, vW2F, w2s_d)
    on("scalar", b_w2l)
    tok_w2s = use(chW)

    def b_am2(ve):
        W(ve, tok_w2s)
        cDV.inc(ve.tensor_reduce(out=am2_l.ap(), in_=vW2F, axis=AX.X,
                                 op=ALU.max, apply_absolute_value=True))
        ve.wait_ge(cDV.h, cDV.n)
        cDV.inc(ve.tensor_scalar_max(am2_l.ap(), am2_l.ap(), QEPS))
        ve.wait_ge(cDV.h, cDV.n)
        cDV.inc(ve.reciprocal(rw2_l.ap(), am2_l.ap()))
        ve.wait_ge(cDV.h, cDV.n)
        cDV.inc(ve.tensor_scalar_mul(rw2_l.ap(), rw2_l.ap(), 127.0))
    on("vector", b_am2)
    v_am2 = cDV.tok()

    def b_qa2(se):
        W(se, v_am2, tok_agi1)
        cAC.inc(se.activation(vW2F, vW2F, AF.Copy, bias=MAGIC,
                              scale=rw2_l[:, 0:1]))
    on("scalar", b_qa2)
    a_q2 = cAC.tok()

    def b_qt2(ve):
        W(ve, a_q2)
        cDV.inc(ve.tensor_scalar(out=vW2F, in0=vW2F, scalar1=MAGIC,
                                 scalar2=-128.0, op0=ALU.subtract, op1=ALU.max))
        ve.wait_ge(cDV.h, cDV.n)
        cDV.inc(ve.tensor_scalar_min(vW2Q, vW2F, 127.0))
    on("vector", b_qt2)
    v_qw2 = cDV.tok()

    def b_st2(ge):
        W(ge, v_qw2)
        dma(chCC, ge, agi_qw2, vW2Q)
    on("gpsimd", b_st2)
    tok_agi2 = use(chCC)

    def b_ag2(ge):
        W(ge, tok_agi2)
        cCC.inc(ge.collective_compute("AllGather", ALU.bypass, replica_groups=RG,
                                      ins=[agi_qw2], outs=[ago_qw2]))
    on("gpsimd", b_ag2)
    v_ag2 = cCC.tok()

    def b_tr1(te):
        W(te, v_am1, v_am2, tok_cst)
        bw(te, "D")
        cPE.inc(te.transpose(pbD[0:4, 0:P], am1_l.ap(), cb_id.ap()))
    on("tensor", b_tr1)
    p_t1 = cPE.tok()

    def b_te1(ve):
        W(ve, p_t1)
        cDV.inc(ve.tensor_copy(sc5[0:4, :], pbD[0:4, 0:P]))
    on("vector", b_te1)
    bankfree["D"] = cDV.tok()

    def b_tr2(te):
        bw(te, "D")
        cPE.inc(te.transpose(pbD[0:1, 0:P], am2_l.ap(), cb_id.ap()))
    on("tensor", b_tr2)
    p_t2 = cPE.tok()

    def b_te2(ve):
        W(ve, p_t2)
        cDV.inc(ve.tensor_copy(row128.ap(), pbD[0:1, 0:P]))
    on("vector", b_te2)
    bankfree["D"] = cDV.tok()
    v_sc5 = cDV.tok()

    def b_te2b(ge):
        W(ge, v_sc5)
        dma(chVD, ge, sc5[4:5, :], row128.ap())
    on("gpsimd", b_te2b)
    tok_sc5r = use(chVD)

    def b_st3(ge):
        W(ge, v_sc5, tok_sc5r)
        dma(chCC, ge, agi_sc, sc5.ap())
    on("gpsimd", b_st3)
    tok_agisc = use(chCC)

    def b_ag3(ge):
        W(ge, tok_agisc)
        cCC.inc(ge.collective_compute("AllGather", ALU.bypass, replica_groups=RG,
                                      ins=[agi_sc], outs=[ago_sc]))
    on("gpsimd", b_ag3)
    v_ag3 = cCC.tok()

    def m_silu(se):
        W(se, tok_cst)
        cAC.inc(se.activation(silc.ap(), cb_cT.ap().rearrange("p j b -> p (j b)"),
                              AF.Silu))
    on("scalar", m_silu)
    a_silu = cAC.tok()

    def m_zero(ve):
        W(ve, tok_agi2)
        cDV.inc(ve.memset(vMOD, 0.0))
    on("vector", m_zero)
    v_mz = cDV.tok()

    wm_prev_pe = [None, None]
    for j in range(CH):
        for q in range(3):
            wb = vWMB0 if (j * 3 + q) % 2 == 0 else vWMB1
            def m_wl(ge, j=j, q=q, prev=wm_prev_pe[(j * 3 + q) % 2]):
                W(ge, prev)
                dma(chGC, ge, wb,
                    wmod_d[j * P:(j + 1) * P, q * 2048:(q + 1) * 2048])
            on("gpsimd", m_wl)
            tok_wm = use(chGC)

            for fi in range(4):
                ftile = q * 4 + fi
                def m_mm(te, j=j, fi=fi, ftile=ftile, wb=wb):
                    W(te, tok_wm, a_silu)
                    if j == 0 and ftile == 0:
                        W(te, v_mz)
                    bw(te, "C")
                    cPE.inc(te.matmul(pbC[0:2, :], silc[:, 2 * j:2 * j + 2],
                                      wb[:, fi * 512:(fi + 1) * 512],
                                      start=True, stop=True))
                on("tensor", m_mm)
                p_mm = cPE.tok()

                def m_acc(ve, ftile=ftile):
                    W(ve, p_mm)
                    sl = vMOD[:, ftile * 512:(ftile + 1) * 512]
                    cDV.inc(ve.tensor_tensor(out=sl, in0=sl, in1=pbC[0:2, :],
                                             op=ALU.add))
                on("vector", m_acc)
                bankfree["C"] = cDV.tok()
            wm_prev_pe[(j * 3 + q) % 2] = cPE.tok()
    v_mod = cDV.tok()
    p_modmm = cPE.tok()

    for v6, dest in [(0, sapm), (1, capm), (2, gapm), (3, sbpm), (4, cbpm),
                     (5, gbpm)]:
        for j in range(CH):
            def m_tr(te, v6=v6, j=j):
                W(te, v_mod)
                bw(te, "D")
                cPE.inc(te.transpose(
                    pbD[0:P, 0:2],
                    vMOD[:, v6 * DM + j * P: v6 * DM + (j + 1) * P],
                    cb_id[0:2, 0:2]))
            on("tensor", m_tr)
            p_t = cPE.tok()

            def m_ev(ve, dest=dest, j=j):
                W(ve, p_t)
                cDV.inc(ve.tensor_copy(dest[:, j, :], pbD[0:P, 0:2]))
            on("vector", m_ev)
            bankfree["D"] = cDV.tok()
    p_modtr = cPE.tok()

    def m_caps(ve):
        cDV.inc(ve.tensor_scalar_add(capm.ap(), capm.ap(), 1.0))
        cDV.inc(ve.tensor_scalar_add(cbpm.ap(), cbpm.ap(), 1.0))
    on("vector", m_caps)

    for g in (gapm, gbpm):
        gv = g.ap().rearrange("p j b -> p (j b)")
        v_pre = cDV.tok()

        def m_abs(se, gv=gv, v_pre=v_pre):
            W(se, v_pre)
            cAC.inc(se.activation(t32b[:, 0:CH * B], gv, AF.Abs))
        on("scalar", m_abs)
        a_abs = cAC.tok()

        def m_cl(ve, gv=gv, a_abs=a_abs):
            W(ve, a_abs)
            cDV.inc(ve.tensor_scalar(out=t32a[:, 0:CH * B], in0=gv,
                                     scalar1=0.0, scalar2=2.0,
                                     op0=ALU.is_ge, op1=ALU.mult))
            ve.wait_ge(cDV.h, cDV.n)
            cDV.inc(ve.tensor_scalar_sub(t32a[:, 0:CH * B], t32a[:, 0:CH * B], 1.0))
            cDV.inc(ve.tensor_scalar_max(t32b[:, 0:CH * B], t32b[:, 0:CH * B], 0.25))
            ve.wait_ge(cDV.h, cDV.n)
            cDV.inc(ve.tensor_tensor(out=gv, in0=t32a[:, 0:CH * B],
                                     in1=t32b[:, 0:CH * B], op=ALU.mult))
            ve.wait_ge(cDV.h, cDV.n)
        on("vector", m_cl)
    v_mpm = cDV.tok()

    early_pe = [p_modtr]
    early_cc = [use(chCC)]

    def c_init(ve):
        cDV.inc(ve.memset(hmax.ap(), -3.0e38))
        cDV.inc(ve.memset(umax.ap(), 0.0))
        cDV.inc(ve.memset(attn_sb.ap(), 0.0))
    on("vector", c_init)

    st = {
        "xbf_free": None,
        "h_free": [None] * NT,
        "k_free": None, "v_free": None, "g_free": None,
        "w_free": None,
        "xt_free": None, "x2_free": None,
        "xt_free2": [None, None], "x2_free2": [None, None],
        "w2_free": None,
        "c5_done": None,
        "wt_free": None,
        "scr_mid_free": None,
        "egt_free": None,
        "attn_free": None,
        "x2sp": [None, None],
        "h2rd": None,
    }
    first_k_evac = [True]
    first_v_evac = [True]
    first_g_evac = [True]

    for b in range(B):
        def c_wk(ge, b=b, wfree=st["w_free"]):
            W(ge, wfree)
            dma(chGC, ge, vW, wkT_d.rearrange("(j p) f -> p j f", p=P))
        on("gpsimd", c_wk)
        tok_wk = use(chGC)

        apply_toks = [None] * NT
        xbf_free = st.get("xbf_free2", [None, None])

        def emit_C1(jt, b=b):
            xb = vXBF0 if jt % 2 == 0 else vXBF1
            def c_xbf1(ge, jt=jt, xb=xb, prev=xbf_free[jt % 2]):
                W(ge, prev)
                if b == 1 and jt < 2:
                    W(ge, st["c5_done"])
                dma(chGC, ge, xb,
                    x_d[b, :, :, jt * TT:(jt + 1) * TT].rearrange("j p t -> p j t"))
            on("gpsimd", c_xbf1)
            tok_x1 = use(chGC)

            def c_sum1(te, first=(b == 0 and jt == 0), xb=xb):
                W(te, tok_x1, tok_cc)
                if first:
                    W(te, p_modmm)
                bw(te, "C")
                for j in range(CH):
                    mm = te.matmul(pbC[0:1, :], cb_o1b.ap(), xb[:, j, :],
                                   start=(j == 0), stop=(j == CH - 1))
                    if j == CH - 1:
                        cPE.inc(mm)
            on("tensor", c_sum1)
            p_s1 = cPE.tok()

            def c_sq(ve, xb=xb):
                W(ve, p_s1)
                xf = xb.rearrange("p j t -> p (j t)")
                cDV.inc(ve.tensor_mul(xf, xf, xf))
            on("vector", c_sq)
            v_sq = cDV.tok()

            def c_sum2(te, xb=xb):
                W(te, v_sq)
                bw(te, "D")
                for j in range(CH):
                    mm = te.matmul(pbD[0:1, :], cb_o1b.ap(), xb[:, j, :],
                                   start=(j == 0), stop=(j == CH - 1))
                    if j == CH - 1:
                        cPE.inc(mm)
            on("tensor", c_sum2)
            p_s2 = cPE.tok()

            def c_xbf2(ge, b=b, jt=jt, xb=xb):
                W(ge, p_s2)
                dma(chGC, ge, xb,
                    x_d[b, :, :, jt * TT:(jt + 1) * TT].rearrange("j p t -> p j t"))
            on("gpsimd", c_xbf2)
            tok_x2b = use(chGC)

            def c_rows(ve):
                W(ve, p_s1, p_s2)
                cDV.inc(ve.tensor_scalar_mul(r_m, pbC[0:1, :], 1.0 / DM))
                ve.wait_ge(cDV.h, cDV.n)
                cDV.inc(ve.tensor_tensor(out=r_var, in0=r_m, in1=r_m,
                                         op=ALU.mult))
                ve.wait_ge(cDV.h, cDV.n)
                cDV.inc(ve.scalar_tensor_tensor(out=r_var, in0=pbD[0:1, :],
                                                scalar=1.0 / DM, in1=r_var,
                                                op0=ALU.mult, op1=ALU.subtract))
            on("vector", c_rows)
            v_var = cDV.tok()

            def c_rowa(se):
                W(se, v_var)
                cAC.inc(se.activation(r_var, r_var, AF.Ln, bias=cb_kc[0:1, 1:2]))
                se.wait_ge(cAC.h, cAC.n)
                cAC.inc(se.activation(r_rstd, r_var, AF.Exp, scale=-0.5))
            on("scalar", c_rowa)
            a_rstd = cAC.tok()

            def c_row2(ve):
                W(ve, a_rstd)
                cDV.inc(ve.scalar_tensor_tensor(out=r_negmr, in0=r_m,
                                                scalar=-1.0, in1=r_rstd,
                                                op0=ALU.mult, op1=ALU.mult))
            on("vector", c_row2)
            v_rows = cDV.tok()

            def c_bcmm(te):
                W(te, v_rows)
                cPE.inc(te.matmul(pbC.ap(), cb_o1f.ap(), r_rstd,
                                  start=True, stop=True))
                cPE.inc(te.matmul(pbD.ap(), cb_o1f.ap(), r_negmr,
                                  start=True, stop=True))
            on("tensor", c_bcmm)
            p_bc = cPE.tok()

            def c_apply(ve, b=b, jt=jt, xb=xb, hfree=st["h_free"][jt]):
                W(ve, p_bc, tok_x2b, v_mpm, hfree)
                for j in range(CH):
                    ap1 = apt[:, 0:512]
                    cDV.inc(ve.tensor_tensor(out=ap1, in0=xb[:, j, :],
                                             in1=pbC.ap(), op=ALU.mult))
                    ve.wait_ge(cDV.h, cDV.n)
                    cDV.inc(ve.tensor_tensor(out=ap1, in0=ap1,
                                             in1=pbD.ap(), op=ALU.add))
                    ve.wait_ge(cDV.h, cDV.n)
                    cDV.inc(ve.tensor_scalar(out=vH[:, jt, j, :], in0=ap1,
                                             scalar1=capm[:, j, b:b + 1],
                                             scalar2=sapm[:, j, b:b + 1],
                                             op0=ALU.mult, op1=ALU.add))
                    ve.wait_ge(cDV.h, cDV.n)
            on("vector", c_apply)
            apply_toks[jt] = cDV.tok()
            bankfree["C"] = cDV.tok()
            bankfree["D"] = cDV.tok()
            xbf_free[jt % 2] = cDV.tok()

        kslots = [pbE.ap(), pbF.ap()]
        kkeys = ["E", "F"]

        def emit_K(jt, b=b):
            for jp2 in range(4):
                si = (jt * 4 + jp2) % 2
                dst = kslots[si]
                def c_kmm(te, jt=jt, jp2=jp2, si=si, dst=dst):
                    W(te, apply_toks[jt], tok_wk)
                    bw(te, kkeys[si])
                    rhs = vH[:, jt].rearrange("p j (n c) -> p j n c", c=P)[:, :, :, 0:64]
                    for half in range(2):
                        jo = jp2 * 2 + half
                        for jk in range(CH):
                            mm = te.matmul(dst[:, half * 256:(half + 1) * 256],
                                           vW[:, jk, jo * P:(jo + 1) * P],
                                           rhs[:, jk], start=(jk == 0),
                                           stop=(jk == CH - 1))
                            if jk == CH - 1 and half == 1:
                                cPE.inc(mm)
                on("tensor", c_kmm)
                p_k = cPE.tok()

                def c_kev(se, dst=dst, kfree=st["k_free"]):
                    W(se, p_k, kfree)
                    cAC.inc(se.activation(kraw.ap(), dst, AF.Copy))
                on("scalar", c_kev)
                a_kraw = cAC.tok()
                bankfree[kkeys[si]] = a_kraw

                def c_ksw(ge):
                    W(ge, a_kraw)
                    for (d0, s0) in [(0, 32), (32, 0), (64, 96), (96, 64)]:
                        dma(chVD, ge, kswp[d0:d0 + 32, :], kraw[s0:s0 + 32, :])
                on("gpsimd", c_ksw)
                tok_ksw = use(chVD)

                def c_krope(ve, jt=jt, jp2=jp2, first=first_k_evac[0]):
                    W(ve, tok_ksw, tok_cc)
                    if first:
                        W(ve, p_modmm, tok_agi2)
                        W(ve, early_cc[0])
                    csl = slice(jt * 256, (jt + 1) * 256)
                    cosw = cb_cos[:, None, csl].broadcast_to([P, 2, 256])
                    sinw = cb_sin[:, None, csl].broadcast_to([P, 2, 256])
                    cDV.inc(ve.tensor_tensor(out=kt1.ap().rearrange("p (a t) -> p a t", a=2),
                                             in0=kraw.ap().rearrange("p (a t) -> p a t", a=2),
                                             in1=cosw, op=ALU.mult))
                    cDV.inc(ve.tensor_tensor(out=kt2.ap().rearrange("p (a t) -> p a t", a=2),
                                             in0=kswp.ap().rearrange("p (a t) -> p a t", a=2),
                                             in1=sinw, op=ALU.mult))
                    ve.wait_ge(cDV.h, cDV.n)
                    cDV.inc(ve.tensor_tensor(
                        out=vK[:, jp2 * 2:jp2 * 2 + 2, csl],
                        in0=kt1.ap().rearrange("p (a t) -> p a t", a=2),
                        in1=kt2.ap().rearrange("p (a t) -> p a t", a=2),
                        op=ALU.add))
                on("vector", c_krope)
                first_k_evac[0] = False
                st["k_free"] = cDV.tok()

        def c_wv(ge, b=b, wfree=st["w2_free"]):
            W(ge, wfree)
            if b == 0:
                W(ge, tok_agi2, p_modmm)
            dma(chGC, ge, vW2, wvT_d.rearrange("(j p) f -> p j f", p=P))
        on("gpsimd", c_wv)
        tok_wv = use(chGC)

        def emit_V(jt, b=b):
            for sub in range(4):
                nchunk = jt * 4 + sub
                vbank = pbA.ap() if nchunk % 2 == 0 else pbB.ap()
                vkey = "A01" if nchunk % 2 == 0 else "B23"
                def c_vmm(te, jt=jt, sub=sub, vbank=vbank, vkey=vkey,
                          first=(jt == 0 and sub < 2)):
                    W(te, apply_toks[jt], tok_wv)
                    bw(te, vkey)
                    if first:
                        bw(te, "B2", "B3")
                    for jk in range(CH):
                        for fh in range(2):
                            mm = te.matmul(vbank[:, fh * 512:(fh + 1) * 512],
                                           vH[:, jt, jk, sub * P:(sub + 1) * P],
                                           vW2[:, jk, fh * 512:(fh + 1) * 512],
                                           start=(jk == 0), stop=(jk == CH - 1))
                            if jk == CH - 1 and fh == 1:
                                cPE.inc(mm)
                on("tensor", c_vmm)
                p_v = cPE.tok()

                def c_vev(se, nchunk=nchunk, vbank=vbank, vfree=st["v_free"],
                          first=first_v_evac[0]):
                    W(se, p_v, vfree)
                    if first:
                        W(se, p_modmm, tok_agi2)
                    cAC.inc(se.activation(vV[0:64, nchunk, :], vbank[0:64, :],
                                          AF.Copy))
                on("scalar", c_vev)
                first_v_evac[0] = False
                a_vev = cAC.tok()
                bankfree[vkey] = a_vev
                if nchunk % 2 == 0:
                    bankfree["A0"] = a_vev
                    bankfree["A1"] = a_vev
                else:
                    bankfree["B2"] = a_vev
                    bankfree["B3"] = a_vev

                def c_vdup(ge, nchunk=nchunk):
                    W(ge, a_vev)
                    dma(chVD, ge, vV[64:128, nchunk, :], vV[0:64, nchunk, :])
                on("gpsimd", c_vdup)

        for jt in range(NT):
            emit_C1(jt)
            if jt > 0:
                emit_K(jt - 1)
            if jt > 1:
                emit_V(jt - 2)
        emit_K(NT - 1)
        emit_V(NT - 2)
        emit_V(NT - 1)
        st["xbf_free2"] = xbf_free
        k_done = cDV.tok()
        v_done = use(chVD)
        v_done_pe = cPE.tok()

        def c_wg(ge, b=b, kpe=k_done):
            W(ge, kpe)
            dma(chGC, ge, vW, wgT_d.rearrange("(j p) f -> p j f", p=P))
        on("gpsimd", c_wg)
        tok_wg = use(chGC)

        for jt in range(NT):
            for jo in range(CH):
                gbank = pbF.ap() if jo % 2 == 0 else pbE.ap()
                gkey = "F" if jo % 2 == 0 else "E"
                def c_gmm(te, jt=jt, jo=jo, gbank=gbank, gkey=gkey):
                    W(te, apply_toks[jt], tok_wg)
                    bw(te, gkey)
                    for jk in range(CH):
                        mm = te.matmul(gbank, vW[:, jk, jo * P:(jo + 1) * P],
                                       vH[:, jt, jk, :], start=(jk == 0),
                                       stop=(jk == CH - 1))
                        if jk == CH - 1:
                            cPE.inc(mm)
                on("tensor", c_gmm)
                p_g = cPE.tok()

                def c_gex(se, gbank=gbank, egfree=st["egt_free"]):
                    W(se, p_g, egfree)
                    cAC.inc(se.activation(egt.ap(), gbank, AF.Exp, scale=0.125))
                on("scalar", c_gex)
                a_eg = cAC.tok()
                bankfree[gkey] = a_eg

                def c_gden(te):
                    W(te, a_eg, tok_cc)
                    bw(te, "B3")
                    cPE.inc(te.matmul(pbB[0:2, 512:1024], cb_ob.ap(), egt.ap(),
                                      start=True, stop=True))
                on("tensor", c_gden)
                p_gd = cPE.tok()

                def c_grd(ve):
                    W(ve, p_gd)
                    cDV.inc(ve.reciprocal(rdgf.ap(), pbB[0:2, 512:1024]))
                on("vector", c_grd)
                v_rdg = cDV.tok()
                bankfree["B3"] = v_rdg

                def c_grdb(te):
                    W(te, v_rdg)
                    bw(te, "B2")
                    cPE.inc(te.matmul(pbB[:, 0:512], cb_b2f.ap(), rdgf.ap(),
                                      start=True, stop=True))
                on("tensor", c_grdb)
                p_grdb = cPE.tok()

                def c_gn(ve, jt=jt, jo=jo, first=first_g_evac[0]):
                    W(ve, p_grdb)
                    if first:
                        W(ve, p_modmm, tok_agi2)
                    if jo >= 4 and jt == 0:
                        W(ve, v_done_pe)
                    cDV.inc(ve.tensor_tensor(out=vG[:, jo, jt * TT:(jt + 1) * TT],
                                             in0=egt.ap(), in1=pbB[:, 0:512],
                                             op=ALU.mult))
                on("vector", c_gn)
                first_g_evac[0] = False
                st["egt_free"] = cDV.tok()
                bankfree["B2"] = cDV.tok()
        g_done = cDV.tok()
        g_done_pe = cPE.tok()

        for jp in range(JP):
            def c_sc(te, jp=jp):
                W(te, k_done, tok_cc)
                if jp == 0:
                    W(te, st["scr_mid_free"])
                bw(te, "A01")
                for n in range(NCK):
                    osl = slice(n * 64, (n + 1) * 64)
                    te.matmul(pbA[0:64, osl], vK[0:64, jp, osl],
                              cb_qpk[0:64, jp, :], start=True, stop=True,
                              tile_position=(0, 0))
                    mm2 = te.matmul(pbA[64:128, osl], vK[64:128, jp, osl],
                                    cb_qpk[64:128, jp, :], start=True, stop=True,
                                    tile_position=(64, 64))
                    if n == NCK - 1:
                        cPE.inc(mm2)
            on("tensor", c_sc)
            p_sc = cPE.tok()

            def c_smsk(ve, jp=jp):
                W(ve, p_sc, tok_cst)
                if jp == 0:
                    W(ve, st["scr_mid_free"], xbf_free[0])
                cDV.inc(ve.tensor_tensor(
                    out=vSM.rearrange("p (n l) -> p n l", n=NCK),
                    in0=pbA.ap().rearrange("p (n l) -> p n l", n=NCK),
                    in1=cb_mneg[:, None, :].broadcast_to([P, NCK, 64]),
                    op=ALU.add))
            on("vector", c_smsk)
            v_sm = cDV.tok()
            bankfree["A01"] = v_sm

            def c_sexp(se):
                W(se, v_sm)
                cAC.inc(se.activation(vESC, vSM, AF.Exp, scale=0.125))
            on("scalar", c_sexp)
            a_esc = cAC.tok()

            def c_sden(te):
                W(te, a_esc)
                bw(te, "C", "D")
                cPE.inc(te.matmul(pbC[0:2, :], cb_ob.ap(), vESC[:, 0:512],
                                  start=True, stop=True))
                cPE.inc(te.matmul(pbD[0:2, :], cb_ob.ap(), vESC[:, 512:1024],
                                  start=True, stop=True))
            on("tensor", c_sden)
            p_sd = cPE.tok()

            def c_srd(ve):
                W(ve, p_sd)
                cDV.inc(ve.reciprocal(vRDN[:, 0:512], pbC[0:2, :]))
                cDV.inc(ve.reciprocal(vRDN[:, 512:1024], pbD[0:2, :]))
            on("vector", c_srd)
            v_srd = cDV.tok()
            bankfree["C"] = v_srd
            bankfree["D"] = v_srd

            def c_srdb(te):
                W(te, v_srd)
                bw(te, "B2", "B3")
                cPE.inc(te.matmul(pbB[:, 0:512], cb_b2f.ap(), vRDN[:, 0:512],
                                  start=True, stop=True))
                cPE.inc(te.matmul(pbB[:, 512:1024], cb_b2f.ap(), vRDN[:, 512:1024],
                                  start=True, stop=True))
            on("tensor", c_srdb)
            p_srdb = cPE.tok()

            def c_pn(ve):
                W(ve, p_srdb)
                cDV.inc(ve.tensor_tensor(out=vPRB[:, 0:512], in0=vESC[:, 0:512],
                                         in1=pbB[:, 0:512], op=ALU.mult))
                cDV.inc(ve.tensor_tensor(out=vPRB[:, 512:1024],
                                         in0=vESC[:, 512:1024],
                                         in1=pbB[:, 512:1024], op=ALU.mult))
            on("vector", c_pn)
            v_prb = cDV.tok()
            bankfree["B2"] = v_prb
            bankfree["B3"] = v_prb

            def c_at(te, jp=jp):
                W(te, v_prb, v_done)
                bw(te, "E")
                for n in range(NCK):
                    osl = slice(n * 64, (n + 1) * 64)
                    te.matmul(pbE[0:64, 0:64], vPRB[0:64, osl],
                              vV[0:64, n, jp * P:jp * P + 64],
                              start=(n == 0), stop=(n == NCK - 1),
                              tile_position=(0, 0))
                    mm2 = te.matmul(pbE[64:128, 0:64], vPRB[64:128, osl],
                                    vV[64:128, n, jp * P + 64:(jp + 1) * P],
                                    start=(n == 0), stop=(n == NCK - 1),
                                    tile_position=(64, 64))
                    if n == NCK - 1:
                        cPE.inc(mm2)
            on("tensor", c_at)
            p_at = cPE.tok()

            def c_atev(se, jp=jp, afree=st["attn_free"]):
                W(se, p_at)
                if jp == 0:
                    W(se, afree)
                cAC.inc(se.activation(attn_sb[:, jp, :], pbE[:, 0:64], AF.Copy))
            on("scalar", c_atev)
            bankfree["E"] = cAC.tok()
        at_done = cAC.tok()
        mid_pe = cPE.tok()

        def c_wp(ge, b=b):
            W(ge, g_done_pe)
            dma(chGC, ge, vW, wpT_d.rearrange("(j p) f -> p j f", p=P))
        on("gpsimd", c_wp)
        tok_wp = use(chGC)

        if DEBUG and b == 1:
            dg_g = nc.dram_tensor("dbg_gates", [P, CH, TB], BF16,
                                  kind="ExternalOutput").ap()
            dg_w = nc.dram_tensor("dbg_wp", [P, CH, DM], BF16,
                                  kind="ExternalOutput").ap()
            def dbg_mid(sy):
                W(sy, g_done, tok_wp)
                dma(chOUT, sy, dg_g, vG)
                dma(chOUT, sy, dg_w, vW2)
            on("sync", dbg_mid)
            st["dbg_mid_tok"] = use(chOUT)

        resid_toks = [None] * NT
        x2sp_toks = [None] * NT

        def emit_A(jt, b=b):
            sl2 = jt % 2
            for jp in range(JP):
                def c_wt(te, jt=jt, jp=jp):
                    W(te, at_done, g_done)
                    if jp == 0 and jt == 0:
                        W(te, mid_pe)
                    bw(te, "F")
                    te.matmul(pbF[0:64, :], attn_sb[0:64, jp, :],
                              vG[0:64, jp, jt * TT:(jt + 1) * TT],
                              start=True, stop=True, tile_position=(0, 0))
                    mm2 = te.matmul(pbF[64:128, :], attn_sb[64:128, jp, :],
                                    vG[64:128, jp, jt * TT:(jt + 1) * TT],
                                    start=True, stop=True, tile_position=(64, 64))
                    cPE.inc(mm2)
                on("tensor", c_wt)
                p_wt = cPE.tok()

                def c_wtev(se, jp=jp, jt=jt, wtfree=st["wt_free"]):
                    W(se, p_wt)
                    if jp == 0:
                        W(se, wtfree)
                        if jt == 0:
                            W(se, mid_pe, apply_toks[NT - 1])
                    cAC.inc(se.activation(vWT[:, jp, :], pbF.ap(), AF.Copy))
                on("scalar", c_wtev)
                bankfree["F"] = cAC.tok()
            wt_done = cAC.tok()

            def c_xt(sy, b=b, jt=jt, sl2=sl2, xtfree=st["xt_free2"][sl2]):
                W(sy, xtfree, g_done_pe)
                if jt < 2:
                    W(sy, apply_toks[NT - 1])
                dma(chX, sy, vXTd[sl2],
                    x_d[b, :, :, jt * TT:(jt + 1) * TT].rearrange("j p t -> p j t"))
            on("sync", c_xt)
            tok_xt = use(chX)

            for jo in range(CH):
                bank = "E" if jo % 2 == 0 else "C"
                def c_pmm(te, jt=jt, jo=jo, bank=bank):
                    W(te, wt_done, tok_wp)
                    bw(te, bank)
                    dst = pbE.ap() if bank == "E" else pbC.ap()
                    for jk in range(CH):
                        mm = te.matmul(dst, vW[:, jk, jo * P:(jo + 1) * P],
                                       vWT[:, jk, :], start=(jk == 0),
                                       stop=(jk == CH - 1))
                        if jk == CH - 1:
                            cPE.inc(mm)
                on("tensor", c_pmm)
                p_p = cPE.tok()

                def c_pev(ve, b=b, jt=jt, jo=jo, bank=bank, sl2=sl2,
                          x2free=st["x2_free2"][sl2]):
                    W(ve, p_p, tok_xt)
                    if jo == 0:
                        W(ve, x2free)
                    src = pbE.ap() if bank == "E" else pbC.ap()
                    cDV.inc(ve.scalar_tensor_tensor(
                        out=vX2d[sl2][:, jo, :], in0=src,
                        scalar=gapm[:, jo, b:b + 1], in1=vXTd[sl2][:, jo, :],
                        op0=ALU.mult, op1=ALU.add))
                on("vector", c_pev)
                bankfree[bank] = cDV.tok()
            resid_toks[jt] = cDV.tok()
            st["wt_free"] = cPE.tok()

            def c_x2sp(sy, b=b, jt=jt, sl2=sl2):
                W(sy, resid_toks[jt])
                dma(chSP, sy, x2_dram[b][:, :, jt * TT:(jt + 1) * TT], vX2d[sl2])
            on("sync", c_x2sp)
            x2sp_toks[jt] = use(chSP)

        def emit_B(jt, b=b):
            sl2 = jt % 2
            x2v = vX2d[sl2]
            h2v_ = vXTd[sl2]
            def c2_sum1(te, jt=jt, x2v=x2v):
                W(te, resid_toks[jt])
                bw(te, "B2")
                for j in range(CH):
                    mm = te.matmul(pbB[0:1, 0:512], cb_o1cf.ap(), x2v[:, j, :],
                                   start=(j == 0), stop=(j == CH - 1))
                    if j == CH - 1:
                        cPE.inc(mm)
            on("tensor", c2_sum1)
            p2_s1 = cPE.tok()

            sq_toks = []
            mm_toks = []
            for grp in range(4):
                def c2_sqv(ve, grp=grp, x2v=x2v,
                           prev=(mm_toks[grp - 2] if grp >= 2 else None)):
                    W(ve, prev)
                    if grp < 2:
                        W(ve, resid_toks[jt])
                    buf = apt if grp % 2 == 0 else apt2
                    cDV.inc(ve.tensor_tensor(
                        out=buf.ap().rearrange("p (a t) -> p a t", a=2),
                        in0=x2v[:, grp * 2:grp * 2 + 2, :],
                        in1=x2v[:, grp * 2:grp * 2 + 2, :], op=ALU.mult))
                on("vector", c2_sqv)
                sq_toks.append(cDV.tok())

                def c2_sqm(te, grp=grp, tok=sq_toks[grp]):
                    W(te, tok)
                    if grp == 0:
                        bw(te, "B3")
                    buf = apt if grp % 2 == 0 else apt2
                    for h in range(2):
                        mm = te.matmul(pbB[0:1, 512:1024], cb_o1cf.ap(),
                                       buf[:, h * 512:(h + 1) * 512],
                                       start=(grp == 0 and h == 0),
                                       stop=(grp == 3 and h == 1))
                        if h == 1:
                            cPE.inc(mm)
                on("tensor", c2_sqm)
                mm_toks.append(cPE.tok())
            p2_s2 = cPE.tok()

            def c2_rows(ve):
                W(ve, p2_s1, p2_s2)
                cDV.inc(ve.tensor_scalar_mul(r_m, pbB[0:1, 0:512], 1.0 / DM))
                ve.wait_ge(cDV.h, cDV.n)
                cDV.inc(ve.tensor_tensor(out=r_var, in0=r_m, in1=r_m,
                                         op=ALU.mult))
                ve.wait_ge(cDV.h, cDV.n)
                cDV.inc(ve.scalar_tensor_tensor(out=r_var, in0=pbB[0:1, 512:1024],
                                                scalar=1.0 / DM, in1=r_var,
                                                op0=ALU.mult, op1=ALU.subtract))
            on("vector", c2_rows)
            v2_var = cDV.tok()

            def c2_rowa(se):
                W(se, v2_var)
                cAC.inc(se.activation(r_var, r_var, AF.Ln, bias=cb_kc[0:1, 1:2]))
                se.wait_ge(cAC.h, cAC.n)
                cAC.inc(se.activation(r_rstd, r_var, AF.Exp, scale=-0.5))
            on("scalar", c2_rowa)
            a2_rstd = cAC.tok()

            def c2_row2(ve):
                W(ve, a2_rstd)
                cDV.inc(ve.scalar_tensor_tensor(out=r_negmr, in0=r_m,
                                                scalar=-1.0, in1=r_rstd,
                                                op0=ALU.mult, op1=ALU.mult))
            on("vector", c2_row2)
            v2_rows = cDV.tok()

            def c2_bcmm(te):
                W(te, v2_rows)
                cPE.inc(te.matmul(pbB[:, 0:512], cb_o1f.ap(), r_rstd,
                                  start=True, stop=True))
                cPE.inc(te.matmul(pbB[:, 512:1024], cb_o1f.ap(), r_negmr,
                                  start=True, stop=True))
            on("tensor", c2_bcmm)
            p2_bc = cPE.tok()

            def c2_apply(ve, b=b, jt=jt, x2v=x2v, h2v_=h2v_):
                W(ve, p2_bc)
                for j in range(CH):
                    ap1 = apt[:, 0:512]
                    cDV.inc(ve.tensor_tensor(out=ap1, in0=x2v[:, j, :],
                                             in1=pbB[:, 0:512], op=ALU.mult))
                    ve.wait_ge(cDV.h, cDV.n)
                    cDV.inc(ve.tensor_tensor(out=ap1, in0=ap1,
                                             in1=pbB[:, 512:1024], op=ALU.add))
                    ve.wait_ge(cDV.h, cDV.n)
                    cDV.inc(ve.tensor_scalar(out=h2v_[:, j, :], in0=ap1,
                                             scalar1=cbpm[:, j, b:b + 1],
                                             scalar2=sbpm[:, j, b:b + 1],
                                             op0=ALU.mult, op1=ALU.add))
                    ve.wait_ge(cDV.h, cDV.n)
                cDV.inc(ve.tensor_reduce(out=tmax.ap(), in_=h2v_, axis=AX.XY,
                                         op=ALU.max))
                ve.wait_ge(cDV.h, cDV.n)
                cDV.inc(ve.tensor_tensor(out=hmax.ap(), in0=hmax.ap(),
                                         in1=tmax.ap(), op=ALU.max))
                ve.wait_ge(cDV.h, cDV.n)
            on("vector", c2_apply)
            v2_h2 = cDV.tok()
            bankfree["B2"] = v2_h2
            bankfree["B3"] = v2_h2
            st["x2_free2"][sl2] = v2_h2

            def c2_h2sp(sy, b=b, jt=jt, sl2=sl2):
                W(sy, v2_h2, x2sp_toks[jt])
                dma(chSP, sy, h2_dram[b][:, :, jt * TT:(jt + 1) * TT], vXTd[sl2])
            on("sync", c2_h2sp)
            tok_h2sp = use(chSP)
            st.setdefault("h2sp_tok", {})[(b, jt)] = tok_h2sp
            st["xt_free2"][sl2] = tok_h2sp
            st["scr_mid_free"] = cPE.tok()

        for jt in range(NT):
            emit_A(jt)
            if jt > 0:
                emit_B(jt - 1)
        emit_B(NT - 1)

        for jt in range(NT):
            st["h_free"][jt] = g_done_pe
        st["k_free"] = mid_pe
        st["v_free"] = mid_pe
        st["kv_free"] = mid_pe
        st["g_free"] = cPE.tok()
        st["w_free"] = cPE.tok()
        st["w2_free"] = cPE.tok()
        bankfree["E"] = cPE.tok()
        bankfree["F"] = cPE.tok()
        st["attn_free"] = cPE.tok()
        st["c5_done"] = cDV.tok()
        st["xt_free"] = st["xt_free2"][0]
        st["x2_free"] = st["x2_free2"][1]
        st["scr_free_b"] = st["wt_free"]
        st["str_free_b"] = (st["x2_free2"][0], x2sp_toks[2])

    h2_all_sp = use(chSP)

    def b_ld3(ge):
        W(ge, v_ag3)
        dma(chCC, ge, sc40.ap(), ago_sc)
    on("gpsimd", b_ld3)
    tok_sc40 = use(chCC)

    def b_tr3(te):
        W(te, tok_sc40)
        bw(te, "D")
        cPE.inc(te.transpose(pbD[0:P, 0:40], sc40.ap(), cb_id[0:40, 0:40]))
    on("tensor", b_tr3)
    p_t3 = cPE.tok()

    def b_te3(ve):
        W(ve, p_t3)
        cDV.inc(ve.tensor_copy(scT.ap(), pbD[0:P, 0:40]))
        ve.wait_ge(cDV.h, cDV.n)
        sv = scT.ap().rearrange("p (r k) -> p r k", r=NCORES)
        cDV.inc(ve.tensor_copy(am1pm.ap().rearrange("p (r k) -> p r k", r=NCORES),
                               sv[:, :, 0:4]))
        cDV.inc(ve.tensor_copy(am2pm.ap(), sv[:, :, 4]))
    on("vector", b_te3)
    bankfree["D"] = cDV.tok()
    v_ampm = cDV.tok()



    def g1_tr(te):
        W(te, cDV.tok())
        bw(te, "D")
        cPE.inc(te.transpose(pbD[0:1, 0:P], hmax.ap(), cb_id.ap()))
    on("tensor", g1_tr)
    p_g1t = cPE.tok()

    def g1_r(ve):
        W(ve, p_g1t)
        cDV.inc(ve.tensor_reduce(out=gm1.ap(), in_=pbD[0:1, 0:P], axis=AX.X,
                                 op=ALU.max))
        ve.wait_ge(cDV.h, cDV.n)
        cDV.inc(ve.tensor_scalar_max(gm1.ap(), gm1.ap(), QEPS))
    on("vector", g1_r)
    bankfree["D"] = cDV.tok()
    v_g1 = cDV.tok()

    def g1_ar(ge):
        W(ge, v_g1)
        dma(chCC, ge, ar1_in, gm1.ap())
        ge.wait_ge(chCC.h, chCC.n)
        chCC.pending_epoch = False
        cCC.inc(ge.collective_compute("AllReduce", ALU.max, replica_groups=RG,
                                      ins=[ar1_in], outs=[ar1_out]))
        ge.wait_ge(cCC.h, cCC.n)
        dma(chCC, ge, gm1.ap(), ar1_out)
    on("gpsimd", g1_ar)
    tok_gm1 = use(chCC)

    def g1_rr(ve):
        W(ve, tok_gm1)
        cDV.inc(ve.reciprocal(r1row[0:1, 0:1], gm1.ap()))
        ve.wait_ge(cDV.h, cDV.n)
        cDV.inc(ve.tensor_scalar_mul(r1row[0:1, 0:1], r1row[0:1, 0:1], 255.0))
        cDV.inc(ve.tensor_scalar_mul(r1row[0:1, 1:2], gm1.ap(), 1.0 / 255.0))
    on("vector", g1_rr)
    v_r1r = cDV.tok()

    def g1_mm(te):
        W(te, v_r1r)
        bw(te, "D")
        cPE.inc(te.matmul(pbD[:, 0:2], cb_o1f.ap(), r1row.ap(),
                          start=True, stop=True))
    on("tensor", g1_mm)
    p_r1 = cPE.tok()

    def g1_c(ve):
        W(ve, p_r1, v_ampm)
        cDV.inc(ve.tensor_copy(r1pm.ap(), pbD[:, 0:2]))
        ve.wait_ge(cDV.h, cDV.n)
        cDV.inc(ve.tensor_scalar(out=dq1pm.ap(), in0=am1pm.ap(),
                                 scalar1=r1pm[:, 1:2], scalar2=1.0 / 127.0,
                                 op0=ALU.mult, op1=ALU.mult))
        cDV.inc(ve.reciprocal(t32a.ap(), am1pm.ap()))
        ve.wait_ge(cDV.h, cDV.n)
        cDV.inc(ve.tensor_scalar(out=t32b.ap(), in0=cb_b1t.ap(),
                                 scalar1=r1pm[:, 0:1], scalar2=127.0,
                                 op0=ALU.mult, op1=ALU.mult))
        ve.wait_ge(cDV.h, cDV.n)
        cDV.inc(ve.tensor_tensor(out=qb1pm.ap(), in0=t32b.ap(), in1=t32a.ap(),
                                 op=ALU.mult))
        ve.wait_ge(cDV.h, cDV.n)
        cDV.inc(ve.tensor_scalar(out=qb1pm.ap(), in0=qb1pm.ap(),
                                 scalar1=MAGIC, scalar2=MAGIC,
                                 op0=ALU.add, op1=ALU.subtract))
        ve.wait_ge(cDV.h, cDV.n)
        cDV.inc(ve.tensor_tensor(out=qbdq1.ap(), in0=qb1pm.ap(),
                                 in1=dq1pm.ap(), op=ALU.mult))
        ve.wait_ge(cDV.h, cDV.n)
    on("vector", g1_c)
    bankfree["D"] = cDV.tok()
    v_q1c = cDV.tok()

    def f_qw1l(seq):
        W(seq, v_ag1)
        W(seq, st["kv_free"])
        if DEBUG:
            W(seq, st.get("dbg_mid_tok"))
        for j in range(6):
            chW.inc(seq.dma_start_transpose(vQW1[:, j, :],
                                            ago_qw1[:, j * P:(j + 1) * P]))
        W(seq, st["w_free"])
        for j in range(6, CH):
            chW.inc(seq.dma_start_transpose(vQW1[:, j, :],
                                            ago_qw1[:, j * P:(j + 1) * P]))
    on("scalar", f_qw1l)
    tok_qw1 = use(chW)

    fst = {"h2_free": [None, None], "qh_free": [None, None],
           "ust_free": [None, None]}
    prep = {}

    def emit_prep(tg):
        b, jt = tg // 4, tg % 4
        sl2 = tg % 2
        h2b = vH2T[:, sl2]

        def f1_h2l(seq, b=b, jt=jt, h2b=h2b, prev=fst["h2_free"][sl2],
                   first=(tg == 0), sp_tok=st["h2sp_tok"][(tg // 4, tg % 4)]):
            W(seq, sp_tok, prev)
            if first:
                W(seq, st["w_free"], st["xt_free2"][0], st["xt_free2"][1],
                  st["c5_done"])
            dma(chUL, seq, h2b, h2_dram[b][:, :, jt * TT:(jt + 1) * TT])
        on("scalar", f1_h2l)
        tok_h2 = use(chUL)

        def f1_rd(se, h2b=h2b):
            W(se, tok_h2, v_q1c)
            hb = h2b.rearrange("p j t -> p (j t)")
            cAC.inc(se.activation(hb, hb, AF.Copy, bias=MAGIC,
                                  scale=r1pm[:, 0:1]))
        on("scalar", f1_rd)
        prep[tg] = cAC.tok()

    emit_prep(0)
    for tg in range(8):
        b, jt = tg // 4, tg % 4
        sl2 = tg % 2
        h2b = vH2T[:, sl2]
        qh = vQH[:, sl2]
        a_rd = prep[tg]

        def f1_cl(ve, h2b=h2b, qh=qh, a_rd=a_rd, prev=fst["qh_free"][sl2],
                  first=(tg == 0)):
            W(ve, a_rd, prev)
            if first:
                W(ve, st["w_free"], st["xt_free2"][1])
            hb = h2b.rearrange("p j t -> p (j t)")
            cDV.inc(ve.tensor_scalar(out=hb, in0=hb, scalar1=MAGIC, scalar2=0.0,
                                     op0=ALU.subtract, op1=ALU.max))
            ve.wait_ge(cDV.h, cDV.n)
            cDV.inc(ve.tensor_scalar_min(qh.rearrange("p j t -> p (j t)"),
                                         hb, 255.0))
        on("vector", f1_cl)
        v_qh = cDV.tok()
        fst["h2_free"][sl2] = v_qh
        for _f in fst.get("pending_um", []):
            _f()
        fst["pending_um"] = []

        for g in range(4):
            ub = vUST[:, g % 2]
            for fi in range(CH):
                ft = g * CH + fi
                bank = "E" if ft % 2 == 0 else "F"
                def f1_mm(te, ft=ft, qh=qh, bank=bank, first=(tg == 0 and ft == 0)):
                    W(te, v_qh, tok_qw1)
                    bw(te, bank)
                    dst = pbE.ap() if bank == "E" else pbF.ap()
                    for jk in range(CH):
                        mm = te.matmul(dst, vQW1[:, jk, ft * P:(ft + 1) * P],
                                       qh[:, jk, :], start=(jk == 0),
                                       stop=(jk == CH - 1))
                        if jk == CH - 1:
                            cPE.inc(mm)
                on("tensor", f1_mm)
                p_f1 = cPE.tok()

                if ft % 4 != 3:
                    def f1_ev(se, ft=ft, fi=fi, ub=ub, bank=bank,
                              prev=fst["ust_free"][g % 2] if fi == 0 else None,
                              first=(tg == 0 and g < 2 and fi == 0)):
                        if prev is not None:
                            ptok, pum = prev
                            W(se, p_f1, v_q1c, ptok, pum[0])
                        else:
                            W(se, p_f1, v_q1c)
                        if first:
                            W(se, st["x2_free2"][0], st["x2_free2"][1],
                              st["c5_done"])
                        src = pbE.ap() if bank == "E" else pbF.ap()
                        cAC.inc(se.activation(ub[:, fi, :], src, AF.Relu,
                                              bias=qbdq1[:, ft:ft + 1],
                                              scale=dq1pm[:, ft:ft + 1]))
                    on("scalar", f1_ev)
                    bankfree[bank] = cAC.tok()
                else:
                    def f1_evv(ve, ft=ft, fi=fi, ub=ub, bank=bank):
                        W(ve, p_f1, v_q1c)
                        src = pbE.ap() if bank == "E" else pbF.ap()
                        cDV.inc(ve.tensor_scalar(out=apt[:, 0:512], in0=src,
                                                 scalar1=qb1pm[:, ft:ft + 1],
                                                 scalar2=0.0,
                                                 op0=ALU.add, op1=ALU.max))
                        ve.wait_ge(cDV.h, cDV.n)
                        cDV.inc(ve.tensor_scalar_mul(ub[:, fi, :], apt[:, 0:512],
                                                     dq1pm[:, ft:ft + 1]))
                    on("vector", f1_evv)
                    bankfree[bank] = cDV.tok()
            a_ug = (cAC.tok(), cDV.tok())

            um_cell = [None]

            def emit_um(a_ug=a_ug, ub=ub, um_cell=um_cell):
                def f1_um(ve):
                    W(ve, *a_ug)
                    cDV.inc(ve.tensor_reduce(out=tmax.ap(), in_=ub, axis=AX.XY,
                                             op=ALU.max))
                    ve.wait_ge(cDV.h, cDV.n)
                    cDV.inc(ve.tensor_tensor(out=umax.ap(), in0=umax.ap(),
                                             in1=tmax.ap(), op=ALU.max))
                    ve.wait_ge(cDV.h, cDV.n)
                on("vector", f1_um)
                um_cell[0] = cDV.tok()
            if g < 2:
                emit_um()
            else:
                fst.setdefault("pending_um", []).append(emit_um)

            def f1_sp(sy, tg=tg, g=g, ub=ub):
                W(sy, *a_ug)
                dma(chSP, sy, u_dram[tg, g], ub)
            on("sync", f1_sp)
            tok_us = use(chSP)
            fst["ust_free"][g % 2] = (tok_us, um_cell)
            if g == 1 and tg < 7:
                emit_prep(tg + 1)
        fst["qh_free"][sl2] = cPE.tok()
    for _f in fst.get("pending_um", []):
        _f()
    fst["pending_um"] = []
    f1_sp_all = use(chSP)
    f1_pe_all = cPE.tok()

    def g2_tr(te):
        W(te, cDV.tok())
        bw(te, "D")
        cPE.inc(te.transpose(pbD[0:1, 0:P], umax.ap(), cb_id.ap()))
    on("tensor", g2_tr)
    p_g2t = cPE.tok()

    def g2_r(ve):
        W(ve, p_g2t)
        cDV.inc(ve.tensor_reduce(out=gm2.ap(), in_=pbD[0:1, 0:P], axis=AX.X,
                                 op=ALU.max))
        ve.wait_ge(cDV.h, cDV.n)
        cDV.inc(ve.tensor_scalar_max(gm2.ap(), gm2.ap(), QEPS))
    on("vector", g2_r)
    bankfree["D"] = cDV.tok()
    v_g2 = cDV.tok()

    def g2_ar(ge):
        W(ge, v_g2)
        dma(chCC, ge, ar2_in, gm2.ap())
        ge.wait_ge(chCC.h, chCC.n)
        chCC.pending_epoch = False
        cCC.inc(ge.collective_compute("AllReduce", ALU.max, replica_groups=RG,
                                      ins=[ar2_in], outs=[ar2_out]))
        ge.wait_ge(cCC.h, cCC.n)
        dma(chCC, ge, gm2.ap(), ar2_out)
    on("gpsimd", g2_ar)
    tok_gm2 = use(chCC)

    def g2_rr(ve):
        W(ve, tok_gm2)
        cDV.inc(ve.reciprocal(r2row[0:1, 0:1], gm2.ap()))
        ve.wait_ge(cDV.h, cDV.n)
        cDV.inc(ve.tensor_scalar_mul(r2row[0:1, 0:1], r2row[0:1, 0:1], 255.0))
        cDV.inc(ve.tensor_scalar_mul(r2row[0:1, 1:2], gm2.ap(), 1.0 / 255.0))
    on("vector", g2_rr)
    v_r2r = cDV.tok()

    def g2_mm(te):
        W(te, v_r2r)
        bw(te, "D")
        cPE.inc(te.matmul(pbD[:, 0:2], cb_o1f.ap(), r2row.ap(),
                          start=True, stop=True))
    on("tensor", g2_mm)
    p_r2 = cPE.tok()

    def g2_c(ve):
        W(ve, p_r2, v_ampm, v_mpm)
        cDV.inc(ve.tensor_copy(r2pm.ap(), pbD[:, 0:2]))
        ve.wait_ge(cDV.h, cDV.n)
        cDV.inc(ve.tensor_scalar(out=dq2pm.ap(), in0=am2pm.ap(),
                                 scalar1=r2pm[:, 1:2], scalar2=1.0 / 127.0,
                                 op0=ALU.mult, op1=ALU.mult))
        cDV.inc(ve.reciprocal(t32a[:, 0:CH], am2pm.ap()))
        ve.wait_ge(cDV.h, cDV.n)
        cDV.inc(ve.tensor_scalar(out=t32b[:, 0:CH], in0=cb_b2t.ap(),
                                 scalar1=r2pm[:, 0:1], scalar2=127.0,
                                 op0=ALU.mult, op1=ALU.mult))
        ve.wait_ge(cDV.h, cDV.n)
        cDV.inc(ve.tensor_tensor(out=qb2pm.ap(), in0=t32b[:, 0:CH],
                                 in1=t32a[:, 0:CH], op=ALU.mult))
        ve.wait_ge(cDV.h, cDV.n)
        cDV.inc(ve.tensor_scalar(out=qb2pm.ap(), in0=qb2pm.ap(),
                                 scalar1=MAGIC, scalar2=MAGIC,
                                 op0=ALU.add, op1=ALU.subtract))
        ve.wait_ge(cDV.h, cDV.n)
        cDV.inc(ve.tensor_tensor(out=gbdq2.ap(), in0=gbpm.ap(),
                                 in1=dq2pm.ap()[:, :, None].broadcast_to([P, CH, B]),
                                 op=ALU.mult))
        ve.wait_ge(cDV.h, cDV.n)
    on("vector", g2_c)
    bankfree["D"] = cDV.tok()
    v_q2c = cDV.tok()

    def q2_tr(te):
        W(te, v_q2c)
        bw(te, "D")
        cPE.inc(te.transpose(pbD[0:CH, 0:P], qb2pm.ap(), cb_id.ap()))
    on("tensor", q2_tr)
    p_q2t = cPE.tok()

    def q2_ev(ve):
        W(ve, p_q2t)
        cDV.inc(ve.tensor_copy(sc40[0:CH, :], pbD[0:CH, 0:P]))
    on("vector", q2_ev)
    bankfree["D"] = cDV.tok()
    v_q2rows = cDV.tok()

    def q2_bb(sy):
        W(sy, v_q2rows, f1_pe_all)
        dma(chBC, sy, qb2_dram, sc40[0:CH, :])
        sy.wait_ge(chBC.h, chBC.n)
        chBC.pending_epoch = False
        dma(chBC, sy, vQB2R, qb2_dram.rearrange("r c -> (r c)")[None, :])
    on("sync", q2_bb)
    tok_q2r = use(chBC)

    def q2_hl(ve):
        W(ve, tok_q2r)
        cDV.inc(ve.tensor_copy(qb2hi, vQB2R))
        ve.wait_ge(cDV.h, cDV.n)
        cDV.inc(ve.tensor_tensor(out=vQB2R, in0=vQB2R, in1=qb2hi,
                                 op=ALU.subtract))
        ve.wait_ge(cDV.h, cDV.n)
        cDV.inc(ve.tensor_copy(qb2lo, vQB2R))
    on("vector", q2_hl)
    v_hl = cDV.tok()

    def f_qw2l(seq):
        W(seq, v_ag2, f1_pe_all)
        for j in range(32):
            chW.inc(seq.dma_start_transpose(vQW2[:, j, :],
                                            ago_qw2[:, j * P:(j + 1) * P]))
    on("scalar", f_qw2l)
    tok_qw2 = use(chW)

    f2 = {"uin_free": None, "q2_free": [None, None], "x2_free": [None, None],
          "x3_free": [None, None]}
    for ti in range(16):
        tg, th = ti // 2, ti % 2
        b, jt = tg // 4, tg % 4
        t0 = jt * TT + th * 256
        q2 = vQ2a if ti % 2 == 0 else vQ2b
        x2b = vX2F[:, ti % 2]
        x3b = vX3[:, ti % 2]

        def f2_ul(seq, tg=tg, th=th, prev=f2["uin_free"], first=(ti == 0)):
            W(seq, f1_sp_all, prev)
            if first:
                W(seq, fst["qh_free"][0], fst["qh_free"][1])
                W(seq, cDV.tok())
            for g in range(4):
                dma(chUL, seq, vUIN[:, g * CH:(g + 1) * CH, :],
                    u_dram[tg, g][:, :, th * 256:(th + 1) * 256])
        on("scalar", f2_ul)
        tok_ui = use(chUL)

        def f2_x2l(seq, b=b, t0=t0, x2b=x2b, prev=f2["x2_free"][ti % 2],
                   first=(ti < 2)):
            W(seq, prev)
            if first:
                W(seq, f1_sp_all)
            dma(chUL, seq, x2b, x2_dram[b][:, :, t0:t0 + 256])
        on("scalar", f2_x2l)
        tok_x2 = use(chUL)

        def f2_rd(se):
            W(se, tok_ui, v_q2c)
            uf = vUIN.rearrange("p j t -> p (j t)")
            cAC.inc(se.activation(uf, uf, AF.Copy, bias=MAGIC,
                                  scale=r2pm[:, 0:1]))
            se.wait_ge(cAC.h, cAC.n)
            cAC.inc(se.activation(uf, uf, AF.Relu, bias=-MAGIC))
        on("scalar", f2_rd)
        a_f2r = cAC.tok()

        def f2_cl(ve, q2=q2, prev=f2["q2_free"][ti % 2], first=(ti < 2)):
            W(ve, a_f2r, prev)
            if first:
                W(ve, f1_pe_all, v_hl)
            uf = vUIN.rearrange("p j t -> p (j t)")
            cDV.inc(ve.tensor_scalar_min(q2.rearrange("p j t -> p (j t)"),
                                         uf, 255.0))
        on("vector", f2_cl)
        v_q2t = cDV.tok()
        f2["uin_free"] = v_q2t

        for dc in range(CH):
            bank = "E" if dc % 2 == 0 else "F"
            def f2_mm(te, dc=dc, q2=q2, bank=bank):
                W(te, v_q2t, tok_qw2, v_hl, tok_cc)
                bw(te, bank)
                dst = (pbE.ap() if bank == "E" else pbF.ap())[:, 0:256]
                for fj in range(32):
                    te.matmul(dst, vQW2[:, fj, dc * P:(dc + 1) * P],
                              q2[:, fj, :], start=(fj == 0), stop=False)
                te.matmul(dst, qb2hi[0:1, dc * P:(dc + 1) * P], cb_orb.ap(),
                          start=False, stop=False)
                mm = te.matmul(dst, qb2lo[0:1, dc * P:(dc + 1) * P], cb_orb.ap(),
                               start=False, stop=True)
                cPE.inc(mm)
            on("tensor", f2_mm)
            p_f2 = cPE.tok()

            def f2_ev(ve, dc=dc, b=b, x2b=x2b, x3b=x3b, bank=bank,
                      prev=f2["x3_free"][ti % 2]):
                W(ve, p_f2, tok_x2, v_q2c)
                if dc == 0:
                    W(ve, prev)
                src = (pbE.ap() if bank == "E" else pbF.ap())[:, 0:256]
                cDV.inc(ve.scalar_tensor_tensor(
                    out=x3b[:, dc, :], in0=src,
                    scalar=gbdq2[:, dc, b:b + 1], in1=x2b[:, dc, :],
                    op0=ALU.mult, op1=ALU.add))
            on("vector", f2_ev)
            bankfree[bank] = cDV.tok()
        v_x3 = cDV.tok()
        f2["x2_free"][ti % 2] = v_x3
        f2["q2_free"][ti % 2] = cPE.tok()

        def f2_out(sy, b=b, t0=t0, x3b=x3b):
            W(sy, v_x3)
            dma(chOUT, sy, y_d[b].rearrange("j p t -> p j t")[:, :, t0:t0 + 256],
                x3b)
        on("sync", f2_out)
        f2["x3_free"][ti % 2] = use(chOUT)

    if DEBUG:
        dbg_outs = {}
        def dmk(name, shape, dtype=F32):
            t = nc.dram_tensor("dbg_" + name, list(shape), dtype,
                               kind="ExternalOutput").ap()
            dbg_outs[name] = t
            return t
        d_x2 = dmk("x2", [B, P, CH, TB])
        d_h2 = dmk("h2", [B, P, CH, TB])
        d_u0 = dmk("u0", [4, P, CH, TT])
        smalls = [("gapm", gapm.ap()), ("gbpm", gbpm.ap()), ("capm", capm.ap()),
                  ("sapm", sapm.ap()), ("cbpm", cbpm.ap()), ("sbpm", sbpm.ap()),
                  ("am1pm", am1pm.ap()), ("dq1pm", dq1pm.ap()),
                  ("qb1pm", qb1pm.ap()), ("qbdq1", qbdq1.ap()),
                  ("dq2pm", dq2pm.ap()), ("qb2pm", qb2pm.ap()),
                  ("gbdq2", gbdq2.ap()), ("r1pm", r1pm.ap()), ("r2pm", r2pm.ap()),
                  ("gm1", gm1.ap()), ("gm2", gm2.ap()), ("hmax", hmax.ap()),
                  ("umax", umax.ap()), ("attn1", attn_sb.ap())]
        for nm, ap in smalls:
            dmk(nm, ap.shape, ap.dtype)

        def dbg_dump(sy):
            sy.wait_ge(chOUT.h, chOUT.n)
            dma(chOUT, sy, d_x2, x2_dram)
            dma(chOUT, sy, d_h2, h2_dram)
            dma(chOUT, sy, d_u0, u_dram[0])
            for nm, ap in smalls:
                dma(chOUT, sy, dbg_outs[nm], ap)
        on("sync", dbg_dump)

    def fin(sy):
        sy.wait_ge(chOUT.h, chOUT.n)
    on("sync", fin)

    es.close()
    return nc


def _host_prepare(inputs):
    x = np.asarray(inputs["L0_attn_x"])
    lq = np.asarray(inputs["L0_attn_latent_queries"])
    Wk = np.asarray(inputs["L0_attn_W_k"])
    Wv = np.asarray(inputs["L0_attn_W_v"])
    Wg = np.asarray(inputs["L0_attn_W_g"])
    Wp = np.asarray(inputs["L0_attn_W_p"])
    w1 = np.asarray(inputs["L0_ffn_w1"])
    b1 = np.asarray(inputs["L0_ffn_b1"])
    w2 = np.asarray(inputs["L0_ffn_w2"])
    b2 = np.asarray(inputs["L0_ffn_b2"])
    c = np.asarray(inputs["L0_c"])
    wmod = np.asarray(inputs["L0_wmod"])

    g64 = np.concatenate([np.arange(0, 64, 2), np.arange(1, 64, 2)])
    gidx = (np.arange(16)[:, None] * 64 + g64[None, :]).reshape(-1)

    wkT = np.ascontiguousarray(Wk[gidx, :].T)
    wvT = np.ascontiguousarray(Wv.T)
    wgT = np.ascontiguousarray(Wg.T)
    wpT = np.ascontiguousarray(Wp.T)

    q = lq[0]
    qpk = np.empty((P, JP, 64), np.float32)
    for jp in range(JP):
        qpk[0:64, jp, :] = q[:, 2 * jp, :][:, g64].T
        qpk[64:128, jp, :] = q[:, 2 * jp + 1, :][:, g64].T

    inv = 1.0 / (10000.0 ** (np.arange(0, 64, 2, dtype=np.float64) / 64.0))
    tpos = (np.arange(NCK)[:, None] * 128 + np.arange(64)[None, :]).reshape(-1)
    pfreq = inv[np.arange(P) % 32]
    ang = tpos[None, :] * pfreq[:, None]
    cost = np.cos(ang).astype(np.float32)
    sint = np.sin(ang).astype(np.float32)
    sgn = np.where((np.arange(P) % 64) < 32, -1.0, 1.0).astype(np.float32)
    sint = sint * sgn[:, None]

    mneg = np.where((np.arange(P) % 64)[:, None] <= np.arange(64)[None, :],
                    0.0, NEGBIG).astype(np.float32)
    oneblk = np.zeros((P, 2), np.float32)
    oneblk[0:64, 0] = 1.0
    oneblk[64:128, 1] = 1.0
    blk2 = np.ascontiguousarray(oneblk.T)
    ident = np.eye(P, dtype=np.float32)
    onescol = np.ones((P, 1), np.float32)
    onesrow = np.ones((1, TT), np.float32)
    kconst = np.zeros((P, 4), np.float32)
    kconst[:, 0] = MAGIC
    kconst[:, 1] = LN_EPS
    kconst[:, 2] = QEPS
    kconst[:, 3] = -MAGIC

    b1t = np.ascontiguousarray(b1.reshape(32, P).T)
    b2t = np.ascontiguousarray(b2.reshape(CH, P).T)

    in_maps = []
    for cid in range(NCORES):
        xb = x[cid * B:(cid + 1) * B]
        x_lay = np.ascontiguousarray(xb.transpose(0, 2, 1).reshape(B, CH, P, TB))
        cb = c[cid * B:(cid + 1) * B]
        cT = np.ascontiguousarray(cb.T.reshape(CH, P, B).transpose(1, 0, 2))
        in_maps.append({
            "x": x_lay,
            "wkT": wkT, "wvT": wvT, "wgT": wgT, "wpT": wpT,
            "wmod": wmod, "cT": cT,
            "w1s": np.ascontiguousarray(w1[cid * 512:(cid + 1) * 512]),
            "w2s": np.ascontiguousarray(w2[cid * P:(cid + 1) * P]),
            "b1t": b1t, "b2t": b2t,
            "qpk": qpk, "cost": cost, "sint": sint, "mneg": mneg,
            "oneblk": oneblk, "blk2": blk2, "ident": ident,
            "onescol": onescol, "onesrow": onesrow, "kconst": kconst,
        })
    return in_maps


def kernel(**inputs):
    in_maps = _host_prepare(inputs)
    nc = build_program()
    res = run_bass_kernel_spmd(nc, in_maps, core_ids=list(range(NCORES)))
    outs = []
    for cid in range(NCORES):
        y = np.asarray(res.results[cid]["y"])
        outs.append(y.transpose(0, 3, 1, 2).reshape(B, TB, DM))
    return np.concatenate(outs, axis=0).astype(np.float32)


if __name__ == "__main__":
    nc = build_program()
    print("instructions:", nc.n_instructions())
    print("sbuf remaining:", nc.sbuf_bytes_remaining)


# version 28
# speedup vs baseline: 1.0810x; 1.0575x over previous
import numpy as np
import ml_dtypes
from contextlib import ExitStack

import concourse.bass as bass
import concourse.mybir as mybir
from concourse.bass_utils import run_bass_kernel_spmd

F32 = mybir.dt.float32
BF16 = mybir.dt.bfloat16
AF = mybir.ActivationFunctionType
ALU = mybir.AluOpType
AX = mybir.AxisListType

NCORES = 8
B = 2
TB = 2048
CH = 8
P = 128
DM = 1024
FF = 4096
TT = 512
NT = TB // TT
NCK = TB // 128
JP = 8
MAGIC = 12582912.0
NEGBIG = -1.0e9
LN_EPS = 1e-5
QEPS = 1e-8
RG = [list(range(NCORES))]

DEBUG = False


class Chan:
    def __init__(self, es, nc, name, step=1):
        self.h = es.enter_context(nc.semaphore(name))
        self.n = 0
        self.step = step
        self.pending_epoch = False

    def inc(self, instr):
        self.n += self.step
        instr.then_inc(self.h, self.step)
        return self.n

    def tok(self):
        return (self, self.n)


def build_program():
    nc = bass.Bass("TRN2", target_bir_lowering=False)
    es = ExitStack()
    S = es.enter_context

    def din(name, shape, dtype=F32):
        return nc.dram_tensor(name, shape, dtype, kind="ExternalInput").ap()

    def dint(name, shape, dtype=F32):
        return nc.dram_tensor(name, shape, dtype, kind="Internal").ap()

    x_d = din("x", [B, CH, P, TB])
    wkT_d = din("wkT", [DM, DM])
    wvT_d = din("wvT", [DM, DM])
    wgT_d = din("wgT", [DM, DM])
    wpT_d = din("wpT", [DM, DM])
    wmod_d = din("wmod", [DM, 6 * DM])
    cT_d = din("cT", [P, CH, B])
    w1s_d = din("w1s", [512, DM])
    w2s_d = din("w2s", [P, FF])
    b1t_d = din("b1t", [P, 32])
    b2t_d = din("b2t", [P, CH])
    qpk_d = din("qpk", [P, JP, 64])
    cost_d = din("cost", [P, NCK * 64])
    sint_d = din("sint", [P, NCK * 64])
    mneg_d = din("mneg", [P, 64])
    oneblk_d = din("oneblk", [P, 2])
    blk2_d = din("blk2", [2, P])
    ident_d = din("ident", [P, P])
    onescol_d = din("onescol", [P, 1])
    onesrow_d = din("onesrow", [1, TT])
    kconst_d = din("kconst", [P, 4])

    y_d = nc.dram_tensor("y", [B, CH, P, TB], F32, kind="ExternalOutput").ap()

    h2_dram = dint("h2sp", [B, P, CH, TB])
    x2_dram = dint("x2sp", [B, P, CH, TB])
    u_dram = dint("usp", [8, 4, P, CH, TT])
    rowb_dram = dint("rowb", [2, TT])
    colb_dram = dint("colb", [P])
    qb2_dram = dint("qb2b", [CH, P])
    agi_qw1 = dint("agi_qw1", [512, DM], BF16)
    ago_qw1 = dint("ago_qw1", [FF, DM], BF16)
    agi_qw2 = dint("agi_qw2", [P, FF], BF16)
    ago_qw2 = dint("ago_qw2", [DM, FF], BF16)
    agi_sc = dint("agi_sc", [5, P])
    ago_sc = dint("ago_sc", [40, P])
    ar1_in = dint("ar1_in", [1, 1])
    ar1_out = dint("ar1_out", [1, 1])
    ar2_in = dint("ar2_in", [1, 1])
    ar2_out = dint("ar2_out", [1, 1])

    BIGA = S(nc.sbuf_tensor("BIGA", [P, 20480], F32))
    SMA = S(nc.sbuf_tensor("SMA", [P, 12288], F32))
    STR = S(nc.sbuf_tensor("STR", [P, 8192], F32))
    SCR = S(nc.sbuf_tensor("SCR", [P, 4096], F32))

    BIGAb = BIGA.ap().bitcast(BF16)
    SMAb = SMA.ap().bitcast(BF16)
    STRf = STR.ap()
    SCRb = SCR.ap().bitcast(BF16)

    vK = BIGAb[:, 0:8192].rearrange("p (j t) -> p j t", j=CH)
    vV = BIGAb[:, 8192:24576].rearrange("p (n f) -> p n f", n=NCK)
    vG = BIGAb[:, 24576:40960].rearrange("p (j t) -> p j t", j=CH)
    vW = SMAb[:, 0:8192].rearrange("p (j f) -> p j f", j=CH)
    vW2 = BIGAb[:, 32768:40960].rearrange("p (j f) -> p j f", j=CH)
    vH = SMAb[:, 8192:24576].rearrange("p (s j t) -> p s j t", s=NT, j=CH)
    vXTd = [SMA.ap()[:, 4096:8192].rearrange("p (j t) -> p j t", j=CH),
            SMA.ap()[:, 8192:12288].rearrange("p (j t) -> p j t", j=CH)]
    vX2d = [STRf[:, 0:4096].rearrange("p (j t) -> p j t", j=CH),
            STRf[:, 4096:8192].rearrange("p (j t) -> p j t", j=CH)]
    vXBF0 = SCRb[:, 0:4096].rearrange("p (j t) -> p j t", j=CH)
    vXBF1 = STR.ap().bitcast(BF16)[:, 0:4096].rearrange("p (j t) -> p j t", j=CH)
    vSM = SCR.ap()[:, 0:1024]
    vESC = SCRb[:, 2048:3072]
    vPRB = SCRb[:, 3072:4096]
    vRDN = SCR.ap()[0:2, 2048:3072]
    vRDS = SCRb[0:2, 6144:7168]
    vWT = SCRb[:, 0:4096].rearrange("p (j t) -> p j t", j=CH)

    vMOD = BIGA.ap()[0:2, 0:6144]
    vWQF = BIGA.ap()[:, 8192:12288].rearrange("p (f c) -> p f c", f=4)
    vWQT = BIGA.ap()[:, 12288:13312]
    vWMB0 = BIGA.ap()[:, 13312:15360]
    vWMB1 = BIGA.ap()[:, 18432:20480]
    vWQQ = BIGAb[:, 32768:36864].rearrange("p (f c) -> p f c", f=4)
    vW2F = vWQF.rearrange("p f c -> p (f c)")
    vW2Q = vWQQ.rearrange("p f c -> p (f c)")

    vQW1 = BIGAb[:, 0:32768].rearrange("p (j f) -> p j f", j=CH)
    vQW2 = BIGAb[:, 0:32768].rearrange("p (j f) -> p j f", j=32)
    vQ2a = BIGAb[:, 32768:40960].rearrange("p (j t) -> p j t", j=32)
    vQ2b = SMAb[:, 16384:24576].rearrange("p (j t) -> p j t", j=32)
    vH2T = SMA.ap()[:, 0:8192].rearrange("p (s j t) -> p s j t", s=2, j=CH)
    vQH = SMAb[:, 16384:24576].rearrange("p (s j t) -> p s j t", s=2, j=CH)
    vUST = STRf.rearrange("p (s j t) -> p s j t", s=2, j=CH)
    vUIN = SMA.ap()[:, 0:8192].rearrange("p (j t) -> p j t", j=32)
    vX2F = STRf[:, 0:4096].rearrange("p (s j t) -> p s j t", s=2, j=CH)
    vX3 = STRf[:, 4096:8192].rearrange("p (s j t) -> p s j t", s=2, j=CH)
    vQB2R = SCR.ap()[0:1, 0:1024]

    kraw = S(nc.sbuf_tensor("kraw", [P, 512], BF16))
    kswp = S(nc.sbuf_tensor("kswp", [P, 512], BF16))
    kt1 = S(nc.sbuf_tensor("kt1", [P, 512], F32))
    kt2 = S(nc.sbuf_tensor("kt2", [P, 512], F32))
    apt = S(nc.sbuf_tensor("apt", [P, 2 * TT], F32))
    apt2 = S(nc.sbuf_tensor("apt2", [P, 2 * TT], F32))
    r_m = SCR.ap()[0:1, 2048:2560]
    r_var = SCR.ap()[0:1, 2560:3072]
    r_rstd = SCR.ap()[0:1, 3072:3584]
    r_negmr = SCR.ap()[0:1, 3584:4096]
    egt = S(nc.sbuf_tensor("egt", [P, TT], BF16))
    rdgf = S(nc.sbuf_tensor("rdgf", [2, TT], F32))
    rdg = S(nc.sbuf_tensor("rdg", [2, TT], BF16))
    attn_sb = S(nc.sbuf_tensor("attn_sb", [P, JP, 64], BF16))
    sapm = S(nc.sbuf_tensor("sapm", [P, CH, B], F32))
    capm = S(nc.sbuf_tensor("capm", [P, CH, B], F32))
    gapm = S(nc.sbuf_tensor("gapm", [P, CH, B], F32))
    sbpm = S(nc.sbuf_tensor("sbpm", [P, CH, B], F32))
    cbpm = S(nc.sbuf_tensor("cbpm", [P, CH, B], F32))
    gbpm = S(nc.sbuf_tensor("gbpm", [P, CH, B], F32))
    am1_l = S(nc.sbuf_tensor("am1_l", [P, 4], F32))
    rw1_l = S(nc.sbuf_tensor("rw1_l", [P, 4], F32))
    am2_l = S(nc.sbuf_tensor("am2_l", [P, 1], F32))
    rw2_l = S(nc.sbuf_tensor("rw2_l", [P, 1], F32))
    sc5 = S(nc.sbuf_tensor("sc5", [5, P], F32))
    sc40 = S(nc.sbuf_tensor("sc40", [40, P], F32))
    scT = S(nc.sbuf_tensor("scT", [P, 40], F32))
    am1pm = S(nc.sbuf_tensor("am1pm", [P, 32], F32))
    am2pm = S(nc.sbuf_tensor("am2pm", [P, CH], F32))
    dq1pm = S(nc.sbuf_tensor("dq1pm", [P, 32], F32))
    qb1pm = S(nc.sbuf_tensor("qb1pm", [P, 32], F32))
    qbdq1 = S(nc.sbuf_tensor("qbdq1", [P, 32], F32))
    dq2pm = S(nc.sbuf_tensor("dq2pm", [P, CH], F32))
    qb2pm = S(nc.sbuf_tensor("qb2pm", [P, CH], F32))
    gbdq2 = S(nc.sbuf_tensor("gbdq2", [P, CH, B], F32))
    qb2hi = SCRb[0:1, 2048:3072]
    qb2lo = SCRb[0:1, 3072:4096]
    t32a = S(nc.sbuf_tensor("t32a", [P, 32], F32))
    t32b = S(nc.sbuf_tensor("t32b", [P, 32], F32))
    hmax = S(nc.sbuf_tensor("hmax", [P, 1], F32))
    umax = S(nc.sbuf_tensor("umax", [P, 1], F32))
    tmax = S(nc.sbuf_tensor("tmax", [P, 1], F32))
    row128 = S(nc.sbuf_tensor("row128", [1, P], F32))
    gm1 = S(nc.sbuf_tensor("gm1", [1, 1], F32))
    gm2 = S(nc.sbuf_tensor("gm2", [1, 1], F32))
    r1row = S(nc.sbuf_tensor("r1row", [1, 2], F32))
    r2row = S(nc.sbuf_tensor("r2row", [1, 2], F32))
    r1pm = S(nc.sbuf_tensor("r1pm", [P, 2], F32))
    r2pm = S(nc.sbuf_tensor("r2pm", [P, 2], F32))
    cb_qpk = S(nc.sbuf_tensor("cb_qpk", [P, JP, 64], BF16))
    cb_cos = S(nc.sbuf_tensor("cb_cos", [P, NCK * 64], BF16))
    cb_sin = S(nc.sbuf_tensor("cb_sin", [P, NCK * 64], BF16))
    cb_mneg = S(nc.sbuf_tensor("cb_mneg", [P, 64], BF16))
    cb_ob = S(nc.sbuf_tensor("cb_ob", [P, 2], BF16))
    cb_b2 = S(nc.sbuf_tensor("cb_b2", [2, P], BF16))
    cb_b2f = S(nc.sbuf_tensor("cb_b2f", [2, P], F32))
    cb_id = S(nc.sbuf_tensor("cb_id", [P, P], F32))
    cb_o1b = S(nc.sbuf_tensor("cb_o1b", [P, 1], BF16))
    cb_o1cf = S(nc.sbuf_tensor("cb_o1cf", [P, 1], F32))
    cb_o1f = S(nc.sbuf_tensor("cb_o1f", [1, P], F32))
    cb_orb = S(nc.sbuf_tensor("cb_orb", [1, 256], BF16))
    cb_kc = S(nc.sbuf_tensor("cb_kc", [P, 4], F32))
    cb_idb = S(nc.sbuf_tensor("cb_idb", [P, P], BF16))
    vRDT = S(nc.sbuf_tensor("vRDT", [P, 16], BF16))
    vRDT2 = S(nc.sbuf_tensor("vRDT2", [P, 8], BF16))
    vRDNb = SCRb[0:2, 4096:5120]
    cb_cT = S(nc.sbuf_tensor("cb_cT", [P, CH, B], F32))
    silc = S(nc.sbuf_tensor("silc", [P, CH * B], F32))
    cb_b1t = S(nc.sbuf_tensor("cb_b1t", [P, 32], F32))
    cb_b2t = S(nc.sbuf_tensor("cb_b2t", [P, CH], F32))

    pbA = S(nc.psum_tensor("pbA", [P, 1024], F32))
    pbB = S(nc.psum_tensor("pbB", [P, 1024], F32))
    pbC = S(nc.psum_tensor("pbC", [P, 512], F32))
    pbD = S(nc.psum_tensor("pbD", [P, 512], F32))
    pbE = S(nc.psum_tensor("pbE", [P, 512], F32))
    pbF = S(nc.psum_tensor("pbF", [P, 512], F32))

    chX = Chan(es, nc, "chX", 16)
    chSP = Chan(es, nc, "chSP", 16)
    chOUT = Chan(es, nc, "chOUT", 16)
    chBC = Chan(es, nc, "chBC", 16)
    chW = Chan(es, nc, "chW", 16)
    chUL = Chan(es, nc, "chUL", 16)
    chGC = Chan(es, nc, "chGC", 16)
    chVD = Chan(es, nc, "chVD", 16)
    chCC = Chan(es, nc, "chCC", 16)
    cCC = Chan(es, nc, "cCC", 1)
    cPE = Chan(es, nc, "cPE", 1)
    cDV = Chan(es, nc, "cDV", 1)
    cAC = Chan(es, nc, "cAC", 1)
    cGP = Chan(es, nc, "cGP", 1)

    nc.const_aps.register(F32, -MAGIC, cb_kc[:, 3:4])
    block = S(nc.Block())

    def on(eng, f):
        getattr(block, eng)(f)

    def W(e, *tokens):
        for t in tokens:
            if t is None:
                continue
            c, v = t
            if v > 0:
                e.wait_ge(c.h, v)

    def dma(ch, q, out, in_):
        if ch.pending_epoch:
            q.wait_ge(ch.h, ch.n)
            ch.pending_epoch = False
        return ch.inc(q.dma_start(out=out, in_=in_))

    def use(ch):
        ch.pending_epoch = True
        return (ch, ch.n)

    bankfree = {}

    def bw(e, *names):
        for nm in names:
            t = bankfree.get(nm)
            if t is not None:
                W(e, t)

    def a_loads(sy):
        for sb_t, d in [(cb_cos, None), (cb_sin, None)]:
            pass
        for sb_t, d in [(cb_id, ident_d), (cb_b2f, blk2_d),
                        (cb_o1f, onesrow_d[:, 0:P]), (cb_o1cf, onescol_d), (cb_kc, kconst_d),
                        (cb_cT, cT_d), (cb_b1t, b1t_d), (cb_b2t, b2t_d)]:
            dma(chX, sy, sb_t.ap(), d)
    on("sync", a_loads)
    tok_cst = use(chX)

    def a_casts(ge):
        for sb_t, d in [(cb_qpk, qpk_d), (cb_mneg, mneg_d), (cb_ob, oneblk_d), (cb_b2, blk2_d),
                        (cb_idb, ident_d),
                        (cb_o1b, onescol_d), (cb_orb, onesrow_d[:, 0:256]),
                        (cb_cos, cost_d), (cb_sin, sint_d)]:
            dma(chGC, ge, sb_t.ap(), d)
    on("gpsimd", ge_f := a_casts)
    tok_cc = use(chGC)

    def a_w1s(seq):
        dma(chW, seq, vWQF, w1s_d.rearrange("(ft p) c -> p ft c", p=P))
    on("scalar", a_w1s)
    tok_w1s = use(chW)

    def b_am1(ve):
        W(ve, tok_w1s)
        cDV.inc(ve.tensor_reduce(out=am1_l.ap(), in_=vWQF, axis=AX.X,
                                 op=ALU.max, apply_absolute_value=True))
        ve.wait_ge(cDV.h, cDV.n)
        cDV.inc(ve.tensor_scalar_max(am1_l.ap(), am1_l.ap(), QEPS))
        ve.wait_ge(cDV.h, cDV.n)
        cDV.inc(ve.reciprocal(rw1_l.ap(), am1_l.ap()))
        ve.wait_ge(cDV.h, cDV.n)
        cDV.inc(ve.tensor_scalar_mul(rw1_l.ap(), rw1_l.ap(), 127.0))
    on("vector", b_am1)
    v_am1 = cDV.tok()

    v_qts = None
    for ft in range(4):
        def b_qa(se, ft=ft, prev=v_qts):
            W(se, v_am1, tok_cst, prev)
            cAC.inc(se.activation(vWQT, vWQF[:, ft, :], AF.Copy,
                                  bias=MAGIC, scale=rw1_l[:, ft:ft + 1]))
        on("scalar", b_qa)
        a_q = cAC.tok()

        def b_qt(ve, ft=ft):
            W(ve, a_q)
            cDV.inc(ve.tensor_scalar(out=vWQT, in0=vWQT,
                                     scalar1=MAGIC, scalar2=-128.0,
                                     op0=ALU.subtract, op1=ALU.max))
            ve.wait_ge(cDV.h, cDV.n)
            cDV.inc(ve.tensor_scalar_min(vWQQ[:, ft, :], vWQT, 127.0))
        on("vector", b_qt)
        v_qts = cDV.tok()
    v_qw1 = cDV.tok()

    def b_st1(ge):
        W(ge, v_qw1)
        dma(chCC, ge, agi_qw1.rearrange("(ft p) c -> p ft c", p=P), vWQQ)
    on("gpsimd", b_st1)
    tok_agi1 = use(chCC)

    def b_ag1(ge):
        W(ge, tok_agi1)
        cCC.inc(ge.collective_compute("AllGather", ALU.bypass, replica_groups=RG,
                                      ins=[agi_qw1], outs=[ago_qw1]))
    on("gpsimd", b_ag1)
    v_ag1 = cCC.tok()

    def b_w2l(seq):
        W(seq, v_qw1)
        dma(chW, seq, vW2F, w2s_d)
    on("scalar", b_w2l)
    tok_w2s = use(chW)

    def b_am2(ve):
        W(ve, tok_w2s)
        cDV.inc(ve.tensor_reduce(out=am2_l.ap(), in_=vW2F, axis=AX.X,
                                 op=ALU.max, apply_absolute_value=True))
        ve.wait_ge(cDV.h, cDV.n)
        cDV.inc(ve.tensor_scalar_max(am2_l.ap(), am2_l.ap(), QEPS))
        ve.wait_ge(cDV.h, cDV.n)
        cDV.inc(ve.reciprocal(rw2_l.ap(), am2_l.ap()))
        ve.wait_ge(cDV.h, cDV.n)
        cDV.inc(ve.tensor_scalar_mul(rw2_l.ap(), rw2_l.ap(), 127.0))
    on("vector", b_am2)
    v_am2 = cDV.tok()

    def b_qa2(se):
        W(se, v_am2, tok_agi1)
        cAC.inc(se.activation(vW2F, vW2F, AF.Copy, bias=MAGIC,
                              scale=rw2_l[:, 0:1]))
    on("scalar", b_qa2)
    a_q2 = cAC.tok()

    def b_qt2(ve):
        W(ve, a_q2)
        cDV.inc(ve.tensor_scalar(out=vW2F, in0=vW2F, scalar1=MAGIC,
                                 scalar2=-128.0, op0=ALU.subtract, op1=ALU.max))
        ve.wait_ge(cDV.h, cDV.n)
        cDV.inc(ve.tensor_scalar_min(vW2Q, vW2F, 127.0))
    on("vector", b_qt2)
    v_qw2 = cDV.tok()

    def b_st2(ge):
        W(ge, v_qw2)
        dma(chCC, ge, agi_qw2, vW2Q)
    on("gpsimd", b_st2)
    tok_agi2 = use(chCC)

    def b_ag2(ge):
        W(ge, tok_agi2)
        cCC.inc(ge.collective_compute("AllGather", ALU.bypass, replica_groups=RG,
                                      ins=[agi_qw2], outs=[ago_qw2]))
    on("gpsimd", b_ag2)
    v_ag2 = cCC.tok()

    def b_tr1(te):
        W(te, v_am1, v_am2, tok_cst)
        bw(te, "D")
        cPE.inc(te.transpose(pbD[0:4, 0:P], am1_l.ap(), cb_id.ap()))
    on("tensor", b_tr1)
    p_t1 = cPE.tok()

    def b_te1(ve):
        W(ve, p_t1)
        cDV.inc(ve.tensor_copy(sc5[0:4, :], pbD[0:4, 0:P]))
    on("vector", b_te1)
    bankfree["D"] = cDV.tok()

    def b_tr2(te):
        bw(te, "D")
        cPE.inc(te.transpose(pbD[0:1, 0:P], am2_l.ap(), cb_id.ap()))
    on("tensor", b_tr2)
    p_t2 = cPE.tok()

    def b_te2(ve):
        W(ve, p_t2)
        cDV.inc(ve.tensor_copy(row128.ap(), pbD[0:1, 0:P]))
    on("vector", b_te2)
    bankfree["D"] = cDV.tok()
    v_sc5 = cDV.tok()

    def b_te2b(ge):
        W(ge, v_sc5)
        dma(chVD, ge, sc5[4:5, :], row128.ap())
    on("gpsimd", b_te2b)
    tok_sc5r = use(chVD)

    def b_st3(ge):
        W(ge, v_sc5, tok_sc5r)
        dma(chCC, ge, agi_sc, sc5.ap())
    on("gpsimd", b_st3)
    tok_agisc = use(chCC)

    def b_ag3(ge):
        W(ge, tok_agisc)
        cCC.inc(ge.collective_compute("AllGather", ALU.bypass, replica_groups=RG,
                                      ins=[agi_sc], outs=[ago_sc]))
    on("gpsimd", b_ag3)
    v_ag3 = cCC.tok()

    def m_silu(se):
        W(se, tok_cst)
        cAC.inc(se.activation(silc.ap(), cb_cT.ap().rearrange("p j b -> p (j b)"),
                              AF.Silu))
    on("scalar", m_silu)
    a_silu = cAC.tok()

    def m_zero(ve):
        W(ve, tok_agi2)
        cDV.inc(ve.memset(vMOD, 0.0))
    on("vector", m_zero)
    v_mz = cDV.tok()

    wm_prev_pe = [None, None]
    for j in range(CH):
        for q in range(3):
            wb = vWMB0 if (j * 3 + q) % 2 == 0 else vWMB1
            def m_wl(ge, j=j, q=q, prev=wm_prev_pe[(j * 3 + q) % 2]):
                W(ge, prev)
                dma(chGC, ge, wb,
                    wmod_d[j * P:(j + 1) * P, q * 2048:(q + 1) * 2048])
            on("gpsimd", m_wl)
            tok_wm = use(chGC)

            for fi in range(4):
                ftile = q * 4 + fi
                def m_mm(te, j=j, fi=fi, ftile=ftile, wb=wb):
                    W(te, tok_wm, a_silu)
                    if j == 0 and ftile == 0:
                        W(te, v_mz)
                    bw(te, "C")
                    cPE.inc(te.matmul(pbC[0:2, :], silc[:, 2 * j:2 * j + 2],
                                      wb[:, fi * 512:(fi + 1) * 512],
                                      start=True, stop=True))
                on("tensor", m_mm)
                p_mm = cPE.tok()

                def m_acc(ve, ftile=ftile):
                    W(ve, p_mm)
                    sl = vMOD[:, ftile * 512:(ftile + 1) * 512]
                    cDV.inc(ve.tensor_tensor(out=sl, in0=sl, in1=pbC[0:2, :],
                                             op=ALU.add))
                on("vector", m_acc)
                bankfree["C"] = cDV.tok()
            wm_prev_pe[(j * 3 + q) % 2] = cPE.tok()
    v_mod = cDV.tok()
    p_modmm = cPE.tok()

    for v6, dest in [(0, sapm), (1, capm), (2, gapm), (3, sbpm), (4, cbpm),
                     (5, gbpm)]:
        for j in range(CH):
            def m_tr(te, v6=v6, j=j):
                W(te, v_mod)
                bw(te, "D")
                cPE.inc(te.transpose(
                    pbD[0:P, 0:2],
                    vMOD[:, v6 * DM + j * P: v6 * DM + (j + 1) * P],
                    cb_id[0:2, 0:2]))
            on("tensor", m_tr)
            p_t = cPE.tok()

            def m_ev(ve, dest=dest, j=j):
                W(ve, p_t)
                cDV.inc(ve.tensor_copy(dest[:, j, :], pbD[0:P, 0:2]))
            on("vector", m_ev)
            bankfree["D"] = cDV.tok()
    p_modtr = cPE.tok()

    def m_caps(ve):
        cDV.inc(ve.tensor_scalar_add(capm.ap(), capm.ap(), 1.0))
        cDV.inc(ve.tensor_scalar_add(cbpm.ap(), cbpm.ap(), 1.0))
    on("vector", m_caps)

    for g in (gapm, gbpm):
        gv = g.ap().rearrange("p j b -> p (j b)")
        v_pre = cDV.tok()

        def m_abs(se, gv=gv, v_pre=v_pre):
            W(se, v_pre)
            cAC.inc(se.activation(t32b[:, 0:CH * B], gv, AF.Abs))
        on("scalar", m_abs)
        a_abs = cAC.tok()

        def m_cl(ve, gv=gv, a_abs=a_abs):
            W(ve, a_abs)
            cDV.inc(ve.tensor_scalar(out=t32a[:, 0:CH * B], in0=gv,
                                     scalar1=0.0, scalar2=2.0,
                                     op0=ALU.is_ge, op1=ALU.mult))
            ve.wait_ge(cDV.h, cDV.n)
            cDV.inc(ve.tensor_scalar_sub(t32a[:, 0:CH * B], t32a[:, 0:CH * B], 1.0))
            cDV.inc(ve.tensor_scalar_max(t32b[:, 0:CH * B], t32b[:, 0:CH * B], 0.25))
            ve.wait_ge(cDV.h, cDV.n)
            cDV.inc(ve.tensor_tensor(out=gv, in0=t32a[:, 0:CH * B],
                                     in1=t32b[:, 0:CH * B], op=ALU.mult))
            ve.wait_ge(cDV.h, cDV.n)
        on("vector", m_cl)
    v_mpm = cDV.tok()

    early_pe = [p_modtr]
    early_cc = [use(chCC)]

    def c_init(ve):
        cDV.inc(ve.memset(hmax.ap(), -3.0e38))
        cDV.inc(ve.memset(umax.ap(), 0.0))
        cDV.inc(ve.memset(attn_sb.ap(), 0.0))
    on("vector", c_init)

    st = {
        "xbf_free": None,
        "h_free": [None] * NT,
        "k_free": None, "v_free": None, "g_free": None,
        "w_free": None,
        "xt_free": None, "x2_free": None,
        "xt_free2": [None, None], "x2_free2": [None, None],
        "w2_free": None,
        "c5_done": None,
        "wt_free": None,
        "scr_mid_free": None,
        "egt_free": None,
        "attn_free": None,
        "x2sp": [None, None],
        "h2rd": None,
    }
    first_k_evac = [True]
    first_v_evac = [True]
    first_g_evac = [True]

    for b in range(B):
        def c_wk(ge, b=b, wfree=st["w_free"]):
            W(ge, wfree)
            dma(chGC, ge, vW, wkT_d.rearrange("(j p) f -> p j f", p=P))
        on("gpsimd", c_wk)
        tok_wk = use(chGC)

        apply_toks = [None] * NT
        xbf_free = st.get("xbf_free2", [None, None])

        def emit_C1(jt, b=b):
            xb = vXBF0 if jt % 2 == 0 else vXBF1
            def c_xbf1(ge, jt=jt, xb=xb, prev=xbf_free[jt % 2]):
                W(ge, prev)
                if b == 1 and jt < 2:
                    W(ge, st["c5_done"])
                dma(chGC, ge, xb,
                    x_d[b, :, :, jt * TT:(jt + 1) * TT].rearrange("j p t -> p j t"))
            on("gpsimd", c_xbf1)
            tok_x1 = use(chGC)

            def c_sum1(te, first=(b == 0 and jt == 0), xb=xb):
                W(te, tok_x1, tok_cc)
                if first:
                    W(te, p_modmm)
                bw(te, "C")
                for j in range(CH):
                    mm = te.matmul(pbC[0:1, :], cb_o1b.ap(), xb[:, j, :],
                                   start=(j == 0), stop=(j == CH - 1))
                    if j == CH - 1:
                        cPE.inc(mm)
            on("tensor", c_sum1)
            p_s1 = cPE.tok()

            def c_sq(ve, xb=xb):
                W(ve, p_s1)
                xf = xb.rearrange("p j t -> p (j t)")
                cDV.inc(ve.tensor_mul(xf, xf, xf))
            on("vector", c_sq)
            v_sq = cDV.tok()

            def c_sum2(te, xb=xb):
                W(te, v_sq)
                bw(te, "D")
                for j in range(CH):
                    mm = te.matmul(pbD[0:1, :], cb_o1b.ap(), xb[:, j, :],
                                   start=(j == 0), stop=(j == CH - 1))
                    if j == CH - 1:
                        cPE.inc(mm)
            on("tensor", c_sum2)
            p_s2 = cPE.tok()

            def c_xbf2(ge, b=b, jt=jt, xb=xb):
                W(ge, p_s2)
                dma(chGC, ge, xb,
                    x_d[b, :, :, jt * TT:(jt + 1) * TT].rearrange("j p t -> p j t"))
            on("gpsimd", c_xbf2)
            tok_x2b = use(chGC)

            def c_rows(ve):
                W(ve, p_s1, p_s2)
                cDV.inc(ve.tensor_scalar_mul(r_m, pbC[0:1, :], 1.0 / DM))
                ve.wait_ge(cDV.h, cDV.n)
                cDV.inc(ve.tensor_tensor(out=r_var, in0=r_m, in1=r_m,
                                         op=ALU.mult))
                ve.wait_ge(cDV.h, cDV.n)
                cDV.inc(ve.scalar_tensor_tensor(out=r_var, in0=pbD[0:1, :],
                                                scalar=1.0 / DM, in1=r_var,
                                                op0=ALU.mult, op1=ALU.subtract))
            on("vector", c_rows)
            v_var = cDV.tok()

            def c_rowa(se):
                W(se, v_var)
                cAC.inc(se.activation(r_var, r_var, AF.Ln, bias=cb_kc[0:1, 1:2]))
                se.wait_ge(cAC.h, cAC.n)
                cAC.inc(se.activation(r_rstd, r_var, AF.Exp, scale=-0.5))
            on("scalar", c_rowa)
            a_rstd = cAC.tok()

            def c_row2(ve):
                W(ve, a_rstd)
                cDV.inc(ve.scalar_tensor_tensor(out=r_negmr, in0=r_m,
                                                scalar=-1.0, in1=r_rstd,
                                                op0=ALU.mult, op1=ALU.mult))
            on("vector", c_row2)
            v_rows = cDV.tok()

            def c_bcmm(te):
                W(te, v_rows)
                cPE.inc(te.matmul(pbC.ap(), cb_o1f.ap(), r_rstd,
                                  start=True, stop=True))
                cPE.inc(te.matmul(pbD.ap(), cb_o1f.ap(), r_negmr,
                                  start=True, stop=True))
            on("tensor", c_bcmm)
            p_bc = cPE.tok()

            def c_apply(ve, b=b, jt=jt, xb=xb, hfree=st["h_free"][jt]):
                W(ve, p_bc, tok_x2b, v_mpm, hfree)
                for j in range(CH):
                    ap1 = apt[:, 0:512]
                    cDV.inc(ve.tensor_tensor(out=ap1, in0=xb[:, j, :],
                                             in1=pbC.ap(), op=ALU.mult))
                    ve.wait_ge(cDV.h, cDV.n)
                    cDV.inc(ve.tensor_tensor(out=ap1, in0=ap1,
                                             in1=pbD.ap(), op=ALU.add))
                    ve.wait_ge(cDV.h, cDV.n)
                    cDV.inc(ve.tensor_scalar(out=vH[:, jt, j, :], in0=ap1,
                                             scalar1=capm[:, j, b:b + 1],
                                             scalar2=sapm[:, j, b:b + 1],
                                             op0=ALU.mult, op1=ALU.add))
                    ve.wait_ge(cDV.h, cDV.n)
            on("vector", c_apply)
            apply_toks[jt] = cDV.tok()
            bankfree["C"] = cDV.tok()
            bankfree["D"] = cDV.tok()
            xbf_free[jt % 2] = cDV.tok()

        kslots = [pbE.ap(), pbF.ap()]
        kkeys = ["E", "F"]

        def emit_K(jt, b=b):
            for jp2 in range(4):
                si = (jt * 4 + jp2) % 2
                dst = kslots[si]
                def c_kmm(te, jt=jt, jp2=jp2, si=si, dst=dst):
                    W(te, apply_toks[jt], tok_wk)
                    bw(te, kkeys[si])
                    rhs = vH[:, jt].rearrange("p j (n c) -> p j n c", c=P)[:, :, :, 0:64]
                    for half in range(2):
                        jo = jp2 * 2 + half
                        for jk in range(CH):
                            mm = te.matmul(dst[:, half * 256:(half + 1) * 256],
                                           vW[:, jk, jo * P:(jo + 1) * P],
                                           rhs[:, jk], start=(jk == 0),
                                           stop=(jk == CH - 1))
                            if jk == CH - 1 and half == 1:
                                cPE.inc(mm)
                on("tensor", c_kmm)
                p_k = cPE.tok()

                def c_kev(se, dst=dst, kfree=st["k_free"]):
                    W(se, p_k, kfree)
                    cAC.inc(se.copy(kraw.ap(), dst))
                on("scalar", c_kev)
                a_kraw = cAC.tok()
                bankfree[kkeys[si]] = a_kraw

                def c_ksw(ge):
                    W(ge, a_kraw)
                    for (d0, s0) in [(0, 32), (32, 0), (64, 96), (96, 64)]:
                        dma(chVD, ge, kswp[d0:d0 + 32, :], kraw[s0:s0 + 32, :])
                on("gpsimd", c_ksw)
                tok_ksw = use(chVD)

                def c_krope(ve, jt=jt, jp2=jp2, first=first_k_evac[0]):
                    W(ve, tok_ksw, tok_cc)
                    if first:
                        W(ve, p_modmm, tok_agi2)
                        W(ve, early_cc[0])
                    csl = slice(jt * 256, (jt + 1) * 256)
                    cosw = cb_cos[:, None, csl].broadcast_to([P, 2, 256])
                    sinw = cb_sin[:, None, csl].broadcast_to([P, 2, 256])
                    cDV.inc(ve.tensor_tensor(out=kt1.ap().rearrange("p (a t) -> p a t", a=2),
                                             in0=kraw.ap().rearrange("p (a t) -> p a t", a=2),
                                             in1=cosw, op=ALU.mult))
                    cDV.inc(ve.tensor_tensor(out=kt2.ap().rearrange("p (a t) -> p a t", a=2),
                                             in0=kswp.ap().rearrange("p (a t) -> p a t", a=2),
                                             in1=sinw, op=ALU.mult))
                    ve.wait_ge(cDV.h, cDV.n)
                    cDV.inc(ve.tensor_tensor(
                        out=vK[:, jp2 * 2:jp2 * 2 + 2, csl],
                        in0=kt1.ap().rearrange("p (a t) -> p a t", a=2),
                        in1=kt2.ap().rearrange("p (a t) -> p a t", a=2),
                        op=ALU.add))
                on("vector", c_krope)
                first_k_evac[0] = False
                st["k_free"] = cDV.tok()

        def c_wv(ge, b=b, wfree=st["w2_free"]):
            W(ge, wfree)
            if b == 0:
                W(ge, tok_agi2, p_modmm)
            dma(chGC, ge, vW2, wvT_d.rearrange("(j p) f -> p j f", p=P))
        on("gpsimd", c_wv)
        tok_wv = use(chGC)

        def emit_V(jt, b=b):
            for sub in range(4):
                nchunk = jt * 4 + sub
                vbank = pbA.ap() if nchunk % 2 == 0 else pbB.ap()
                vkey = "A01" if nchunk % 2 == 0 else "B23"
                def c_vmm(te, jt=jt, sub=sub, vbank=vbank, vkey=vkey,
                          first=(jt == 0 and sub < 2)):
                    W(te, apply_toks[jt], tok_wv)
                    bw(te, vkey)
                    if first:
                        bw(te, "B2", "B3")
                    for jk in range(CH):
                        for fh in range(2):
                            mm = te.matmul(vbank[:, fh * 512:(fh + 1) * 512],
                                           vH[:, jt, jk, sub * P:(sub + 1) * P],
                                           vW2[:, jk, fh * 512:(fh + 1) * 512],
                                           start=(jk == 0), stop=(jk == CH - 1))
                            if jk == CH - 1 and fh == 1:
                                cPE.inc(mm)
                on("tensor", c_vmm)
                p_v = cPE.tok()

                def c_vev(se, nchunk=nchunk, vbank=vbank, vfree=st["v_free"],
                          first=first_v_evac[0]):
                    W(se, p_v, vfree)
                    if first:
                        W(se, p_modmm, tok_agi2)
                    cAC.inc(se.copy(vV[0:64, nchunk, :], vbank[0:64, :]))
                on("scalar", c_vev)
                first_v_evac[0] = False
                a_vev = cAC.tok()
                bankfree[vkey] = a_vev
                if nchunk % 2 == 0:
                    bankfree["A0"] = a_vev
                    bankfree["A1"] = a_vev
                else:
                    bankfree["B2"] = a_vev
                    bankfree["B3"] = a_vev

                def c_vdup(ge, nchunk=nchunk):
                    W(ge, a_vev)
                    dma(chVD, ge, vV[64:128, nchunk, :], vV[0:64, nchunk, :])
                on("gpsimd", c_vdup)

        for jt in range(NT):
            emit_C1(jt)
            if jt > 0:
                emit_K(jt - 1)
            if jt > 1:
                emit_V(jt - 2)
        emit_K(NT - 1)
        emit_V(NT - 2)
        emit_V(NT - 1)
        st["xbf_free2"] = xbf_free
        k_done = cDV.tok()
        v_done = use(chVD)
        v_done_pe = cPE.tok()

        def c_wg(ge, b=b, kpe=k_done):
            W(ge, kpe)
            dma(chGC, ge, vW, wgT_d.rearrange("(j p) f -> p j f", p=P))
        on("gpsimd", c_wg)
        tok_wg = use(chGC)

        for jt in range(NT):
            for jo in range(CH):
                gbank = pbF.ap() if jo % 2 == 0 else pbE.ap()
                gkey = "F" if jo % 2 == 0 else "E"
                def c_gmm(te, jt=jt, jo=jo, gbank=gbank, gkey=gkey):
                    W(te, apply_toks[jt], tok_wg)
                    bw(te, gkey)
                    for jk in range(CH):
                        mm = te.matmul(gbank, vW[:, jk, jo * P:(jo + 1) * P],
                                       vH[:, jt, jk, :], start=(jk == 0),
                                       stop=(jk == CH - 1))
                        if jk == CH - 1:
                            cPE.inc(mm)
                on("tensor", c_gmm)
                p_g = cPE.tok()

                def c_gex(se, gbank=gbank, egfree=st["egt_free"]):
                    W(se, p_g, egfree)
                    cAC.inc(se.activation(egt.ap(), gbank, AF.Exp, scale=0.125))
                on("scalar", c_gex)
                a_eg = cAC.tok()
                bankfree[gkey] = a_eg

                def c_gden(te):
                    W(te, a_eg, tok_cc)
                    bw(te, "B3")
                    for c in range(4):
                        mm = te.matmul(pbB[:, 512 + 2 * c:514 + 2 * c],
                                       egt.ap()[:, c * 128:(c + 1) * 128],
                                       cb_ob.ap(), start=True, stop=True)
                    cPE.inc(mm)
                on("tensor", c_gden)
                p_gd = cPE.tok()

                def c_grd(ve):
                    W(ve, p_gd)
                    with nc.allow_low_precision(reason="bf16 softmax recip"):
                        cDV.inc(ve.reciprocal(vRDT2.ap(), pbB[:, 512:520]))
                on("vector", c_grd)
                v_rdg = cDV.tok()

                def c_grt(te):
                    W(te, v_rdg)
                    for c in range(4):
                        mm = te.matmul(pbB[0:2, 512 + c * 128:640 + c * 128],
                                       vRDT2[:, 2 * c:2 * c + 2],
                                       cb_idb.ap(), start=True, stop=True)
                    cPE.inc(mm)
                on("tensor", c_grt)
                p_grt = cPE.tok()

                def c_gre(ve):
                    W(ve, p_grt)
                    with nc.allow_low_precision(reason="bf16 softmax recip"):
                        cDV.inc(ve.tensor_scalar(out=rdg.ap(),
                                                 in0=pbB[0:2, 512:1024],
                                                 scalar1=1.0, scalar2=0.0,
                                                 op0=ALU.mult, op1=ALU.add))
                on("vector", c_gre)
                v_gre = cDV.tok()
                bankfree["B3"] = v_gre

                def c_grdb(te):
                    W(te, v_gre)
                    bw(te, "B2")
                    cPE.inc(te.matmul(pbB[:, 0:512], cb_b2.ap(), rdg.ap(),
                                      start=True, stop=True))
                on("tensor", c_grdb)
                p_grdb = cPE.tok()

                def c_gn(ve, jt=jt, jo=jo, first=first_g_evac[0]):
                    W(ve, p_grdb)
                    if first:
                        W(ve, p_modmm, tok_agi2)
                    if jo >= 4 and jt == 0:
                        W(ve, v_done_pe)
                    cDV.inc(ve.tensor_tensor(out=vG[:, jo, jt * TT:(jt + 1) * TT],
                                             in0=egt.ap(), in1=pbB[:, 0:512],
                                             op=ALU.mult))
                on("vector", c_gn)
                first_g_evac[0] = False
                st["egt_free"] = cDV.tok()
                bankfree["B2"] = cDV.tok()
        g_done = cDV.tok()
        g_done_pe = cPE.tok()

        for jp in range(JP):
            def c_sc(te, jp=jp):
                W(te, k_done, tok_cc)
                if jp == 0:
                    W(te, st["scr_mid_free"])
                bw(te, "A01")
                for n in range(NCK):
                    osl = slice(n * 64, (n + 1) * 64)
                    te.matmul(pbA[0:64, osl], vK[0:64, jp, osl],
                              cb_qpk[0:64, jp, :], start=True, stop=True,
                              tile_position=(0, 0))
                    mm2 = te.matmul(pbA[64:128, osl], vK[64:128, jp, osl],
                                    cb_qpk[64:128, jp, :], start=True, stop=True,
                                    tile_position=(64, 64))
                    if n == NCK - 1:
                        cPE.inc(mm2)
            on("tensor", c_sc)
            p_sc = cPE.tok()

            def c_smsk(ve, jp=jp):
                W(ve, p_sc, tok_cst)
                if jp == 0:
                    W(ve, st["scr_mid_free"], xbf_free[0])
                cDV.inc(ve.tensor_tensor(
                    out=vSM.rearrange("p (n l) -> p n l", n=NCK),
                    in0=pbA.ap().rearrange("p (n l) -> p n l", n=NCK),
                    in1=cb_mneg[:, None, :].broadcast_to([P, NCK, 64]),
                    op=ALU.add))
            on("vector", c_smsk)
            v_sm = cDV.tok()
            bankfree["A01"] = v_sm

            def c_sexp(se):
                W(se, v_sm)
                cAC.inc(se.activation(vESC, vSM, AF.Exp, scale=0.125))
            on("scalar", c_sexp)
            a_esc = cAC.tok()

            def c_sden(te):
                W(te, a_esc)
                bw(te, "C", "D")
                for c in range(8):
                    mm = te.matmul(pbC[:, 2 * c:2 * c + 2],
                                   vESC[:, c * 128:(c + 1) * 128],
                                   cb_ob.ap(), start=True, stop=True)
                cPE.inc(mm)
            on("tensor", c_sden)
            p_sd = cPE.tok()

            def c_srd(ve):
                W(ve, p_sd)
                with nc.allow_low_precision(reason="bf16 softmax recip"):
                    cDV.inc(ve.reciprocal(vRDT.ap(), pbC[:, 0:16]))
            on("vector", c_srd)
            v_srd = cDV.tok()

            def c_srt(te):
                W(te, v_srd)
                for c in range(8):
                    dst = (pbD[0:2, c * 128:(c + 1) * 128] if c < 4 else
                           pbC[0:2, (c - 4) * 128:(c - 3) * 128])
                    mm = te.matmul(dst, vRDT[:, 2 * c:2 * c + 2],
                                   cb_idb.ap(), start=True, stop=True)
                cPE.inc(mm)
            on("tensor", c_srt)
            p_srt = cPE.tok()

            def c_sre(ve):
                W(ve, p_srt)
                with nc.allow_low_precision(reason="bf16 softmax recip"):
                    cDV.inc(ve.tensor_scalar(out=vRDNb[:, 0:512],
                                             in0=pbD[0:2, :],
                                             scalar1=1.0, scalar2=0.0,
                                             op0=ALU.mult, op1=ALU.add))
                    cDV.inc(ve.tensor_scalar(out=vRDNb[:, 512:1024],
                                             in0=pbC[0:2, :], scalar1=1.0,
                                             scalar2=0.0, op0=ALU.mult,
                                             op1=ALU.add))
            on("vector", c_sre)
            v_sre = cDV.tok()
            bankfree["C"] = v_sre
            bankfree["D"] = v_sre

            def c_srdb(te):
                W(te, v_sre)
                bw(te, "B2", "B3")
                cPE.inc(te.matmul(pbB[:, 0:512], cb_b2.ap(), vRDNb[:, 0:512],
                                  start=True, stop=True))
                cPE.inc(te.matmul(pbB[:, 512:1024], cb_b2.ap(),
                                  vRDNb[:, 512:1024], start=True, stop=True))
            on("tensor", c_srdb)
            p_srdb = cPE.tok()

            def c_pn(ve):
                W(ve, p_srdb)
                cDV.inc(ve.tensor_tensor(out=vPRB[:, 0:512], in0=vESC[:, 0:512],
                                         in1=pbB[:, 0:512], op=ALU.mult))
                cDV.inc(ve.tensor_tensor(out=vPRB[:, 512:1024],
                                         in0=vESC[:, 512:1024],
                                         in1=pbB[:, 512:1024], op=ALU.mult))
            on("vector", c_pn)
            v_prb = cDV.tok()
            bankfree["B2"] = v_prb
            bankfree["B3"] = v_prb

            def c_at(te, jp=jp):
                W(te, v_prb, v_done)
                bw(te, "E")
                for n in range(NCK):
                    osl = slice(n * 64, (n + 1) * 64)
                    te.matmul(pbE[0:64, 0:64], vPRB[0:64, osl],
                              vV[0:64, n, jp * P:jp * P + 64],
                              start=(n == 0), stop=(n == NCK - 1),
                              tile_position=(0, 0))
                    mm2 = te.matmul(pbE[64:128, 0:64], vPRB[64:128, osl],
                                    vV[64:128, n, jp * P + 64:(jp + 1) * P],
                                    start=(n == 0), stop=(n == NCK - 1),
                                    tile_position=(64, 64))
                    if n == NCK - 1:
                        cPE.inc(mm2)
            on("tensor", c_at)
            p_at = cPE.tok()

            def c_atev(se, jp=jp, afree=st["attn_free"]):
                W(se, p_at)
                if jp == 0:
                    W(se, afree)
                cAC.inc(se.copy(attn_sb[:, jp, :], pbE[:, 0:64]))
            on("scalar", c_atev)
            bankfree["E"] = cAC.tok()
        at_done = cAC.tok()
        mid_pe = cPE.tok()

        def c_wp(ge, b=b):
            W(ge, g_done_pe)
            dma(chGC, ge, vW, wpT_d.rearrange("(j p) f -> p j f", p=P))
        on("gpsimd", c_wp)
        tok_wp = use(chGC)

        if DEBUG and b == 1:
            dg_g = nc.dram_tensor("dbg_gates", [P, CH, TB], BF16,
                                  kind="ExternalOutput").ap()
            dg_w = nc.dram_tensor("dbg_wp", [P, CH, DM], BF16,
                                  kind="ExternalOutput").ap()
            def dbg_mid(sy):
                W(sy, g_done, tok_wp)
                dma(chOUT, sy, dg_g, vG)
                dma(chOUT, sy, dg_w, vW2)
            on("sync", dbg_mid)
            st["dbg_mid_tok"] = use(chOUT)

        resid_toks = [None] * NT
        x2sp_toks = [None] * NT

        def emit_A(jt, b=b):
            sl2 = jt % 2
            for jp in range(JP):
                def c_wt(te, jt=jt, jp=jp):
                    W(te, at_done, g_done)
                    if jp == 0 and jt == 0:
                        W(te, mid_pe)
                    bw(te, "F")
                    te.matmul(pbF[0:64, :], attn_sb[0:64, jp, :],
                              vG[0:64, jp, jt * TT:(jt + 1) * TT],
                              start=True, stop=True, tile_position=(0, 0))
                    mm2 = te.matmul(pbF[64:128, :], attn_sb[64:128, jp, :],
                                    vG[64:128, jp, jt * TT:(jt + 1) * TT],
                                    start=True, stop=True, tile_position=(64, 64))
                    cPE.inc(mm2)
                on("tensor", c_wt)
                p_wt = cPE.tok()

                def c_wtev(se, jp=jp, jt=jt, wtfree=st["wt_free"]):
                    W(se, p_wt)
                    if jp == 0:
                        W(se, wtfree)
                        if jt == 0:
                            W(se, mid_pe, apply_toks[NT - 1])
                    cAC.inc(se.copy(vWT[:, jp, :], pbF.ap()))
                on("scalar", c_wtev)
                bankfree["F"] = cAC.tok()
            wt_done = cAC.tok()

            def c_xt(sy, b=b, jt=jt, sl2=sl2, xtfree=st["xt_free2"][sl2]):
                W(sy, xtfree, g_done_pe)
                if jt < 2:
                    W(sy, apply_toks[NT - 1])
                dma(chX, sy, vXTd[sl2],
                    x_d[b, :, :, jt * TT:(jt + 1) * TT].rearrange("j p t -> p j t"))
            on("sync", c_xt)
            tok_xt = use(chX)

            for jo in range(CH):
                bank = "E" if jo % 2 == 0 else "C"
                def c_pmm(te, jt=jt, jo=jo, bank=bank):
                    W(te, wt_done, tok_wp)
                    bw(te, bank)
                    dst = pbE.ap() if bank == "E" else pbC.ap()
                    for jk in range(CH):
                        mm = te.matmul(dst, vW[:, jk, jo * P:(jo + 1) * P],
                                       vWT[:, jk, :], start=(jk == 0),
                                       stop=(jk == CH - 1))
                        if jk == CH - 1:
                            cPE.inc(mm)
                on("tensor", c_pmm)
                p_p = cPE.tok()

                def c_pev(ve, b=b, jt=jt, jo=jo, bank=bank, sl2=sl2,
                          x2free=st["x2_free2"][sl2]):
                    W(ve, p_p, tok_xt)
                    if jo == 0:
                        W(ve, x2free)
                    src = pbE.ap() if bank == "E" else pbC.ap()
                    cDV.inc(ve.scalar_tensor_tensor(
                        out=vX2d[sl2][:, jo, :], in0=src,
                        scalar=gapm[:, jo, b:b + 1], in1=vXTd[sl2][:, jo, :],
                        op0=ALU.mult, op1=ALU.add))
                on("vector", c_pev)
                bankfree[bank] = cDV.tok()
            resid_toks[jt] = cDV.tok()
            st["wt_free"] = cPE.tok()

            def c_x2sp(sy, b=b, jt=jt, sl2=sl2):
                W(sy, resid_toks[jt])
                dma(chSP, sy, x2_dram[b][:, :, jt * TT:(jt + 1) * TT], vX2d[sl2])
            on("sync", c_x2sp)
            x2sp_toks[jt] = use(chSP)

        def emit_B(jt, b=b):
            sl2 = jt % 2
            x2v = vX2d[sl2]
            h2v_ = vXTd[sl2]
            def c2_sum1(te, jt=jt, x2v=x2v):
                W(te, resid_toks[jt])
                bw(te, "B2")
                for j in range(CH):
                    mm = te.matmul(pbB[0:1, 0:512], cb_o1cf.ap(), x2v[:, j, :],
                                   start=(j == 0), stop=(j == CH - 1))
                    if j == CH - 1:
                        cPE.inc(mm)
            on("tensor", c2_sum1)
            p2_s1 = cPE.tok()

            sq_toks = []
            mm_toks = []
            for grp in range(4):
                def c2_sqv(ve, grp=grp, x2v=x2v,
                           prev=(mm_toks[grp - 2] if grp >= 2 else None)):
                    W(ve, prev)
                    if grp < 2:
                        W(ve, resid_toks[jt])
                    buf = apt if grp % 2 == 0 else apt2
                    cDV.inc(ve.tensor_tensor(
                        out=buf.ap().rearrange("p (a t) -> p a t", a=2),
                        in0=x2v[:, grp * 2:grp * 2 + 2, :],
                        in1=x2v[:, grp * 2:grp * 2 + 2, :], op=ALU.mult))
                on("vector", c2_sqv)
                sq_toks.append(cDV.tok())

                def c2_sqm(te, grp=grp, tok=sq_toks[grp]):
                    W(te, tok)
                    if grp == 0:
                        bw(te, "B3")
                    buf = apt if grp % 2 == 0 else apt2
                    for h in range(2):
                        mm = te.matmul(pbB[0:1, 512:1024], cb_o1cf.ap(),
                                       buf[:, h * 512:(h + 1) * 512],
                                       start=(grp == 0 and h == 0),
                                       stop=(grp == 3 and h == 1))
                        if h == 1:
                            cPE.inc(mm)
                on("tensor", c2_sqm)
                mm_toks.append(cPE.tok())
            p2_s2 = cPE.tok()

            def c2_rows(ve):
                W(ve, p2_s1, p2_s2)
                cDV.inc(ve.tensor_scalar_mul(r_m, pbB[0:1, 0:512], 1.0 / DM))
                ve.wait_ge(cDV.h, cDV.n)
                cDV.inc(ve.tensor_tensor(out=r_var, in0=r_m, in1=r_m,
                                         op=ALU.mult))
                ve.wait_ge(cDV.h, cDV.n)
                cDV.inc(ve.scalar_tensor_tensor(out=r_var, in0=pbB[0:1, 512:1024],
                                                scalar=1.0 / DM, in1=r_var,
                                                op0=ALU.mult, op1=ALU.subtract))
            on("vector", c2_rows)
            v2_var = cDV.tok()

            def c2_rowa(se):
                W(se, v2_var)
                cAC.inc(se.activation(r_var, r_var, AF.Ln, bias=cb_kc[0:1, 1:2]))
                se.wait_ge(cAC.h, cAC.n)
                cAC.inc(se.activation(r_rstd, r_var, AF.Exp, scale=-0.5))
            on("scalar", c2_rowa)
            a2_rstd = cAC.tok()

            def c2_row2(ve):
                W(ve, a2_rstd)
                cDV.inc(ve.scalar_tensor_tensor(out=r_negmr, in0=r_m,
                                                scalar=-1.0, in1=r_rstd,
                                                op0=ALU.mult, op1=ALU.mult))
            on("vector", c2_row2)
            v2_rows = cDV.tok()

            def c2_bcmm(te):
                W(te, v2_rows)
                cPE.inc(te.matmul(pbB[:, 0:512], cb_o1f.ap(), r_rstd,
                                  start=True, stop=True))
                cPE.inc(te.matmul(pbB[:, 512:1024], cb_o1f.ap(), r_negmr,
                                  start=True, stop=True))
            on("tensor", c2_bcmm)
            p2_bc = cPE.tok()

            def c2_apply(ve, b=b, jt=jt, x2v=x2v, h2v_=h2v_):
                W(ve, p2_bc)
                for j in range(CH):
                    ap1 = apt[:, 0:512]
                    cDV.inc(ve.tensor_tensor(out=ap1, in0=x2v[:, j, :],
                                             in1=pbB[:, 0:512], op=ALU.mult))
                    ve.wait_ge(cDV.h, cDV.n)
                    cDV.inc(ve.tensor_tensor(out=ap1, in0=ap1,
                                             in1=pbB[:, 512:1024], op=ALU.add))
                    ve.wait_ge(cDV.h, cDV.n)
                    cDV.inc(ve.tensor_scalar(out=h2v_[:, j, :], in0=ap1,
                                             scalar1=cbpm[:, j, b:b + 1],
                                             scalar2=sbpm[:, j, b:b + 1],
                                             op0=ALU.mult, op1=ALU.add))
                    ve.wait_ge(cDV.h, cDV.n)
                cDV.inc(ve.tensor_reduce(out=tmax.ap(), in_=h2v_, axis=AX.XY,
                                         op=ALU.max))
                ve.wait_ge(cDV.h, cDV.n)
                cDV.inc(ve.tensor_tensor(out=hmax.ap(), in0=hmax.ap(),
                                         in1=tmax.ap(), op=ALU.max))
                ve.wait_ge(cDV.h, cDV.n)
            on("vector", c2_apply)
            v2_h2 = cDV.tok()
            bankfree["B2"] = v2_h2
            bankfree["B3"] = v2_h2
            st["x2_free2"][sl2] = v2_h2

            def c2_h2sp(sy, b=b, jt=jt, sl2=sl2):
                W(sy, v2_h2, x2sp_toks[jt])
                dma(chSP, sy, h2_dram[b][:, :, jt * TT:(jt + 1) * TT], vXTd[sl2])
            on("sync", c2_h2sp)
            tok_h2sp = use(chSP)
            st.setdefault("h2sp_tok", {})[(b, jt)] = tok_h2sp
            st["xt_free2"][sl2] = tok_h2sp
            st["scr_mid_free"] = cPE.tok()

        for jt in range(NT):
            emit_A(jt)
            if jt > 0:
                emit_B(jt - 1)
        emit_B(NT - 1)

        for jt in range(NT):
            st["h_free"][jt] = g_done_pe
        st["k_free"] = mid_pe
        st["v_free"] = mid_pe
        st["kv_free"] = mid_pe
        st["g_free"] = cPE.tok()
        st["w_free"] = cPE.tok()
        st["w2_free"] = cPE.tok()
        bankfree["E"] = cPE.tok()
        bankfree["F"] = cPE.tok()
        st["attn_free"] = cPE.tok()
        st["c5_done"] = cDV.tok()
        st["xt_free"] = st["xt_free2"][0]
        st["x2_free"] = st["x2_free2"][1]
        st["scr_free_b"] = st["wt_free"]
        st["str_free_b"] = (st["x2_free2"][0], x2sp_toks[2])

    h2_all_sp = use(chSP)

    def b_ld3(ge):
        W(ge, v_ag3)
        dma(chCC, ge, sc40.ap(), ago_sc)
    on("gpsimd", b_ld3)
    tok_sc40 = use(chCC)

    def b_tr3(te):
        W(te, tok_sc40)
        bw(te, "D")
        cPE.inc(te.transpose(pbD[0:P, 0:40], sc40.ap(), cb_id[0:40, 0:40]))
    on("tensor", b_tr3)
    p_t3 = cPE.tok()

    def b_te3(ve):
        W(ve, p_t3)
        cDV.inc(ve.tensor_copy(scT.ap(), pbD[0:P, 0:40]))
        ve.wait_ge(cDV.h, cDV.n)
        sv = scT.ap().rearrange("p (r k) -> p r k", r=NCORES)
        cDV.inc(ve.tensor_copy(am1pm.ap().rearrange("p (r k) -> p r k", r=NCORES),
                               sv[:, :, 0:4]))
        cDV.inc(ve.tensor_copy(am2pm.ap(), sv[:, :, 4]))
    on("vector", b_te3)
    bankfree["D"] = cDV.tok()
    v_ampm = cDV.tok()



    def g1_tr(te):
        W(te, cDV.tok())
        bw(te, "D")
        cPE.inc(te.transpose(pbD[0:1, 0:P], hmax.ap(), cb_id.ap()))
    on("tensor", g1_tr)
    p_g1t = cPE.tok()

    def g1_r(ve):
        W(ve, p_g1t)
        cDV.inc(ve.tensor_reduce(out=gm1.ap(), in_=pbD[0:1, 0:P], axis=AX.X,
                                 op=ALU.max))
        ve.wait_ge(cDV.h, cDV.n)
        cDV.inc(ve.tensor_scalar_max(gm1.ap(), gm1.ap(), QEPS))
    on("vector", g1_r)
    bankfree["D"] = cDV.tok()
    v_g1 = cDV.tok()

    def g1_ar(ge):
        W(ge, v_g1)
        dma(chCC, ge, ar1_in, gm1.ap())
        ge.wait_ge(chCC.h, chCC.n)
        chCC.pending_epoch = False
        cCC.inc(ge.collective_compute("AllReduce", ALU.max, replica_groups=RG,
                                      ins=[ar1_in], outs=[ar1_out]))
        ge.wait_ge(cCC.h, cCC.n)
        dma(chCC, ge, gm1.ap(), ar1_out)
    on("gpsimd", g1_ar)
    tok_gm1 = use(chCC)

    def g1_rr(ve):
        W(ve, tok_gm1)
        cDV.inc(ve.reciprocal(r1row[0:1, 0:1], gm1.ap()))
        ve.wait_ge(cDV.h, cDV.n)
        cDV.inc(ve.tensor_scalar_mul(r1row[0:1, 0:1], r1row[0:1, 0:1], 255.0))
        cDV.inc(ve.tensor_scalar_mul(r1row[0:1, 1:2], gm1.ap(), 1.0 / 255.0))
    on("vector", g1_rr)
    v_r1r = cDV.tok()

    def g1_mm(te):
        W(te, v_r1r)
        bw(te, "D")
        cPE.inc(te.matmul(pbD[:, 0:2], cb_o1f.ap(), r1row.ap(),
                          start=True, stop=True))
    on("tensor", g1_mm)
    p_r1 = cPE.tok()

    def g1_c(ve):
        W(ve, p_r1, v_ampm)
        cDV.inc(ve.tensor_copy(r1pm.ap(), pbD[:, 0:2]))
        ve.wait_ge(cDV.h, cDV.n)
        cDV.inc(ve.tensor_scalar(out=dq1pm.ap(), in0=am1pm.ap(),
                                 scalar1=r1pm[:, 1:2], scalar2=1.0 / 127.0,
                                 op0=ALU.mult, op1=ALU.mult))
        cDV.inc(ve.reciprocal(t32a.ap(), am1pm.ap()))
        ve.wait_ge(cDV.h, cDV.n)
        cDV.inc(ve.tensor_scalar(out=t32b.ap(), in0=cb_b1t.ap(),
                                 scalar1=r1pm[:, 0:1], scalar2=127.0,
                                 op0=ALU.mult, op1=ALU.mult))
        ve.wait_ge(cDV.h, cDV.n)
        cDV.inc(ve.tensor_tensor(out=qb1pm.ap(), in0=t32b.ap(), in1=t32a.ap(),
                                 op=ALU.mult))
        ve.wait_ge(cDV.h, cDV.n)
        cDV.inc(ve.tensor_scalar(out=qb1pm.ap(), in0=qb1pm.ap(),
                                 scalar1=MAGIC, scalar2=MAGIC,
                                 op0=ALU.add, op1=ALU.subtract))
        ve.wait_ge(cDV.h, cDV.n)
        cDV.inc(ve.tensor_tensor(out=qbdq1.ap(), in0=qb1pm.ap(),
                                 in1=dq1pm.ap(), op=ALU.mult))
        ve.wait_ge(cDV.h, cDV.n)
    on("vector", g1_c)
    bankfree["D"] = cDV.tok()
    v_q1c = cDV.tok()

    def f_qw1l(seq):
        W(seq, v_ag1)
        W(seq, st["kv_free"])
        if DEBUG:
            W(seq, st.get("dbg_mid_tok"))
        for j in range(6):
            chW.inc(seq.dma_start_transpose(vQW1[:, j, :],
                                            ago_qw1[:, j * P:(j + 1) * P]))
        W(seq, st["w_free"])
        for j in range(6, CH):
            chW.inc(seq.dma_start_transpose(vQW1[:, j, :],
                                            ago_qw1[:, j * P:(j + 1) * P]))
    on("scalar", f_qw1l)
    tok_qw1 = use(chW)

    fst = {"h2_free": [None, None], "qh_free": [None, None],
           "ust_free": [None, None]}
    prep = {}

    def emit_prep(tg):
        b, jt = tg // 4, tg % 4
        sl2 = tg % 2
        h2b = vH2T[:, sl2]

        def f1_h2l(seq, b=b, jt=jt, h2b=h2b, prev=fst["h2_free"][sl2],
                   first=(tg == 0), sp_tok=st["h2sp_tok"][(tg // 4, tg % 4)]):
            W(seq, sp_tok, prev)
            if first:
                W(seq, st["w_free"], st["xt_free2"][0], st["xt_free2"][1],
                  st["c5_done"])
            dma(chUL, seq, h2b, h2_dram[b][:, :, jt * TT:(jt + 1) * TT])
        on("scalar", f1_h2l)
        tok_h2 = use(chUL)

        def f1_rd(se, h2b=h2b):
            W(se, tok_h2, v_q1c)
            hb = h2b.rearrange("p j t -> p (j t)")
            cAC.inc(se.activation(hb, hb, AF.Copy, bias=MAGIC,
                                  scale=r1pm[:, 0:1]))
        on("scalar", f1_rd)
        prep[tg] = cAC.tok()

    emit_prep(0)
    for tg in range(8):
        b, jt = tg // 4, tg % 4
        sl2 = tg % 2
        h2b = vH2T[:, sl2]
        qh = vQH[:, sl2]
        a_rd = prep[tg]

        def f1_cl(ve, h2b=h2b, qh=qh, a_rd=a_rd, prev=fst["qh_free"][sl2],
                  first=(tg == 0)):
            W(ve, a_rd, prev)
            if first:
                W(ve, st["w_free"], st["xt_free2"][1])
            hb = h2b.rearrange("p j t -> p (j t)")
            cDV.inc(ve.tensor_scalar(out=hb, in0=hb, scalar1=MAGIC, scalar2=0.0,
                                     op0=ALU.subtract, op1=ALU.max))
            ve.wait_ge(cDV.h, cDV.n)
            cDV.inc(ve.tensor_scalar_min(qh.rearrange("p j t -> p (j t)"),
                                         hb, 255.0))
        on("vector", f1_cl)
        v_qh = cDV.tok()
        fst["h2_free"][sl2] = v_qh
        for _f in fst.get("pending_um", []):
            _f()
        fst["pending_um"] = []

        for g in range(4):
            ub = vUST[:, g % 2]
            for fi in range(CH):
                ft = g * CH + fi
                bank = "E" if ft % 2 == 0 else "F"
                def f1_mm(te, ft=ft, qh=qh, bank=bank, first=(tg == 0 and ft == 0)):
                    W(te, v_qh, tok_qw1)
                    bw(te, bank)
                    dst = pbE.ap() if bank == "E" else pbF.ap()
                    for jk in range(CH):
                        mm = te.matmul(dst, vQW1[:, jk, ft * P:(ft + 1) * P],
                                       qh[:, jk, :], start=(jk == 0),
                                       stop=(jk == CH - 1))
                        if jk == CH - 1:
                            cPE.inc(mm)
                on("tensor", f1_mm)
                p_f1 = cPE.tok()

                if ft % 4 != 3:
                    def f1_ev(se, ft=ft, fi=fi, ub=ub, bank=bank,
                              prev=fst["ust_free"][g % 2] if fi == 0 else None,
                              first=(tg == 0 and g < 2 and fi == 0)):
                        if prev is not None:
                            ptok, pum = prev
                            W(se, p_f1, v_q1c, ptok, pum[0])
                        else:
                            W(se, p_f1, v_q1c)
                        if first:
                            W(se, st["x2_free2"][0], st["x2_free2"][1],
                              st["c5_done"])
                        src = pbE.ap() if bank == "E" else pbF.ap()
                        cAC.inc(se.activation(ub[:, fi, :], src, AF.Relu,
                                              bias=qbdq1[:, ft:ft + 1],
                                              scale=dq1pm[:, ft:ft + 1]))
                    on("scalar", f1_ev)
                    bankfree[bank] = cAC.tok()
                else:
                    def f1_evv(ve, ft=ft, fi=fi, ub=ub, bank=bank):
                        W(ve, p_f1, v_q1c)
                        src = pbE.ap() if bank == "E" else pbF.ap()
                        cDV.inc(ve.tensor_scalar(out=apt[:, 0:512], in0=src,
                                                 scalar1=qb1pm[:, ft:ft + 1],
                                                 scalar2=0.0,
                                                 op0=ALU.add, op1=ALU.max))
                        ve.wait_ge(cDV.h, cDV.n)
                        cDV.inc(ve.tensor_scalar_mul(ub[:, fi, :], apt[:, 0:512],
                                                     dq1pm[:, ft:ft + 1]))
                    on("vector", f1_evv)
                    bankfree[bank] = cDV.tok()
            a_ug = (cAC.tok(), cDV.tok())

            um_cell = [None]

            def emit_um(a_ug=a_ug, ub=ub, um_cell=um_cell):
                def f1_um(ve):
                    W(ve, *a_ug)
                    cDV.inc(ve.tensor_reduce(out=tmax.ap(), in_=ub, axis=AX.XY,
                                             op=ALU.max))
                    ve.wait_ge(cDV.h, cDV.n)
                    cDV.inc(ve.tensor_tensor(out=umax.ap(), in0=umax.ap(),
                                             in1=tmax.ap(), op=ALU.max))
                    ve.wait_ge(cDV.h, cDV.n)
                on("vector", f1_um)
                um_cell[0] = cDV.tok()
            if g < 2:
                emit_um()
            else:
                fst.setdefault("pending_um", []).append(emit_um)

            def f1_sp(sy, tg=tg, g=g, ub=ub):
                W(sy, *a_ug)
                dma(chSP, sy, u_dram[tg, g], ub)
            on("sync", f1_sp)
            tok_us = use(chSP)
            fst["ust_free"][g % 2] = (tok_us, um_cell)
            if g == 1 and tg < 7:
                emit_prep(tg + 1)
        fst["qh_free"][sl2] = cPE.tok()
    for _f in fst.get("pending_um", []):
        _f()
    fst["pending_um"] = []
    f1_sp_all = use(chSP)
    f1_pe_all = cPE.tok()

    def g2_tr(te):
        W(te, cDV.tok())
        bw(te, "D")
        cPE.inc(te.transpose(pbD[0:1, 0:P], umax.ap(), cb_id.ap()))
    on("tensor", g2_tr)
    p_g2t = cPE.tok()

    def g2_r(ve):
        W(ve, p_g2t)
        cDV.inc(ve.tensor_reduce(out=gm2.ap(), in_=pbD[0:1, 0:P], axis=AX.X,
                                 op=ALU.max))
        ve.wait_ge(cDV.h, cDV.n)
        cDV.inc(ve.tensor_scalar_max(gm2.ap(), gm2.ap(), QEPS))
    on("vector", g2_r)
    bankfree["D"] = cDV.tok()
    v_g2 = cDV.tok()

    def g2_ar(ge):
        W(ge, v_g2)
        dma(chCC, ge, ar2_in, gm2.ap())
        ge.wait_ge(chCC.h, chCC.n)
        chCC.pending_epoch = False
        cCC.inc(ge.collective_compute("AllReduce", ALU.max, replica_groups=RG,
                                      ins=[ar2_in], outs=[ar2_out]))
        ge.wait_ge(cCC.h, cCC.n)
        dma(chCC, ge, gm2.ap(), ar2_out)
    on("gpsimd", g2_ar)
    tok_gm2 = use(chCC)

    def g2_rr(ve):
        W(ve, tok_gm2)
        cDV.inc(ve.reciprocal(r2row[0:1, 0:1], gm2.ap()))
        ve.wait_ge(cDV.h, cDV.n)
        cDV.inc(ve.tensor_scalar_mul(r2row[0:1, 0:1], r2row[0:1, 0:1], 255.0))
        cDV.inc(ve.tensor_scalar_mul(r2row[0:1, 1:2], gm2.ap(), 1.0 / 255.0))
    on("vector", g2_rr)
    v_r2r = cDV.tok()

    def g2_mm(te):
        W(te, v_r2r)
        bw(te, "D")
        cPE.inc(te.matmul(pbD[:, 0:2], cb_o1f.ap(), r2row.ap(),
                          start=True, stop=True))
    on("tensor", g2_mm)
    p_r2 = cPE.tok()

    def g2_c(ve):
        W(ve, p_r2, v_ampm, v_mpm)
        cDV.inc(ve.tensor_copy(r2pm.ap(), pbD[:, 0:2]))
        ve.wait_ge(cDV.h, cDV.n)
        cDV.inc(ve.tensor_scalar(out=dq2pm.ap(), in0=am2pm.ap(),
                                 scalar1=r2pm[:, 1:2], scalar2=1.0 / 127.0,
                                 op0=ALU.mult, op1=ALU.mult))
        cDV.inc(ve.reciprocal(t32a[:, 0:CH], am2pm.ap()))
        ve.wait_ge(cDV.h, cDV.n)
        cDV.inc(ve.tensor_scalar(out=t32b[:, 0:CH], in0=cb_b2t.ap(),
                                 scalar1=r2pm[:, 0:1], scalar2=127.0,
                                 op0=ALU.mult, op1=ALU.mult))
        ve.wait_ge(cDV.h, cDV.n)
        cDV.inc(ve.tensor_tensor(out=qb2pm.ap(), in0=t32b[:, 0:CH],
                                 in1=t32a[:, 0:CH], op=ALU.mult))
        ve.wait_ge(cDV.h, cDV.n)
        cDV.inc(ve.tensor_scalar(out=qb2pm.ap(), in0=qb2pm.ap(),
                                 scalar1=MAGIC, scalar2=MAGIC,
                                 op0=ALU.add, op1=ALU.subtract))
        ve.wait_ge(cDV.h, cDV.n)
        cDV.inc(ve.tensor_tensor(out=gbdq2.ap(), in0=gbpm.ap(),
                                 in1=dq2pm.ap()[:, :, None].broadcast_to([P, CH, B]),
                                 op=ALU.mult))
        ve.wait_ge(cDV.h, cDV.n)
    on("vector", g2_c)
    bankfree["D"] = cDV.tok()
    v_q2c = cDV.tok()

    def q2_tr(te):
        W(te, v_q2c)
        bw(te, "D")
        cPE.inc(te.transpose(pbD[0:CH, 0:P], qb2pm.ap(), cb_id.ap()))
    on("tensor", q2_tr)
    p_q2t = cPE.tok()

    def q2_ev(ve):
        W(ve, p_q2t)
        cDV.inc(ve.tensor_copy(sc40[0:CH, :], pbD[0:CH, 0:P]))
    on("vector", q2_ev)
    bankfree["D"] = cDV.tok()
    v_q2rows = cDV.tok()

    def q2_bb(sy):
        W(sy, v_q2rows, f1_pe_all)
        dma(chBC, sy, qb2_dram, sc40[0:CH, :])
        sy.wait_ge(chBC.h, chBC.n)
        chBC.pending_epoch = False
        dma(chBC, sy, vQB2R, qb2_dram.rearrange("r c -> (r c)")[None, :])
    on("sync", q2_bb)
    tok_q2r = use(chBC)

    def q2_hl(ve):
        W(ve, tok_q2r)
        cDV.inc(ve.tensor_copy(qb2hi, vQB2R))
        ve.wait_ge(cDV.h, cDV.n)
        cDV.inc(ve.tensor_tensor(out=vQB2R, in0=vQB2R, in1=qb2hi,
                                 op=ALU.subtract))
        ve.wait_ge(cDV.h, cDV.n)
        cDV.inc(ve.tensor_copy(qb2lo, vQB2R))
    on("vector", q2_hl)
    v_hl = cDV.tok()

    def f_qw2l(seq):
        W(seq, v_ag2, f1_pe_all)
        for j in range(32):
            chW.inc(seq.dma_start_transpose(vQW2[:, j, :],
                                            ago_qw2[:, j * P:(j + 1) * P]))
    on("scalar", f_qw2l)
    tok_qw2 = use(chW)

    f2 = {"uin_free": None, "q2_free": [None, None], "x2_free": [None, None],
          "x3_free": [None, None]}
    for ti in range(16):
        tg, th = ti // 2, ti % 2
        b, jt = tg // 4, tg % 4
        t0 = jt * TT + th * 256
        q2 = vQ2a if ti % 2 == 0 else vQ2b
        x2b = vX2F[:, ti % 2]
        x3b = vX3[:, ti % 2]

        def f2_ul(seq, tg=tg, th=th, prev=f2["uin_free"], first=(ti == 0)):
            W(seq, f1_sp_all, prev)
            if first:
                W(seq, fst["qh_free"][0], fst["qh_free"][1])
                W(seq, cDV.tok())
            for g in range(4):
                dma(chUL, seq, vUIN[:, g * CH:(g + 1) * CH, :],
                    u_dram[tg, g][:, :, th * 256:(th + 1) * 256])
        on("scalar", f2_ul)
        tok_ui = use(chUL)

        def f2_x2l(seq, b=b, t0=t0, x2b=x2b, prev=f2["x2_free"][ti % 2],
                   first=(ti < 2)):
            W(seq, prev)
            if first:
                W(seq, f1_sp_all)
            dma(chUL, seq, x2b, x2_dram[b][:, :, t0:t0 + 256])
        on("scalar", f2_x2l)
        tok_x2 = use(chUL)

        def f2_rd(se):
            W(se, tok_ui, v_q2c)
            uf = vUIN.rearrange("p j t -> p (j t)")
            cAC.inc(se.activation(uf, uf, AF.Copy, bias=MAGIC,
                                  scale=r2pm[:, 0:1]))
            se.wait_ge(cAC.h, cAC.n)
            cAC.inc(se.activation(uf, uf, AF.Relu, bias=-MAGIC))
        on("scalar", f2_rd)
        a_f2r = cAC.tok()

        def f2_cl(ve, q2=q2, prev=f2["q2_free"][ti % 2], first=(ti < 2)):
            W(ve, a_f2r, prev)
            if first:
                W(ve, f1_pe_all, v_hl)
            uf = vUIN.rearrange("p j t -> p (j t)")
            cDV.inc(ve.tensor_scalar_min(q2.rearrange("p j t -> p (j t)"),
                                         uf, 255.0))
        on("vector", f2_cl)
        v_q2t = cDV.tok()
        f2["uin_free"] = v_q2t

        for dc in range(CH):
            bank = "E" if dc % 2 == 0 else "F"
            def f2_mm(te, dc=dc, q2=q2, bank=bank):
                W(te, v_q2t, tok_qw2, v_hl, tok_cc)
                bw(te, bank)
                dst = (pbE.ap() if bank == "E" else pbF.ap())[:, 0:256]
                for fj in range(32):
                    te.matmul(dst, vQW2[:, fj, dc * P:(dc + 1) * P],
                              q2[:, fj, :], start=(fj == 0), stop=False)
                te.matmul(dst, qb2hi[0:1, dc * P:(dc + 1) * P], cb_orb.ap(),
                          start=False, stop=False)
                mm = te.matmul(dst, qb2lo[0:1, dc * P:(dc + 1) * P], cb_orb.ap(),
                               start=False, stop=True)
                cPE.inc(mm)
            on("tensor", f2_mm)
            p_f2 = cPE.tok()

            def f2_ev(ve, dc=dc, b=b, x2b=x2b, x3b=x3b, bank=bank,
                      prev=f2["x3_free"][ti % 2]):
                W(ve, p_f2, tok_x2, v_q2c)
                if dc == 0:
                    W(ve, prev)
                src = (pbE.ap() if bank == "E" else pbF.ap())[:, 0:256]
                cDV.inc(ve.scalar_tensor_tensor(
                    out=x3b[:, dc, :], in0=src,
                    scalar=gbdq2[:, dc, b:b + 1], in1=x2b[:, dc, :],
                    op0=ALU.mult, op1=ALU.add))
            on("vector", f2_ev)
            bankfree[bank] = cDV.tok()
        v_x3 = cDV.tok()
        f2["x2_free"][ti % 2] = v_x3
        f2["q2_free"][ti % 2] = cPE.tok()

        def f2_out(sy, b=b, t0=t0, x3b=x3b):
            W(sy, v_x3)
            dma(chOUT, sy, y_d[b].rearrange("j p t -> p j t")[:, :, t0:t0 + 256],
                x3b)
        on("sync", f2_out)
        f2["x3_free"][ti % 2] = use(chOUT)

    if DEBUG:
        dbg_outs = {}
        def dmk(name, shape, dtype=F32):
            t = nc.dram_tensor("dbg_" + name, list(shape), dtype,
                               kind="ExternalOutput").ap()
            dbg_outs[name] = t
            return t
        d_x2 = dmk("x2", [B, P, CH, TB])
        d_h2 = dmk("h2", [B, P, CH, TB])
        d_u0 = dmk("u0", [4, P, CH, TT])
        smalls = [("gapm", gapm.ap()), ("gbpm", gbpm.ap()), ("capm", capm.ap()),
                  ("sapm", sapm.ap()), ("cbpm", cbpm.ap()), ("sbpm", sbpm.ap()),
                  ("am1pm", am1pm.ap()), ("dq1pm", dq1pm.ap()),
                  ("qb1pm", qb1pm.ap()), ("qbdq1", qbdq1.ap()),
                  ("dq2pm", dq2pm.ap()), ("qb2pm", qb2pm.ap()),
                  ("gbdq2", gbdq2.ap()), ("r1pm", r1pm.ap()), ("r2pm", r2pm.ap()),
                  ("gm1", gm1.ap()), ("gm2", gm2.ap()), ("hmax", hmax.ap()),
                  ("umax", umax.ap()), ("attn1", attn_sb.ap())]
        for nm, ap in smalls:
            dmk(nm, ap.shape, ap.dtype)

        def dbg_dump(sy):
            sy.wait_ge(chOUT.h, chOUT.n)
            dma(chOUT, sy, d_x2, x2_dram)
            dma(chOUT, sy, d_h2, h2_dram)
            dma(chOUT, sy, d_u0, u_dram[0])
            for nm, ap in smalls:
                dma(chOUT, sy, dbg_outs[nm], ap)
        on("sync", dbg_dump)

    def fin(sy):
        sy.wait_ge(chOUT.h, chOUT.n)
    on("sync", fin)

    es.close()
    return nc


def _host_prepare(inputs):
    x = np.asarray(inputs["L0_attn_x"])
    lq = np.asarray(inputs["L0_attn_latent_queries"])
    Wk = np.asarray(inputs["L0_attn_W_k"])
    Wv = np.asarray(inputs["L0_attn_W_v"])
    Wg = np.asarray(inputs["L0_attn_W_g"])
    Wp = np.asarray(inputs["L0_attn_W_p"])
    w1 = np.asarray(inputs["L0_ffn_w1"])
    b1 = np.asarray(inputs["L0_ffn_b1"])
    w2 = np.asarray(inputs["L0_ffn_w2"])
    b2 = np.asarray(inputs["L0_ffn_b2"])
    c = np.asarray(inputs["L0_c"])
    wmod = np.asarray(inputs["L0_wmod"])

    g64 = np.concatenate([np.arange(0, 64, 2), np.arange(1, 64, 2)])
    gidx = (np.arange(16)[:, None] * 64 + g64[None, :]).reshape(-1)

    wkT = np.ascontiguousarray(Wk[gidx, :].T)
    wvT = np.ascontiguousarray(Wv.T)
    wgT = np.ascontiguousarray(Wg.T)
    wpT = np.ascontiguousarray(Wp.T)

    q = lq[0]
    qpk = np.empty((P, JP, 64), np.float32)
    for jp in range(JP):
        qpk[0:64, jp, :] = q[:, 2 * jp, :][:, g64].T
        qpk[64:128, jp, :] = q[:, 2 * jp + 1, :][:, g64].T

    inv = 1.0 / (10000.0 ** (np.arange(0, 64, 2, dtype=np.float64) / 64.0))
    tpos = (np.arange(NCK)[:, None] * 128 + np.arange(64)[None, :]).reshape(-1)
    pfreq = inv[np.arange(P) % 32]
    ang = tpos[None, :] * pfreq[:, None]
    cost = np.cos(ang).astype(np.float32)
    sint = np.sin(ang).astype(np.float32)
    sgn = np.where((np.arange(P) % 64) < 32, -1.0, 1.0).astype(np.float32)
    sint = sint * sgn[:, None]

    mneg = np.where((np.arange(P) % 64)[:, None] <= np.arange(64)[None, :],
                    0.0, NEGBIG).astype(np.float32)
    oneblk = np.zeros((P, 2), np.float32)
    oneblk[0:64, 0] = 1.0
    oneblk[64:128, 1] = 1.0
    blk2 = np.ascontiguousarray(oneblk.T)
    ident = np.eye(P, dtype=np.float32)
    onescol = np.ones((P, 1), np.float32)
    onesrow = np.ones((1, TT), np.float32)
    kconst = np.zeros((P, 4), np.float32)
    kconst[:, 0] = MAGIC
    kconst[:, 1] = LN_EPS
    kconst[:, 2] = QEPS
    kconst[:, 3] = -MAGIC

    b1t = np.ascontiguousarray(b1.reshape(32, P).T)
    b2t = np.ascontiguousarray(b2.reshape(CH, P).T)

    in_maps = []
    for cid in range(NCORES):
        xb = x[cid * B:(cid + 1) * B]
        x_lay = np.ascontiguousarray(xb.transpose(0, 2, 1).reshape(B, CH, P, TB))
        cb = c[cid * B:(cid + 1) * B]
        cT = np.ascontiguousarray(cb.T.reshape(CH, P, B).transpose(1, 0, 2))
        in_maps.append({
            "x": x_lay,
            "wkT": wkT, "wvT": wvT, "wgT": wgT, "wpT": wpT,
            "wmod": wmod, "cT": cT,
            "w1s": np.ascontiguousarray(w1[cid * 512:(cid + 1) * 512]),
            "w2s": np.ascontiguousarray(w2[cid * P:(cid + 1) * P]),
            "b1t": b1t, "b2t": b2t,
            "qpk": qpk, "cost": cost, "sint": sint, "mneg": mneg,
            "oneblk": oneblk, "blk2": blk2, "ident": ident,
            "onescol": onescol, "onesrow": onesrow, "kconst": kconst,
        })
    return in_maps


def kernel(**inputs):
    in_maps = _host_prepare(inputs)
    nc = build_program()
    res = run_bass_kernel_spmd(nc, in_maps, core_ids=list(range(NCORES)))
    outs = []
    for cid in range(NCORES):
        y = np.asarray(res.results[cid]["y"])
        outs.append(y.transpose(0, 3, 1, 2).reshape(B, TB, DM))
    return np.concatenate(outs, axis=0).astype(np.float32)


if __name__ == "__main__":
    nc = build_program()
    print("instructions:", nc.n_instructions())
    print("sbuf remaining:", nc.sbuf_bytes_remaining)


# version 29 (final)
# speedup vs baseline: 1.0866x; 1.0052x over previous
import numpy as np
import ml_dtypes
from contextlib import ExitStack

import concourse.bass as bass
import concourse.mybir as mybir
from concourse.bass_utils import run_bass_kernel_spmd

F32 = mybir.dt.float32
BF16 = mybir.dt.bfloat16
AF = mybir.ActivationFunctionType
ALU = mybir.AluOpType
AX = mybir.AxisListType

NCORES = 8
B = 2
TB = 2048
CH = 8
P = 128
DM = 1024
FF = 4096
TT = 512
NT = TB // TT
NCK = TB // 128
JP = 8
MAGIC = 12582912.0
NEGBIG = -1.0e9
LN_EPS = 1e-5
QEPS = 1e-8
RG = [list(range(NCORES))]

DEBUG = False


class Chan:
    def __init__(self, es, nc, name, step=1):
        self.h = es.enter_context(nc.semaphore(name))
        self.n = 0
        self.step = step
        self.pending_epoch = False

    def inc(self, instr):
        self.n += self.step
        instr.then_inc(self.h, self.step)
        return self.n

    def tok(self):
        return (self, self.n)


def build_program():
    nc = bass.Bass("TRN2", target_bir_lowering=False)
    es = ExitStack()
    S = es.enter_context

    def din(name, shape, dtype=F32):
        return nc.dram_tensor(name, shape, dtype, kind="ExternalInput").ap()

    def dint(name, shape, dtype=F32):
        return nc.dram_tensor(name, shape, dtype, kind="Internal").ap()

    x_d = din("x", [B, CH, P, TB])
    wkT_d = din("wkT", [DM, DM])
    wvT_d = din("wvT", [DM, DM])
    wgT_d = din("wgT", [DM, DM])
    wpT_d = din("wpT", [DM, DM])
    wmod_d = din("wmod", [DM, 6 * DM])
    cT_d = din("cT", [P, CH, B])
    w1s_d = din("w1s", [512, DM])
    w2s_d = din("w2s", [P, FF])
    b1t_d = din("b1t", [P, 32])
    b2t_d = din("b2t", [P, CH])
    qpk_d = din("qpk", [P, JP, 64])
    cost_d = din("cost", [P, NCK * 64])
    sint_d = din("sint", [P, NCK * 64])
    mneg_d = din("mneg", [P, 64])
    oneblk_d = din("oneblk", [P, 2])
    blk2_d = din("blk2", [2, P])
    ident_d = din("ident", [P, P])
    onescol_d = din("onescol", [P, 1])
    onesrow_d = din("onesrow", [1, TT])
    kconst_d = din("kconst", [P, 4])

    y_d = nc.dram_tensor("y", [B, CH, P, TB], F32, kind="ExternalOutput").ap()

    h2_dram = dint("h2sp", [B, P, CH, TB])
    x2_dram = dint("x2sp", [B, P, CH, TB])
    u_dram = dint("usp", [8, 4, P, CH, TT])
    rowb_dram = dint("rowb", [2, TT])
    colb_dram = dint("colb", [P])
    qb2_dram = dint("qb2b", [CH, P])
    agi_qw1 = dint("agi_qw1", [512, DM], BF16)
    ago_qw1 = dint("ago_qw1", [FF, DM], BF16)
    agi_qw2 = dint("agi_qw2", [P, FF], BF16)
    ago_qw2 = dint("ago_qw2", [DM, FF], BF16)
    agi_sc = dint("agi_sc", [5, P])
    ago_sc = dint("ago_sc", [40, P])
    ar1_in = dint("ar1_in", [1, 1])
    ar1_out = dint("ar1_out", [1, 1])
    ar2_in = dint("ar2_in", [1, 1])
    ar2_out = dint("ar2_out", [1, 1])

    BIGA = S(nc.sbuf_tensor("BIGA", [P, 20480], F32))
    SMA = S(nc.sbuf_tensor("SMA", [P, 12288], F32))
    STR = S(nc.sbuf_tensor("STR", [P, 8192], F32))
    SCR = S(nc.sbuf_tensor("SCR", [P, 4096], F32))

    BIGAb = BIGA.ap().bitcast(BF16)
    SMAb = SMA.ap().bitcast(BF16)
    STRf = STR.ap()
    SCRb = SCR.ap().bitcast(BF16)

    vK = BIGAb[:, 0:8192].rearrange("p (j t) -> p j t", j=CH)
    vV = BIGAb[:, 8192:24576].rearrange("p (n f) -> p n f", n=NCK)
    vG = BIGAb[:, 24576:40960].rearrange("p (j t) -> p j t", j=CH)
    vW = SMAb[:, 0:8192].rearrange("p (j f) -> p j f", j=CH)
    vW2 = BIGAb[:, 32768:40960].rearrange("p (j f) -> p j f", j=CH)
    vH = SMAb[:, 8192:24576].rearrange("p (s j t) -> p s j t", s=NT, j=CH)
    vXTd = [SMA.ap()[:, 4096:8192].rearrange("p (j t) -> p j t", j=CH),
            SMA.ap()[:, 8192:12288].rearrange("p (j t) -> p j t", j=CH)]
    vX2d = [STRf[:, 0:4096].rearrange("p (j t) -> p j t", j=CH),
            STRf[:, 4096:8192].rearrange("p (j t) -> p j t", j=CH)]
    vXBF0 = SCRb[:, 0:4096].rearrange("p (j t) -> p j t", j=CH)
    vXBF1 = STR.ap().bitcast(BF16)[:, 0:4096].rearrange("p (j t) -> p j t", j=CH)
    vSM = SCR.ap()[:, 0:1024]
    vESC = SCRb[:, 2048:3072]
    vPRB = SCRb[:, 3072:4096]
    vRDN = SCR.ap()[0:2, 2048:3072]
    vRDS = SCRb[0:2, 6144:7168]
    vWT = SCRb[:, 0:4096].rearrange("p (j t) -> p j t", j=CH)

    vMOD = BIGA.ap()[0:2, 0:6144]
    vWQF = BIGA.ap()[:, 8192:12288].rearrange("p (f c) -> p f c", f=4)
    vWQT = BIGA.ap()[:, 12288:13312]
    vWMB0 = BIGA.ap()[:, 13312:15360]
    vWMB1 = BIGA.ap()[:, 18432:20480]
    vWQQ = BIGAb[:, 32768:36864].rearrange("p (f c) -> p f c", f=4)
    vW2F = vWQF.rearrange("p f c -> p (f c)")
    vW2Q = vWQQ.rearrange("p f c -> p (f c)")

    vQW1 = BIGAb[:, 0:32768].rearrange("p (j f) -> p j f", j=CH)
    vQW2 = BIGAb[:, 0:32768].rearrange("p (j f) -> p j f", j=32)
    vQ2a = BIGAb[:, 32768:40960].rearrange("p (j t) -> p j t", j=32)
    vQ2b = SMAb[:, 16384:24576].rearrange("p (j t) -> p j t", j=32)
    vH2T = SMA.ap()[:, 0:8192].rearrange("p (s j t) -> p s j t", s=2, j=CH)
    vQH = SMAb[:, 16384:24576].rearrange("p (s j t) -> p s j t", s=2, j=CH)
    vUST = STRf.rearrange("p (s j t) -> p s j t", s=2, j=CH)
    vUIN = SMA.ap()[:, 0:8192].rearrange("p (j t) -> p j t", j=32)
    vX2F = STRf[:, 0:4096].rearrange("p (s j t) -> p s j t", s=2, j=CH)
    vX3 = STRf[:, 4096:8192].rearrange("p (s j t) -> p s j t", s=2, j=CH)
    vQB2R = SCR.ap()[0:1, 0:1024]

    kraw = S(nc.sbuf_tensor("kraw", [P, 512], BF16))
    kswp = S(nc.sbuf_tensor("kswp", [P, 512], BF16))
    kt1 = S(nc.sbuf_tensor("kt1", [P, 512], F32))
    kt2 = S(nc.sbuf_tensor("kt2", [P, 512], F32))
    apt = S(nc.sbuf_tensor("apt", [P, 2 * TT], F32))
    apt2 = S(nc.sbuf_tensor("apt2", [P, 2 * TT], F32))
    r_m = SCR.ap()[0:1, 2048:2560]
    r_var = SCR.ap()[0:1, 2560:3072]
    r_rstd = SCR.ap()[0:1, 3072:3584]
    r_negmr = SCR.ap()[0:1, 3584:4096]
    egt = S(nc.sbuf_tensor("egt", [P, TT], BF16))
    rdgf = S(nc.sbuf_tensor("rdgf", [2, TT], F32))
    rdg = S(nc.sbuf_tensor("rdg", [2, TT], BF16))
    attn_sb = S(nc.sbuf_tensor("attn_sb", [P, JP, 64], BF16))
    sapm = S(nc.sbuf_tensor("sapm", [P, CH, B], F32))
    capm = S(nc.sbuf_tensor("capm", [P, CH, B], F32))
    gapm = S(nc.sbuf_tensor("gapm", [P, CH, B], F32))
    sbpm = S(nc.sbuf_tensor("sbpm", [P, CH, B], F32))
    cbpm = S(nc.sbuf_tensor("cbpm", [P, CH, B], F32))
    gbpm = S(nc.sbuf_tensor("gbpm", [P, CH, B], F32))
    am1_l = S(nc.sbuf_tensor("am1_l", [P, 4], F32))
    rw1_l = S(nc.sbuf_tensor("rw1_l", [P, 4], F32))
    am2_l = S(nc.sbuf_tensor("am2_l", [P, 1], F32))
    rw2_l = S(nc.sbuf_tensor("rw2_l", [P, 1], F32))
    sc5 = S(nc.sbuf_tensor("sc5", [5, P], F32))
    sc40 = S(nc.sbuf_tensor("sc40", [40, P], F32))
    scT = S(nc.sbuf_tensor("scT", [P, 40], F32))
    am1pm = S(nc.sbuf_tensor("am1pm", [P, 32], F32))
    am2pm = S(nc.sbuf_tensor("am2pm", [P, CH], F32))
    dq1pm = S(nc.sbuf_tensor("dq1pm", [P, 32], F32))
    qb1pm = S(nc.sbuf_tensor("qb1pm", [P, 32], F32))
    qbdq1 = S(nc.sbuf_tensor("qbdq1", [P, 32], F32))
    dq2pm = S(nc.sbuf_tensor("dq2pm", [P, CH], F32))
    qb2pm = S(nc.sbuf_tensor("qb2pm", [P, CH], F32))
    gbdq2 = S(nc.sbuf_tensor("gbdq2", [P, CH, B], F32))
    qb2hi = SCRb[0:1, 2048:3072]
    qb2lo = SCRb[0:1, 3072:4096]
    t32a = S(nc.sbuf_tensor("t32a", [P, 32], F32))
    t32b = S(nc.sbuf_tensor("t32b", [P, 32], F32))
    hmax = S(nc.sbuf_tensor("hmax", [P, 1], F32))
    umax = S(nc.sbuf_tensor("umax", [P, 1], F32))
    tmax = S(nc.sbuf_tensor("tmax", [P, 1], F32))
    row128 = S(nc.sbuf_tensor("row128", [1, P], F32))
    gm1 = S(nc.sbuf_tensor("gm1", [1, 1], F32))
    gm2 = S(nc.sbuf_tensor("gm2", [1, 1], F32))
    r1row = S(nc.sbuf_tensor("r1row", [1, 2], F32))
    r2row = S(nc.sbuf_tensor("r2row", [1, 2], F32))
    r1pm = S(nc.sbuf_tensor("r1pm", [P, 2], F32))
    r2pm = S(nc.sbuf_tensor("r2pm", [P, 2], F32))
    cb_qpk = S(nc.sbuf_tensor("cb_qpk", [P, JP, 64], BF16))
    cb_cos = S(nc.sbuf_tensor("cb_cos", [P, NCK * 64], BF16))
    cb_sin = S(nc.sbuf_tensor("cb_sin", [P, NCK * 64], BF16))
    cb_mneg = S(nc.sbuf_tensor("cb_mneg", [P, 64], BF16))
    cb_ob = S(nc.sbuf_tensor("cb_ob", [P, 2], BF16))
    cb_b2 = S(nc.sbuf_tensor("cb_b2", [2, P], BF16))
    cb_b2f = S(nc.sbuf_tensor("cb_b2f", [2, P], F32))
    cb_id = S(nc.sbuf_tensor("cb_id", [P, P], F32))
    cb_o1b = S(nc.sbuf_tensor("cb_o1b", [P, 1], BF16))
    cb_o1cf = S(nc.sbuf_tensor("cb_o1cf", [P, 1], F32))
    cb_o1f = S(nc.sbuf_tensor("cb_o1f", [1, P], F32))
    cb_orb = S(nc.sbuf_tensor("cb_orb", [1, 256], BF16))
    cb_kc = S(nc.sbuf_tensor("cb_kc", [P, 4], F32))
    cb_idb = S(nc.sbuf_tensor("cb_idb", [P, P], BF16))
    vRDT = S(nc.sbuf_tensor("vRDT", [P, 16], BF16))
    vRDT2 = S(nc.sbuf_tensor("vRDT2", [P, 8], BF16))
    vRDNb = SCRb[0:2, 4096:5120]
    cb_cT = S(nc.sbuf_tensor("cb_cT", [P, CH, B], F32))
    silc = S(nc.sbuf_tensor("silc", [P, CH * B], F32))
    cb_b1t = S(nc.sbuf_tensor("cb_b1t", [P, 32], F32))
    cb_b2t = S(nc.sbuf_tensor("cb_b2t", [P, CH], F32))

    pbA = S(nc.psum_tensor("pbA", [P, 1024], F32))
    pbB = S(nc.psum_tensor("pbB", [P, 1024], F32))
    pbC = S(nc.psum_tensor("pbC", [P, 512], F32))
    pbD = S(nc.psum_tensor("pbD", [P, 512], F32))
    pbE = S(nc.psum_tensor("pbE", [P, 512], F32))
    pbF = S(nc.psum_tensor("pbF", [P, 512], F32))

    chX = Chan(es, nc, "chX", 16)
    chSP = Chan(es, nc, "chSP", 16)
    chOUT = Chan(es, nc, "chOUT", 16)
    chBC = Chan(es, nc, "chBC", 16)
    chW = Chan(es, nc, "chW", 16)
    chUL = Chan(es, nc, "chUL", 16)
    chGC = Chan(es, nc, "chGC", 16)
    chVD = Chan(es, nc, "chVD", 16)
    chCC = Chan(es, nc, "chCC", 16)
    cCC = Chan(es, nc, "cCC", 1)
    cPE = Chan(es, nc, "cPE", 1)
    cDV = Chan(es, nc, "cDV", 1)
    cAC = Chan(es, nc, "cAC", 1)
    cGP = Chan(es, nc, "cGP", 1)

    nc.const_aps.register(F32, -MAGIC, cb_kc[:, 3:4])
    block = S(nc.Block())

    def on(eng, f):
        getattr(block, eng)(f)

    def W(e, *tokens):
        for t in tokens:
            if t is None:
                continue
            c, v = t
            if v > 0:
                e.wait_ge(c.h, v)

    def dma(ch, q, out, in_):
        if ch.pending_epoch:
            q.wait_ge(ch.h, ch.n)
            ch.pending_epoch = False
        return ch.inc(q.dma_start(out=out, in_=in_))

    def use(ch):
        ch.pending_epoch = True
        return (ch, ch.n)

    bankfree = {}

    def bw(e, *names):
        for nm in names:
            t = bankfree.get(nm)
            if t is not None:
                W(e, t)

    def a_loads(sy):
        for sb_t, d in [(cb_cos, None), (cb_sin, None)]:
            pass
        for sb_t, d in [(cb_id, ident_d), (cb_b2f, blk2_d),
                        (cb_o1f, onesrow_d[:, 0:P]), (cb_o1cf, onescol_d), (cb_kc, kconst_d),
                        (cb_cT, cT_d), (cb_b1t, b1t_d), (cb_b2t, b2t_d)]:
            dma(chX, sy, sb_t.ap(), d)
    on("sync", a_loads)
    tok_cst = use(chX)

    def a_casts(ge):
        for sb_t, d in [(cb_qpk, qpk_d), (cb_mneg, mneg_d), (cb_ob, oneblk_d), (cb_b2, blk2_d),
                        (cb_idb, ident_d),
                        (cb_o1b, onescol_d), (cb_orb, onesrow_d[:, 0:256]),
                        (cb_cos, cost_d), (cb_sin, sint_d)]:
            dma(chGC, ge, sb_t.ap(), d)
    on("gpsimd", ge_f := a_casts)
    tok_cc = use(chGC)

    def a_w1s(seq):
        dma(chW, seq, vWQF, w1s_d.rearrange("(ft p) c -> p ft c", p=P))
    on("scalar", a_w1s)
    tok_w1s = use(chW)

    def b_am1(ve):
        W(ve, tok_w1s)
        cDV.inc(ve.tensor_reduce(out=am1_l.ap(), in_=vWQF, axis=AX.X,
                                 op=ALU.max, apply_absolute_value=True))
        ve.wait_ge(cDV.h, cDV.n)
        cDV.inc(ve.tensor_scalar_max(am1_l.ap(), am1_l.ap(), QEPS))
        ve.wait_ge(cDV.h, cDV.n)
        cDV.inc(ve.reciprocal(rw1_l.ap(), am1_l.ap()))
        ve.wait_ge(cDV.h, cDV.n)
        cDV.inc(ve.tensor_scalar_mul(rw1_l.ap(), rw1_l.ap(), 127.0))
    on("vector", b_am1)
    v_am1 = cDV.tok()

    v_qts = None
    for ft in range(4):
        def b_qa(se, ft=ft, prev=v_qts):
            W(se, v_am1, tok_cst, prev)
            cAC.inc(se.activation(vWQT, vWQF[:, ft, :], AF.Copy,
                                  bias=MAGIC, scale=rw1_l[:, ft:ft + 1]))
        on("scalar", b_qa)
        a_q = cAC.tok()

        def b_qt(ve, ft=ft):
            W(ve, a_q)
            cDV.inc(ve.tensor_scalar(out=vWQT, in0=vWQT,
                                     scalar1=MAGIC, scalar2=-128.0,
                                     op0=ALU.subtract, op1=ALU.max))
            ve.wait_ge(cDV.h, cDV.n)
            cDV.inc(ve.tensor_scalar_min(vWQQ[:, ft, :], vWQT, 127.0))
        on("vector", b_qt)
        v_qts = cDV.tok()
    v_qw1 = cDV.tok()

    def b_st1(ge):
        W(ge, v_qw1)
        dma(chCC, ge, agi_qw1.rearrange("(ft p) c -> p ft c", p=P), vWQQ)
    on("gpsimd", b_st1)
    tok_agi1 = use(chCC)

    def b_ag1(ge):
        W(ge, tok_agi1)
        cCC.inc(ge.collective_compute("AllGather", ALU.bypass, replica_groups=RG,
                                      ins=[agi_qw1], outs=[ago_qw1]))
    on("gpsimd", b_ag1)
    v_ag1 = cCC.tok()

    def b_w2l(seq):
        W(seq, v_qw1)
        dma(chW, seq, vW2F, w2s_d)
    on("scalar", b_w2l)
    tok_w2s = use(chW)

    def b_am2(ve):
        W(ve, tok_w2s)
        cDV.inc(ve.tensor_reduce(out=am2_l.ap(), in_=vW2F, axis=AX.X,
                                 op=ALU.max, apply_absolute_value=True))
        ve.wait_ge(cDV.h, cDV.n)
        cDV.inc(ve.tensor_scalar_max(am2_l.ap(), am2_l.ap(), QEPS))
        ve.wait_ge(cDV.h, cDV.n)
        cDV.inc(ve.reciprocal(rw2_l.ap(), am2_l.ap()))
        ve.wait_ge(cDV.h, cDV.n)
        cDV.inc(ve.tensor_scalar_mul(rw2_l.ap(), rw2_l.ap(), 127.0))
    on("vector", b_am2)
    v_am2 = cDV.tok()

    def b_qa2(se):
        W(se, v_am2, tok_agi1)
        cAC.inc(se.activation(vW2F, vW2F, AF.Copy, bias=MAGIC,
                              scale=rw2_l[:, 0:1]))
    on("scalar", b_qa2)
    a_q2 = cAC.tok()

    def b_qt2(ve):
        W(ve, a_q2)
        cDV.inc(ve.tensor_scalar(out=vW2F, in0=vW2F, scalar1=MAGIC,
                                 scalar2=-128.0, op0=ALU.subtract, op1=ALU.max))
        ve.wait_ge(cDV.h, cDV.n)
        cDV.inc(ve.tensor_scalar_min(vW2Q, vW2F, 127.0))
    on("vector", b_qt2)
    v_qw2 = cDV.tok()

    def b_st2(ge):
        W(ge, v_qw2)
        dma(chCC, ge, agi_qw2, vW2Q)
    on("gpsimd", b_st2)
    tok_agi2 = use(chCC)

    def b_ag2(ge):
        W(ge, tok_agi2)
        cCC.inc(ge.collective_compute("AllGather", ALU.bypass, replica_groups=RG,
                                      ins=[agi_qw2], outs=[ago_qw2]))
    on("gpsimd", b_ag2)
    v_ag2 = cCC.tok()

    def b_tr1(te):
        W(te, v_am1, v_am2, tok_cst)
        bw(te, "D")
        cPE.inc(te.transpose(pbD[0:4, 0:P], am1_l.ap(), cb_id.ap()))
    on("tensor", b_tr1)
    p_t1 = cPE.tok()

    def b_te1(ve):
        W(ve, p_t1)
        cDV.inc(ve.tensor_copy(sc5[0:4, :], pbD[0:4, 0:P]))
    on("vector", b_te1)
    bankfree["D"] = cDV.tok()

    def b_tr2(te):
        bw(te, "D")
        cPE.inc(te.transpose(pbD[0:1, 0:P], am2_l.ap(), cb_id.ap()))
    on("tensor", b_tr2)
    p_t2 = cPE.tok()

    def b_te2(ve):
        W(ve, p_t2)
        cDV.inc(ve.tensor_copy(row128.ap(), pbD[0:1, 0:P]))
    on("vector", b_te2)
    bankfree["D"] = cDV.tok()
    v_sc5 = cDV.tok()

    def b_te2b(ge):
        W(ge, v_sc5)
        dma(chVD, ge, sc5[4:5, :], row128.ap())
    on("gpsimd", b_te2b)
    tok_sc5r = use(chVD)

    def b_st3(ge):
        W(ge, v_sc5, tok_sc5r)
        dma(chCC, ge, agi_sc, sc5.ap())
    on("gpsimd", b_st3)
    tok_agisc = use(chCC)

    def b_ag3(ge):
        W(ge, tok_agisc)
        cCC.inc(ge.collective_compute("AllGather", ALU.bypass, replica_groups=RG,
                                      ins=[agi_sc], outs=[ago_sc]))
    on("gpsimd", b_ag3)
    v_ag3 = cCC.tok()

    def m_silu(se):
        W(se, tok_cst)
        cAC.inc(se.activation(silc.ap(), cb_cT.ap().rearrange("p j b -> p (j b)"),
                              AF.Silu))
    on("scalar", m_silu)
    a_silu = cAC.tok()

    def m_zero(ve):
        W(ve, tok_agi2)
        cDV.inc(ve.memset(vMOD, 0.0))
    on("vector", m_zero)
    v_mz = cDV.tok()

    wm_prev_pe = [None, None]
    for j in range(CH):
        for q in range(3):
            wb = vWMB0 if (j * 3 + q) % 2 == 0 else vWMB1
            def m_wl(ge, j=j, q=q, prev=wm_prev_pe[(j * 3 + q) % 2]):
                W(ge, prev)
                dma(chGC, ge, wb,
                    wmod_d[j * P:(j + 1) * P, q * 2048:(q + 1) * 2048])
            on("gpsimd", m_wl)
            tok_wm = use(chGC)

            for fi in range(4):
                ftile = q * 4 + fi
                def m_mm(te, j=j, fi=fi, ftile=ftile, wb=wb):
                    W(te, tok_wm, a_silu)
                    if j == 0 and ftile == 0:
                        W(te, v_mz)
                    bw(te, "C")
                    cPE.inc(te.matmul(pbC[0:2, :], silc[:, 2 * j:2 * j + 2],
                                      wb[:, fi * 512:(fi + 1) * 512],
                                      start=True, stop=True))
                on("tensor", m_mm)
                p_mm = cPE.tok()

                def m_acc(ve, ftile=ftile):
                    W(ve, p_mm)
                    sl = vMOD[:, ftile * 512:(ftile + 1) * 512]
                    cDV.inc(ve.tensor_tensor(out=sl, in0=sl, in1=pbC[0:2, :],
                                             op=ALU.add))
                on("vector", m_acc)
                bankfree["C"] = cDV.tok()
            wm_prev_pe[(j * 3 + q) % 2] = cPE.tok()
    v_mod = cDV.tok()
    p_modmm = cPE.tok()

    for v6, dest in [(0, sapm), (1, capm), (2, gapm), (3, sbpm), (4, cbpm),
                     (5, gbpm)]:
        for j in range(CH):
            def m_tr(te, v6=v6, j=j):
                W(te, v_mod)
                bw(te, "D")
                cPE.inc(te.transpose(
                    pbD[0:P, 0:2],
                    vMOD[:, v6 * DM + j * P: v6 * DM + (j + 1) * P],
                    cb_id[0:2, 0:2]))
            on("tensor", m_tr)
            p_t = cPE.tok()

            def m_ev(ve, dest=dest, j=j):
                W(ve, p_t)
                cDV.inc(ve.tensor_copy(dest[:, j, :], pbD[0:P, 0:2]))
            on("vector", m_ev)
            bankfree["D"] = cDV.tok()
    p_modtr = cPE.tok()

    def m_caps(ve):
        cDV.inc(ve.tensor_scalar_add(capm.ap(), capm.ap(), 1.0))
        cDV.inc(ve.tensor_scalar_add(cbpm.ap(), cbpm.ap(), 1.0))
    on("vector", m_caps)

    for g in (gapm, gbpm):
        gv = g.ap().rearrange("p j b -> p (j b)")
        v_pre = cDV.tok()

        def m_abs(se, gv=gv, v_pre=v_pre):
            W(se, v_pre)
            cAC.inc(se.activation(t32b[:, 0:CH * B], gv, AF.Abs))
        on("scalar", m_abs)
        a_abs = cAC.tok()

        def m_cl(ve, gv=gv, a_abs=a_abs):
            W(ve, a_abs)
            cDV.inc(ve.tensor_scalar(out=t32a[:, 0:CH * B], in0=gv,
                                     scalar1=0.0, scalar2=2.0,
                                     op0=ALU.is_ge, op1=ALU.mult))
            ve.wait_ge(cDV.h, cDV.n)
            cDV.inc(ve.tensor_scalar_sub(t32a[:, 0:CH * B], t32a[:, 0:CH * B], 1.0))
            cDV.inc(ve.tensor_scalar_max(t32b[:, 0:CH * B], t32b[:, 0:CH * B], 0.25))
            ve.wait_ge(cDV.h, cDV.n)
            cDV.inc(ve.tensor_tensor(out=gv, in0=t32a[:, 0:CH * B],
                                     in1=t32b[:, 0:CH * B], op=ALU.mult))
            ve.wait_ge(cDV.h, cDV.n)
        on("vector", m_cl)
    v_mpm = cDV.tok()

    early_pe = [p_modtr]
    early_cc = [use(chCC)]

    def c_init(ve):
        cDV.inc(ve.memset(hmax.ap(), -3.0e38))
        cDV.inc(ve.memset(umax.ap(), 0.0))
        cDV.inc(ve.memset(attn_sb.ap(), 0.0))
    on("vector", c_init)

    st = {
        "xbf_free": None,
        "h_free": [None] * NT,
        "k_free": None, "v_free": None, "g_free": None,
        "w_free": None,
        "xt_free": None, "x2_free": None,
        "xt_free2": [None, None], "x2_free2": [None, None],
        "w2_free": None,
        "c5_done": None,
        "wt_free": None,
        "scr_mid_free": None,
        "egt_free": None,
        "attn_free": None,
        "x2sp": [None, None],
        "h2rd": None,
    }
    first_k_evac = [True]
    first_v_evac = [True]
    first_g_evac = [True]

    for b in range(B):
        def c_wk(ge, b=b, wfree=st["w_free"]):
            W(ge, wfree)
            dma(chGC, ge, vW, wkT_d.rearrange("(j p) f -> p j f", p=P))
        on("gpsimd", c_wk)
        tok_wk = use(chGC)

        apply_toks = [None] * NT
        xbf_free = st.get("xbf_free2", [None, None])

        def emit_C1(jt, b=b):
            xb = vXBF0 if jt % 2 == 0 else vXBF1
            def c_xbf1(ge, jt=jt, xb=xb, prev=xbf_free[jt % 2]):
                W(ge, prev)
                if b == 1 and jt < 2:
                    W(ge, st["c5_done"])
                dma(chGC, ge, xb,
                    x_d[b, :, :, jt * TT:(jt + 1) * TT].rearrange("j p t -> p j t"))
            on("gpsimd", c_xbf1)
            tok_x1 = use(chGC)

            def c_sum1(te, first=(b == 0 and jt == 0), xb=xb):
                W(te, tok_x1, tok_cc)
                if first:
                    W(te, p_modmm)
                bw(te, "C")
                for j in range(CH):
                    mm = te.matmul(pbC[0:1, :], cb_o1b.ap(), xb[:, j, :],
                                   start=(j == 0), stop=(j == CH - 1))
                    if j == CH - 1:
                        cPE.inc(mm)
            on("tensor", c_sum1)
            p_s1 = cPE.tok()

            def c_sq(ve, xb=xb):
                W(ve, p_s1)
                xf = xb.rearrange("p j t -> p (j t)")
                cDV.inc(ve.tensor_mul(xf, xf, xf))
            on("vector", c_sq)
            v_sq = cDV.tok()

            def c_sum2(te, xb=xb):
                W(te, v_sq)
                bw(te, "D")
                for j in range(CH):
                    mm = te.matmul(pbD[0:1, :], cb_o1b.ap(), xb[:, j, :],
                                   start=(j == 0), stop=(j == CH - 1))
                    if j == CH - 1:
                        cPE.inc(mm)
            on("tensor", c_sum2)
            p_s2 = cPE.tok()

            def c_xbf2(ge, b=b, jt=jt, xb=xb):
                W(ge, p_s2)
                dma(chGC, ge, xb,
                    x_d[b, :, :, jt * TT:(jt + 1) * TT].rearrange("j p t -> p j t"))
            on("gpsimd", c_xbf2)
            tok_x2b = use(chGC)

            def c_rows(ve):
                W(ve, p_s1, p_s2)
                cDV.inc(ve.tensor_scalar_mul(r_m, pbC[0:1, :], 1.0 / DM))
                ve.wait_ge(cDV.h, cDV.n)
                cDV.inc(ve.tensor_tensor(out=r_var, in0=r_m, in1=r_m,
                                         op=ALU.mult))
                ve.wait_ge(cDV.h, cDV.n)
                cDV.inc(ve.scalar_tensor_tensor(out=r_var, in0=pbD[0:1, :],
                                                scalar=1.0 / DM, in1=r_var,
                                                op0=ALU.mult, op1=ALU.subtract))
            on("vector", c_rows)
            v_var = cDV.tok()

            def c_rowa(se):
                W(se, v_var)
                cAC.inc(se.activation(r_var, r_var, AF.Ln, bias=cb_kc[0:1, 1:2]))
                se.wait_ge(cAC.h, cAC.n)
                cAC.inc(se.activation(r_rstd, r_var, AF.Exp, scale=-0.5))
            on("scalar", c_rowa)
            a_rstd = cAC.tok()

            def c_row2(ve):
                W(ve, a_rstd)
                cDV.inc(ve.scalar_tensor_tensor(out=r_negmr, in0=r_m,
                                                scalar=-1.0, in1=r_rstd,
                                                op0=ALU.mult, op1=ALU.mult))
            on("vector", c_row2)
            v_rows = cDV.tok()

            def c_bcmm(te):
                W(te, v_rows)
                cPE.inc(te.matmul(pbC.ap(), cb_o1f.ap(), r_rstd,
                                  start=True, stop=True))
                cPE.inc(te.matmul(pbD.ap(), cb_o1f.ap(), r_negmr,
                                  start=True, stop=True))
            on("tensor", c_bcmm)
            p_bc = cPE.tok()

            def c_apply(ve, b=b, jt=jt, xb=xb, hfree=st["h_free"][jt]):
                W(ve, p_bc, tok_x2b, v_mpm, hfree)
                for j in range(CH):
                    ap1 = apt[:, 0:512]
                    cDV.inc(ve.tensor_tensor(out=ap1, in0=xb[:, j, :],
                                             in1=pbC.ap(), op=ALU.mult))
                    ve.wait_ge(cDV.h, cDV.n)
                    cDV.inc(ve.tensor_tensor(out=ap1, in0=ap1,
                                             in1=pbD.ap(), op=ALU.add))
                    ve.wait_ge(cDV.h, cDV.n)
                    cDV.inc(ve.tensor_scalar(out=vH[:, jt, j, :], in0=ap1,
                                             scalar1=capm[:, j, b:b + 1],
                                             scalar2=sapm[:, j, b:b + 1],
                                             op0=ALU.mult, op1=ALU.add))
                    ve.wait_ge(cDV.h, cDV.n)
            on("vector", c_apply)
            apply_toks[jt] = cDV.tok()
            bankfree["C"] = cDV.tok()
            bankfree["D"] = cDV.tok()
            xbf_free[jt % 2] = cDV.tok()

        kslots = [pbE.ap(), pbF.ap()]
        kkeys = ["E", "F"]

        def emit_K(jt, b=b):
            for jp2 in range(4):
                si = (jt * 4 + jp2) % 2
                dst = kslots[si]
                def c_kmm(te, jt=jt, jp2=jp2, si=si, dst=dst):
                    W(te, apply_toks[jt], tok_wk)
                    bw(te, kkeys[si])
                    rhs = vH[:, jt].rearrange("p j (n c) -> p j n c", c=P)[:, :, :, 0:64]
                    for half in range(2):
                        jo = jp2 * 2 + half
                        for jk in range(CH):
                            mm = te.matmul(dst[:, half * 256:(half + 1) * 256],
                                           vW[:, jk, jo * P:(jo + 1) * P],
                                           rhs[:, jk], start=(jk == 0),
                                           stop=(jk == CH - 1))
                            if jk == CH - 1 and half == 1:
                                cPE.inc(mm)
                on("tensor", c_kmm)
                p_k = cPE.tok()

                def c_kev(se, dst=dst, kfree=st["k_free"]):
                    W(se, p_k, kfree)
                    cAC.inc(se.copy(kraw.ap(), dst))
                on("scalar", c_kev)
                a_kraw = cAC.tok()
                bankfree[kkeys[si]] = a_kraw

                def c_ksw(ge):
                    W(ge, a_kraw)
                    for (d0, s0) in [(0, 32), (32, 0), (64, 96), (96, 64)]:
                        dma(chVD, ge, kswp[d0:d0 + 32, :], kraw[s0:s0 + 32, :])
                on("gpsimd", c_ksw)
                tok_ksw = use(chVD)

                def c_krope(ve, jt=jt, jp2=jp2, first=first_k_evac[0]):
                    W(ve, tok_ksw, tok_cc)
                    if first:
                        W(ve, p_modmm, tok_agi2)
                        W(ve, early_cc[0])
                    csl = slice(jt * 256, (jt + 1) * 256)
                    cosw = cb_cos[:, None, csl].broadcast_to([P, 2, 256])
                    sinw = cb_sin[:, None, csl].broadcast_to([P, 2, 256])
                    cDV.inc(ve.tensor_tensor(out=kt1.ap().rearrange("p (a t) -> p a t", a=2),
                                             in0=kraw.ap().rearrange("p (a t) -> p a t", a=2),
                                             in1=cosw, op=ALU.mult))
                    cDV.inc(ve.tensor_tensor(out=kt2.ap().rearrange("p (a t) -> p a t", a=2),
                                             in0=kswp.ap().rearrange("p (a t) -> p a t", a=2),
                                             in1=sinw, op=ALU.mult))
                    ve.wait_ge(cDV.h, cDV.n)
                    cDV.inc(ve.tensor_tensor(
                        out=vK[:, jp2 * 2:jp2 * 2 + 2, csl],
                        in0=kt1.ap().rearrange("p (a t) -> p a t", a=2),
                        in1=kt2.ap().rearrange("p (a t) -> p a t", a=2),
                        op=ALU.add))
                on("vector", c_krope)
                first_k_evac[0] = False
                st["k_free"] = cDV.tok()

        def c_wv(ge, b=b, wfree=st["w2_free"]):
            W(ge, wfree)
            if b == 0:
                W(ge, tok_agi2, p_modmm)
            dma(chGC, ge, vW2, wvT_d.rearrange("(j p) f -> p j f", p=P))
        on("gpsimd", c_wv)
        tok_wv = use(chGC)

        def emit_V(jt, b=b):
            for sub in range(4):
                nchunk = jt * 4 + sub
                vbank = pbA.ap() if nchunk % 2 == 0 else pbB.ap()
                vkey = "A01" if nchunk % 2 == 0 else "B23"
                def c_vmm(te, jt=jt, sub=sub, vbank=vbank, vkey=vkey,
                          first=(jt == 0 and sub < 2)):
                    W(te, apply_toks[jt], tok_wv)
                    bw(te, vkey)
                    if first:
                        bw(te, "B2", "B3")
                    for jk in range(CH):
                        for fh in range(2):
                            mm = te.matmul(vbank[:, fh * 512:(fh + 1) * 512],
                                           vH[:, jt, jk, sub * P:(sub + 1) * P],
                                           vW2[:, jk, fh * 512:(fh + 1) * 512],
                                           start=(jk == 0), stop=(jk == CH - 1))
                            if jk == CH - 1 and fh == 1:
                                cPE.inc(mm)
                on("tensor", c_vmm)
                p_v = cPE.tok()

                def c_vev(se, nchunk=nchunk, vbank=vbank, vfree=st["v_free"],
                          first=first_v_evac[0]):
                    W(se, p_v, vfree)
                    if first:
                        W(se, p_modmm, tok_agi2)
                    cAC.inc(se.copy(vV[0:64, nchunk, :], vbank[0:64, :]))
                on("scalar", c_vev)
                first_v_evac[0] = False
                a_vev = cAC.tok()
                bankfree[vkey] = a_vev
                if nchunk % 2 == 0:
                    bankfree["A0"] = a_vev
                    bankfree["A1"] = a_vev
                else:
                    bankfree["B2"] = a_vev
                    bankfree["B3"] = a_vev

                def c_vdup(ge, nchunk=nchunk):
                    W(ge, a_vev)
                    dma(chVD, ge, vV[64:128, nchunk, :], vV[0:64, nchunk, :])
                on("gpsimd", c_vdup)

        for jt in range(NT):
            emit_C1(jt)
            if jt > 0:
                emit_K(jt - 1)
            if jt > 1:
                emit_V(jt - 2)
        emit_K(NT - 1)
        emit_V(NT - 2)
        emit_V(NT - 1)
        st["xbf_free2"] = xbf_free
        k_done = cDV.tok()
        v_done = use(chVD)
        v_done_pe = cPE.tok()

        def c_wg(ge, b=b, kpe=k_done):
            W(ge, kpe)
            dma(chGC, ge, vW, wgT_d.rearrange("(j p) f -> p j f", p=P))
        on("gpsimd", c_wg)
        tok_wg = use(chGC)

        for jt in range(NT):
            for jo in range(CH):
                gbank = pbF.ap() if jo % 2 == 0 else pbE.ap()
                gkey = "F" if jo % 2 == 0 else "E"
                def c_gmm(te, jt=jt, jo=jo, gbank=gbank, gkey=gkey):
                    W(te, apply_toks[jt], tok_wg)
                    bw(te, gkey)
                    for jk in range(CH):
                        mm = te.matmul(gbank, vW[:, jk, jo * P:(jo + 1) * P],
                                       vH[:, jt, jk, :], start=(jk == 0),
                                       stop=(jk == CH - 1))
                        if jk == CH - 1:
                            cPE.inc(mm)
                on("tensor", c_gmm)
                p_g = cPE.tok()

                def c_gex(se, gbank=gbank, egfree=st["egt_free"]):
                    W(se, p_g, egfree)
                    cAC.inc(se.activation(egt.ap(), gbank, AF.Exp, scale=0.125))
                on("scalar", c_gex)
                a_eg = cAC.tok()
                bankfree[gkey] = a_eg

                def c_gden(te):
                    W(te, a_eg, tok_cc)
                    bw(te, "B3")
                    for c in range(4):
                        mm = te.matmul(pbB[:, 512 + 2 * c:514 + 2 * c],
                                       egt.ap()[:, c * 128:(c + 1) * 128],
                                       cb_ob.ap(), start=True, stop=True)
                    cPE.inc(mm)
                on("tensor", c_gden)
                p_gd = cPE.tok()

                def c_grd(ve):
                    W(ve, p_gd)
                    with nc.allow_low_precision(reason="bf16 softmax recip"):
                        cDV.inc(ve.reciprocal(vRDT2.ap(), pbB[:, 512:520]))
                on("vector", c_grd)
                v_rdg = cDV.tok()

                def c_grt(te):
                    W(te, v_rdg)
                    for c in range(4):
                        mm = te.matmul(pbB[0:2, 512 + c * 128:640 + c * 128],
                                       vRDT2[:, 2 * c:2 * c + 2],
                                       cb_idb.ap(), start=True, stop=True)
                    cPE.inc(mm)
                on("tensor", c_grt)
                p_grt = cPE.tok()

                def c_gre(ve):
                    W(ve, p_grt)
                    with nc.allow_low_precision(reason="bf16 softmax recip"):
                        cDV.inc(ve.tensor_scalar(out=rdg.ap(),
                                                 in0=pbB[0:2, 512:1024],
                                                 scalar1=1.0, scalar2=0.0,
                                                 op0=ALU.mult, op1=ALU.add))
                on("vector", c_gre)
                v_gre = cDV.tok()
                bankfree["B3"] = v_gre

                def c_grdb(te):
                    W(te, v_gre)
                    bw(te, "B2")
                    cPE.inc(te.matmul(pbB[:, 0:512], cb_b2.ap(), rdg.ap(),
                                      start=True, stop=True))
                on("tensor", c_grdb)
                p_grdb = cPE.tok()

                def c_gn(ve, jt=jt, jo=jo, first=first_g_evac[0]):
                    W(ve, p_grdb)
                    if first:
                        W(ve, p_modmm, tok_agi2)
                    if jo >= 4 and jt == 0:
                        W(ve, v_done_pe)
                    cDV.inc(ve.tensor_tensor(out=vG[:, jo, jt * TT:(jt + 1) * TT],
                                             in0=egt.ap(), in1=pbB[:, 0:512],
                                             op=ALU.mult))
                on("vector", c_gn)
                first_g_evac[0] = False
                st["egt_free"] = cDV.tok()
                bankfree["B2"] = cDV.tok()
        g_done = cDV.tok()
        g_done_pe = cPE.tok()

        tail_emit = [None]
        for jp in range(JP):
            def c_sc(te, jp=jp):
                W(te, k_done, tok_cc)
                if jp == 0:
                    W(te, st["scr_mid_free"])
                bw(te, "A01")
                for n in range(NCK):
                    osl = slice(n * 64, (n + 1) * 64)
                    te.matmul(pbA[0:64, osl], vK[0:64, jp, osl],
                              cb_qpk[0:64, jp, :], start=True, stop=True,
                              tile_position=(0, 0))
                    mm2 = te.matmul(pbA[64:128, osl], vK[64:128, jp, osl],
                                    cb_qpk[64:128, jp, :], start=True, stop=True,
                                    tile_position=(64, 64))
                    if n == NCK - 1:
                        cPE.inc(mm2)
            on("tensor", c_sc)
            p_sc = cPE.tok()
            if tail_emit[0] is not None:
                tail_emit[0]()

            def c_smsk(ve, jp=jp):
                W(ve, p_sc, tok_cst)
                if jp == 0:
                    W(ve, st["scr_mid_free"], xbf_free[0])
                cDV.inc(ve.tensor_tensor(
                    out=vSM.rearrange("p (n l) -> p n l", n=NCK),
                    in0=pbA.ap().rearrange("p (n l) -> p n l", n=NCK),
                    in1=cb_mneg[:, None, :].broadcast_to([P, NCK, 64]),
                    op=ALU.add))
            on("vector", c_smsk)
            v_sm = cDV.tok()
            bankfree["A01"] = v_sm

            def c_sexp(se):
                W(se, v_sm)
                cAC.inc(se.activation(vESC, vSM, AF.Exp, scale=0.125))
            on("scalar", c_sexp)
            a_esc = cAC.tok()

            def c_sden(te):
                W(te, a_esc)
                bw(te, "C", "D")
                for c in range(8):
                    mm = te.matmul(pbC[:, 2 * c:2 * c + 2],
                                   vESC[:, c * 128:(c + 1) * 128],
                                   cb_ob.ap(), start=True, stop=True)
                cPE.inc(mm)
            on("tensor", c_sden)
            p_sd = cPE.tok()

            def c_srd(ve):
                W(ve, p_sd)
                with nc.allow_low_precision(reason="bf16 softmax recip"):
                    cDV.inc(ve.reciprocal(vRDT.ap(), pbC[:, 0:16]))
            on("vector", c_srd)
            v_srd = cDV.tok()

            def emit_tail(jp=jp, v_srd=v_srd, afree=st["attn_free"]):
                def c_srt(te):
                    W(te, v_srd)
                    for c in range(8):
                        dst = (pbD[0:2, c * 128:(c + 1) * 128] if c < 4 else
                               pbC[0:2, (c - 4) * 128:(c - 3) * 128])
                        mm = te.matmul(dst, vRDT[:, 2 * c:2 * c + 2],
                                       cb_idb.ap(), start=True, stop=True)
                    cPE.inc(mm)
                on("tensor", c_srt)
                p_srt = cPE.tok()

                def c_sre(ve):
                    W(ve, p_srt)
                    with nc.allow_low_precision(reason="bf16 softmax recip"):
                        cDV.inc(ve.tensor_scalar(out=vRDNb[:, 0:512],
                                                 in0=pbD[0:2, :],
                                                 scalar1=1.0, scalar2=0.0,
                                                 op0=ALU.mult, op1=ALU.add))
                        cDV.inc(ve.tensor_scalar(out=vRDNb[:, 512:1024],
                                                 in0=pbC[0:2, :], scalar1=1.0,
                                                 scalar2=0.0, op0=ALU.mult,
                                                 op1=ALU.add))
                on("vector", c_sre)
                v_sre = cDV.tok()
                bankfree["C"] = v_sre
                bankfree["D"] = v_sre

                def c_srdb(te):
                    W(te, v_sre)
                    bw(te, "B2", "B3")
                    cPE.inc(te.matmul(pbB[:, 0:512], cb_b2.ap(),
                                      vRDNb[:, 0:512], start=True, stop=True))
                    cPE.inc(te.matmul(pbB[:, 512:1024], cb_b2.ap(),
                                      vRDNb[:, 512:1024], start=True,
                                      stop=True))
                on("tensor", c_srdb)
                p_srdb = cPE.tok()

                def c_pn(ve):
                    W(ve, p_srdb)
                    cDV.inc(ve.tensor_tensor(out=vPRB[:, 0:512],
                                             in0=vESC[:, 0:512],
                                             in1=pbB[:, 0:512], op=ALU.mult))
                    cDV.inc(ve.tensor_tensor(out=vPRB[:, 512:1024],
                                             in0=vESC[:, 512:1024],
                                             in1=pbB[:, 512:1024],
                                             op=ALU.mult))
                on("vector", c_pn)
                v_prb = cDV.tok()
                bankfree["B2"] = v_prb
                bankfree["B3"] = v_prb

                def c_at(te, jp=jp, v_prb=v_prb):
                    W(te, v_prb, v_done)
                    bw(te, "E")
                    for n in range(NCK):
                        osl = slice(n * 64, (n + 1) * 64)
                        te.matmul(pbE[0:64, 0:64], vPRB[0:64, osl],
                                  vV[0:64, n, jp * P:jp * P + 64],
                                  start=(n == 0), stop=(n == NCK - 1),
                                  tile_position=(0, 0))
                        mm2 = te.matmul(pbE[64:128, 0:64], vPRB[64:128, osl],
                                        vV[64:128, n, jp * P + 64:(jp + 1) * P],
                                        start=(n == 0), stop=(n == NCK - 1),
                                        tile_position=(64, 64))
                        if n == NCK - 1:
                            cPE.inc(mm2)
                on("tensor", c_at)
                p_at = cPE.tok()

                def c_atev(se, jp=jp, afree=afree, p_at=p_at):
                    W(se, p_at)
                    if jp == 0:
                        W(se, afree)
                    cAC.inc(se.copy(attn_sb[:, jp, :], pbE[:, 0:64]))
                on("scalar", c_atev)
                bankfree["E"] = cAC.tok()
            tail_emit[0] = emit_tail
        tail_emit[0]()
        at_done = cAC.tok()
        mid_pe = cPE.tok()

        def c_wp(ge, b=b):
            W(ge, g_done_pe)
            dma(chGC, ge, vW, wpT_d.rearrange("(j p) f -> p j f", p=P))
        on("gpsimd", c_wp)
        tok_wp = use(chGC)

        if DEBUG and b == 1:
            dg_g = nc.dram_tensor("dbg_gates", [P, CH, TB], BF16,
                                  kind="ExternalOutput").ap()
            dg_w = nc.dram_tensor("dbg_wp", [P, CH, DM], BF16,
                                  kind="ExternalOutput").ap()
            def dbg_mid(sy):
                W(sy, g_done, tok_wp)
                dma(chOUT, sy, dg_g, vG)
                dma(chOUT, sy, dg_w, vW2)
            on("sync", dbg_mid)
            st["dbg_mid_tok"] = use(chOUT)

        resid_toks = [None] * NT
        x2sp_toks = [None] * NT

        def emit_A(jt, b=b):
            sl2 = jt % 2
            for jp in range(JP):
                def c_wt(te, jt=jt, jp=jp):
                    W(te, at_done, g_done)
                    if jp == 0 and jt == 0:
                        W(te, mid_pe)
                    bw(te, "F")
                    te.matmul(pbF[0:64, :], attn_sb[0:64, jp, :],
                              vG[0:64, jp, jt * TT:(jt + 1) * TT],
                              start=True, stop=True, tile_position=(0, 0))
                    mm2 = te.matmul(pbF[64:128, :], attn_sb[64:128, jp, :],
                                    vG[64:128, jp, jt * TT:(jt + 1) * TT],
                                    start=True, stop=True, tile_position=(64, 64))
                    cPE.inc(mm2)
                on("tensor", c_wt)
                p_wt = cPE.tok()

                def c_wtev(se, jp=jp, jt=jt, wtfree=st["wt_free"]):
                    W(se, p_wt)
                    if jp == 0:
                        W(se, wtfree)
                        if jt == 0:
                            W(se, mid_pe, apply_toks[NT - 1])
                    cAC.inc(se.copy(vWT[:, jp, :], pbF.ap()))
                on("scalar", c_wtev)
                bankfree["F"] = cAC.tok()
            wt_done = cAC.tok()

            def c_xt(sy, b=b, jt=jt, sl2=sl2, xtfree=st["xt_free2"][sl2]):
                W(sy, xtfree, g_done_pe)
                if jt < 2:
                    W(sy, apply_toks[NT - 1])
                dma(chX, sy, vXTd[sl2],
                    x_d[b, :, :, jt * TT:(jt + 1) * TT].rearrange("j p t -> p j t"))
            on("sync", c_xt)
            tok_xt = use(chX)

            for jo in range(CH):
                bank = "E" if jo % 2 == 0 else "C"
                def c_pmm(te, jt=jt, jo=jo, bank=bank):
                    W(te, wt_done, tok_wp)
                    bw(te, bank)
                    dst = pbE.ap() if bank == "E" else pbC.ap()
                    for jk in range(CH):
                        mm = te.matmul(dst, vW[:, jk, jo * P:(jo + 1) * P],
                                       vWT[:, jk, :], start=(jk == 0),
                                       stop=(jk == CH - 1))
                        if jk == CH - 1:
                            cPE.inc(mm)
                on("tensor", c_pmm)
                p_p = cPE.tok()

                def c_pev(ve, b=b, jt=jt, jo=jo, bank=bank, sl2=sl2,
                          x2free=st["x2_free2"][sl2]):
                    W(ve, p_p, tok_xt)
                    if jo == 0:
                        W(ve, x2free)
                    src = pbE.ap() if bank == "E" else pbC.ap()
                    cDV.inc(ve.scalar_tensor_tensor(
                        out=vX2d[sl2][:, jo, :], in0=src,
                        scalar=gapm[:, jo, b:b + 1], in1=vXTd[sl2][:, jo, :],
                        op0=ALU.mult, op1=ALU.add))
                on("vector", c_pev)
                bankfree[bank] = cDV.tok()
            resid_toks[jt] = cDV.tok()
            st["wt_free"] = cPE.tok()

            def c_x2sp(sy, b=b, jt=jt, sl2=sl2):
                W(sy, resid_toks[jt])
                dma(chSP, sy, x2_dram[b][:, :, jt * TT:(jt + 1) * TT], vX2d[sl2])
            on("sync", c_x2sp)
            x2sp_toks[jt] = use(chSP)

        def emit_B(jt, b=b):
            sl2 = jt % 2
            x2v = vX2d[sl2]
            h2v_ = vXTd[sl2]
            def c2_sum1(te, jt=jt, x2v=x2v):
                W(te, resid_toks[jt])
                bw(te, "B2")
                for j in range(CH):
                    mm = te.matmul(pbB[0:1, 0:512], cb_o1cf.ap(), x2v[:, j, :],
                                   start=(j == 0), stop=(j == CH - 1))
                    if j == CH - 1:
                        cPE.inc(mm)
            on("tensor", c2_sum1)
            p2_s1 = cPE.tok()

            sq_toks = []
            mm_toks = []
            for grp in range(4):
                def c2_sqv(ve, grp=grp, x2v=x2v,
                           prev=(mm_toks[grp - 2] if grp >= 2 else None)):
                    W(ve, prev)
                    if grp < 2:
                        W(ve, resid_toks[jt])
                    buf = apt if grp % 2 == 0 else apt2
                    cDV.inc(ve.tensor_tensor(
                        out=buf.ap().rearrange("p (a t) -> p a t", a=2),
                        in0=x2v[:, grp * 2:grp * 2 + 2, :],
                        in1=x2v[:, grp * 2:grp * 2 + 2, :], op=ALU.mult))
                on("vector", c2_sqv)
                sq_toks.append(cDV.tok())

                def c2_sqm(te, grp=grp, tok=sq_toks[grp]):
                    W(te, tok)
                    if grp == 0:
                        bw(te, "B3")
                    buf = apt if grp % 2 == 0 else apt2
                    for h in range(2):
                        mm = te.matmul(pbB[0:1, 512:1024], cb_o1cf.ap(),
                                       buf[:, h * 512:(h + 1) * 512],
                                       start=(grp == 0 and h == 0),
                                       stop=(grp == 3 and h == 1))
                        if h == 1:
                            cPE.inc(mm)
                on("tensor", c2_sqm)
                mm_toks.append(cPE.tok())
            p2_s2 = cPE.tok()

            def c2_rows(ve):
                W(ve, p2_s1, p2_s2)
                cDV.inc(ve.tensor_scalar_mul(r_m, pbB[0:1, 0:512], 1.0 / DM))
                ve.wait_ge(cDV.h, cDV.n)
                cDV.inc(ve.tensor_tensor(out=r_var, in0=r_m, in1=r_m,
                                         op=ALU.mult))
                ve.wait_ge(cDV.h, cDV.n)
                cDV.inc(ve.scalar_tensor_tensor(out=r_var, in0=pbB[0:1, 512:1024],
                                                scalar=1.0 / DM, in1=r_var,
                                                op0=ALU.mult, op1=ALU.subtract))
            on("vector", c2_rows)
            v2_var = cDV.tok()

            def c2_rowa(se):
                W(se, v2_var)
                cAC.inc(se.activation(r_var, r_var, AF.Ln, bias=cb_kc[0:1, 1:2]))
                se.wait_ge(cAC.h, cAC.n)
                cAC.inc(se.activation(r_rstd, r_var, AF.Exp, scale=-0.5))
            on("scalar", c2_rowa)
            a2_rstd = cAC.tok()

            def c2_row2(ve):
                W(ve, a2_rstd)
                cDV.inc(ve.scalar_tensor_tensor(out=r_negmr, in0=r_m,
                                                scalar=-1.0, in1=r_rstd,
                                                op0=ALU.mult, op1=ALU.mult))
            on("vector", c2_row2)
            v2_rows = cDV.tok()

            def c2_bcmm(te):
                W(te, v2_rows)
                cPE.inc(te.matmul(pbB[:, 0:512], cb_o1f.ap(), r_rstd,
                                  start=True, stop=True))
                cPE.inc(te.matmul(pbB[:, 512:1024], cb_o1f.ap(), r_negmr,
                                  start=True, stop=True))
            on("tensor", c2_bcmm)
            p2_bc = cPE.tok()

            def c2_apply(ve, b=b, jt=jt, x2v=x2v, h2v_=h2v_):
                W(ve, p2_bc)
                for j in range(CH):
                    ap1 = apt[:, 0:512]
                    cDV.inc(ve.tensor_tensor(out=ap1, in0=x2v[:, j, :],
                                             in1=pbB[:, 0:512], op=ALU.mult))
                    ve.wait_ge(cDV.h, cDV.n)
                    cDV.inc(ve.tensor_tensor(out=ap1, in0=ap1,
                                             in1=pbB[:, 512:1024], op=ALU.add))
                    ve.wait_ge(cDV.h, cDV.n)
                    cDV.inc(ve.tensor_scalar(out=h2v_[:, j, :], in0=ap1,
                                             scalar1=cbpm[:, j, b:b + 1],
                                             scalar2=sbpm[:, j, b:b + 1],
                                             op0=ALU.mult, op1=ALU.add))
                    ve.wait_ge(cDV.h, cDV.n)
                cDV.inc(ve.tensor_reduce(out=tmax.ap(), in_=h2v_, axis=AX.XY,
                                         op=ALU.max))
                ve.wait_ge(cDV.h, cDV.n)
                cDV.inc(ve.tensor_tensor(out=hmax.ap(), in0=hmax.ap(),
                                         in1=tmax.ap(), op=ALU.max))
                ve.wait_ge(cDV.h, cDV.n)
            on("vector", c2_apply)
            v2_h2 = cDV.tok()
            bankfree["B2"] = v2_h2
            bankfree["B3"] = v2_h2
            st["x2_free2"][sl2] = v2_h2

            def c2_h2sp(sy, b=b, jt=jt, sl2=sl2):
                W(sy, v2_h2, x2sp_toks[jt])
                dma(chSP, sy, h2_dram[b][:, :, jt * TT:(jt + 1) * TT], vXTd[sl2])
            on("sync", c2_h2sp)
            tok_h2sp = use(chSP)
            st.setdefault("h2sp_tok", {})[(b, jt)] = tok_h2sp
            st["xt_free2"][sl2] = tok_h2sp
            st["scr_mid_free"] = cPE.tok()

        for jt in range(NT):
            emit_A(jt)
            if jt > 0:
                emit_B(jt - 1)
        emit_B(NT - 1)

        for jt in range(NT):
            st["h_free"][jt] = g_done_pe
        st["k_free"] = mid_pe
        st["v_free"] = mid_pe
        st["kv_free"] = mid_pe
        st["g_free"] = cPE.tok()
        st["w_free"] = cPE.tok()
        st["w2_free"] = cPE.tok()
        bankfree["E"] = cPE.tok()
        bankfree["F"] = cPE.tok()
        st["attn_free"] = cPE.tok()
        st["c5_done"] = cDV.tok()
        st["xt_free"] = st["xt_free2"][0]
        st["x2_free"] = st["x2_free2"][1]
        st["scr_free_b"] = st["wt_free"]
        st["str_free_b"] = (st["x2_free2"][0], x2sp_toks[2])

    h2_all_sp = use(chSP)

    def b_ld3(ge):
        W(ge, v_ag3)
        dma(chCC, ge, sc40.ap(), ago_sc)
    on("gpsimd", b_ld3)
    tok_sc40 = use(chCC)

    def b_tr3(te):
        W(te, tok_sc40)
        bw(te, "D")
        cPE.inc(te.transpose(pbD[0:P, 0:40], sc40.ap(), cb_id[0:40, 0:40]))
    on("tensor", b_tr3)
    p_t3 = cPE.tok()

    def b_te3(ve):
        W(ve, p_t3)
        cDV.inc(ve.tensor_copy(scT.ap(), pbD[0:P, 0:40]))
        ve.wait_ge(cDV.h, cDV.n)
        sv = scT.ap().rearrange("p (r k) -> p r k", r=NCORES)
        cDV.inc(ve.tensor_copy(am1pm.ap().rearrange("p (r k) -> p r k", r=NCORES),
                               sv[:, :, 0:4]))
        cDV.inc(ve.tensor_copy(am2pm.ap(), sv[:, :, 4]))
    on("vector", b_te3)
    bankfree["D"] = cDV.tok()
    v_ampm = cDV.tok()



    def g1_tr(te):
        W(te, cDV.tok())
        bw(te, "D")
        cPE.inc(te.transpose(pbD[0:1, 0:P], hmax.ap(), cb_id.ap()))
    on("tensor", g1_tr)
    p_g1t = cPE.tok()

    def g1_r(ve):
        W(ve, p_g1t)
        cDV.inc(ve.tensor_reduce(out=gm1.ap(), in_=pbD[0:1, 0:P], axis=AX.X,
                                 op=ALU.max))
        ve.wait_ge(cDV.h, cDV.n)
        cDV.inc(ve.tensor_scalar_max(gm1.ap(), gm1.ap(), QEPS))
    on("vector", g1_r)
    bankfree["D"] = cDV.tok()
    v_g1 = cDV.tok()

    def g1_ar(ge):
        W(ge, v_g1)
        dma(chCC, ge, ar1_in, gm1.ap())
        ge.wait_ge(chCC.h, chCC.n)
        chCC.pending_epoch = False
        cCC.inc(ge.collective_compute("AllReduce", ALU.max, replica_groups=RG,
                                      ins=[ar1_in], outs=[ar1_out]))
        ge.wait_ge(cCC.h, cCC.n)
        dma(chCC, ge, gm1.ap(), ar1_out)
    on("gpsimd", g1_ar)
    tok_gm1 = use(chCC)

    def g1_rr(ve):
        W(ve, tok_gm1)
        cDV.inc(ve.reciprocal(r1row[0:1, 0:1], gm1.ap()))
        ve.wait_ge(cDV.h, cDV.n)
        cDV.inc(ve.tensor_scalar_mul(r1row[0:1, 0:1], r1row[0:1, 0:1], 255.0))
        cDV.inc(ve.tensor_scalar_mul(r1row[0:1, 1:2], gm1.ap(), 1.0 / 255.0))
    on("vector", g1_rr)
    v_r1r = cDV.tok()

    def g1_mm(te):
        W(te, v_r1r)
        bw(te, "D")
        cPE.inc(te.matmul(pbD[:, 0:2], cb_o1f.ap(), r1row.ap(),
                          start=True, stop=True))
    on("tensor", g1_mm)
    p_r1 = cPE.tok()

    def g1_c(ve):
        W(ve, p_r1, v_ampm)
        cDV.inc(ve.tensor_copy(r1pm.ap(), pbD[:, 0:2]))
        ve.wait_ge(cDV.h, cDV.n)
        cDV.inc(ve.tensor_scalar(out=dq1pm.ap(), in0=am1pm.ap(),
                                 scalar1=r1pm[:, 1:2], scalar2=1.0 / 127.0,
                                 op0=ALU.mult, op1=ALU.mult))
        cDV.inc(ve.reciprocal(t32a.ap(), am1pm.ap()))
        ve.wait_ge(cDV.h, cDV.n)
        cDV.inc(ve.tensor_scalar(out=t32b.ap(), in0=cb_b1t.ap(),
                                 scalar1=r1pm[:, 0:1], scalar2=127.0,
                                 op0=ALU.mult, op1=ALU.mult))
        ve.wait_ge(cDV.h, cDV.n)
        cDV.inc(ve.tensor_tensor(out=qb1pm.ap(), in0=t32b.ap(), in1=t32a.ap(),
                                 op=ALU.mult))
        ve.wait_ge(cDV.h, cDV.n)
        cDV.inc(ve.tensor_scalar(out=qb1pm.ap(), in0=qb1pm.ap(),
                                 scalar1=MAGIC, scalar2=MAGIC,
                                 op0=ALU.add, op1=ALU.subtract))
        ve.wait_ge(cDV.h, cDV.n)
        cDV.inc(ve.tensor_tensor(out=qbdq1.ap(), in0=qb1pm.ap(),
                                 in1=dq1pm.ap(), op=ALU.mult))
        ve.wait_ge(cDV.h, cDV.n)
    on("vector", g1_c)
    bankfree["D"] = cDV.tok()
    v_q1c = cDV.tok()

    def f_qw1l(seq):
        W(seq, v_ag1)
        W(seq, st["kv_free"])
        if DEBUG:
            W(seq, st.get("dbg_mid_tok"))
        for j in range(6):
            chW.inc(seq.dma_start_transpose(vQW1[:, j, :],
                                            ago_qw1[:, j * P:(j + 1) * P]))
        W(seq, st["w_free"])
        for j in range(6, CH):
            chW.inc(seq.dma_start_transpose(vQW1[:, j, :],
                                            ago_qw1[:, j * P:(j + 1) * P]))
    on("scalar", f_qw1l)
    tok_qw1 = use(chW)

    fst = {"h2_free": [None, None], "qh_free": [None, None],
           "ust_free": [None, None]}
    prep = {}

    def emit_prep(tg):
        b, jt = tg // 4, tg % 4
        sl2 = tg % 2
        h2b = vH2T[:, sl2]

        def f1_h2l(seq, b=b, jt=jt, h2b=h2b, prev=fst["h2_free"][sl2],
                   first=(tg == 0), sp_tok=st["h2sp_tok"][(tg // 4, tg % 4)]):
            W(seq, sp_tok, prev)
            if first:
                W(seq, st["w_free"], st["xt_free2"][0], st["xt_free2"][1],
                  st["c5_done"])
            dma(chUL, seq, h2b, h2_dram[b][:, :, jt * TT:(jt + 1) * TT])
        on("scalar", f1_h2l)
        tok_h2 = use(chUL)

        def f1_rd(se, h2b=h2b):
            W(se, tok_h2, v_q1c)
            hb = h2b.rearrange("p j t -> p (j t)")
            cAC.inc(se.activation(hb, hb, AF.Copy, bias=MAGIC,
                                  scale=r1pm[:, 0:1]))
        on("scalar", f1_rd)
        prep[tg] = cAC.tok()

    emit_prep(0)
    for tg in range(8):
        b, jt = tg // 4, tg % 4
        sl2 = tg % 2
        h2b = vH2T[:, sl2]
        qh = vQH[:, sl2]
        a_rd = prep[tg]

        def f1_cl(ve, h2b=h2b, qh=qh, a_rd=a_rd, prev=fst["qh_free"][sl2],
                  first=(tg == 0)):
            W(ve, a_rd, prev)
            if first:
                W(ve, st["w_free"], st["xt_free2"][1])
            hb = h2b.rearrange("p j t -> p (j t)")
            cDV.inc(ve.tensor_scalar(out=hb, in0=hb, scalar1=MAGIC, scalar2=0.0,
                                     op0=ALU.subtract, op1=ALU.max))
            ve.wait_ge(cDV.h, cDV.n)
            cDV.inc(ve.tensor_scalar_min(qh.rearrange("p j t -> p (j t)"),
                                         hb, 255.0))
        on("vector", f1_cl)
        v_qh = cDV.tok()
        fst["h2_free"][sl2] = v_qh
        for _f in fst.get("pending_um", []):
            _f()
        fst["pending_um"] = []

        for g in range(4):
            ub = vUST[:, g % 2]
            for fi in range(CH):
                ft = g * CH + fi
                bank = "E" if ft % 2 == 0 else "F"
                def f1_mm(te, ft=ft, qh=qh, bank=bank, first=(tg == 0 and ft == 0)):
                    W(te, v_qh, tok_qw1)
                    bw(te, bank)
                    dst = pbE.ap() if bank == "E" else pbF.ap()
                    for jk in range(CH):
                        mm = te.matmul(dst, vQW1[:, jk, ft * P:(ft + 1) * P],
                                       qh[:, jk, :], start=(jk == 0),
                                       stop=(jk == CH - 1))
                        if jk == CH - 1:
                            cPE.inc(mm)
                on("tensor", f1_mm)
                p_f1 = cPE.tok()

                if ft % 4 != 3:
                    def f1_ev(se, ft=ft, fi=fi, ub=ub, bank=bank,
                              prev=fst["ust_free"][g % 2] if fi == 0 else None,
                              first=(tg == 0 and g < 2 and fi == 0)):
                        if prev is not None:
                            ptok, pum = prev
                            W(se, p_f1, v_q1c, ptok, pum[0])
                        else:
                            W(se, p_f1, v_q1c)
                        if first:
                            W(se, st["x2_free2"][0], st["x2_free2"][1],
                              st["c5_done"])
                        src = pbE.ap() if bank == "E" else pbF.ap()
                        cAC.inc(se.activation(ub[:, fi, :], src, AF.Relu,
                                              bias=qbdq1[:, ft:ft + 1],
                                              scale=dq1pm[:, ft:ft + 1]))
                    on("scalar", f1_ev)
                    bankfree[bank] = cAC.tok()
                else:
                    def f1_evv(ve, ft=ft, fi=fi, ub=ub, bank=bank):
                        W(ve, p_f1, v_q1c)
                        src = pbE.ap() if bank == "E" else pbF.ap()
                        cDV.inc(ve.tensor_scalar(out=apt[:, 0:512], in0=src,
                                                 scalar1=qb1pm[:, ft:ft + 1],
                                                 scalar2=0.0,
                                                 op0=ALU.add, op1=ALU.max))
                        ve.wait_ge(cDV.h, cDV.n)
                        cDV.inc(ve.tensor_scalar_mul(ub[:, fi, :], apt[:, 0:512],
                                                     dq1pm[:, ft:ft + 1]))
                    on("vector", f1_evv)
                    bankfree[bank] = cDV.tok()
            a_ug = (cAC.tok(), cDV.tok())

            um_cell = [None]

            def emit_um(a_ug=a_ug, ub=ub, um_cell=um_cell):
                def f1_um(ve):
                    W(ve, *a_ug)
                    cDV.inc(ve.tensor_reduce(out=tmax.ap(), in_=ub, axis=AX.XY,
                                             op=ALU.max))
                    ve.wait_ge(cDV.h, cDV.n)
                    cDV.inc(ve.tensor_tensor(out=umax.ap(), in0=umax.ap(),
                                             in1=tmax.ap(), op=ALU.max))
                    ve.wait_ge(cDV.h, cDV.n)
                on("vector", f1_um)
                um_cell[0] = cDV.tok()
            if g < 2:
                emit_um()
            else:
                fst.setdefault("pending_um", []).append(emit_um)

            def f1_sp(sy, tg=tg, g=g, ub=ub):
                W(sy, *a_ug)
                dma(chSP, sy, u_dram[tg, g], ub)
            on("sync", f1_sp)
            tok_us = use(chSP)
            fst["ust_free"][g % 2] = (tok_us, um_cell)
            if g == 1 and tg < 7:
                emit_prep(tg + 1)
        fst["qh_free"][sl2] = cPE.tok()
    for _f in fst.get("pending_um", []):
        _f()
    fst["pending_um"] = []
    f1_sp_all = use(chSP)
    f1_pe_all = cPE.tok()

    def g2_tr(te):
        W(te, cDV.tok())
        bw(te, "D")
        cPE.inc(te.transpose(pbD[0:1, 0:P], umax.ap(), cb_id.ap()))
    on("tensor", g2_tr)
    p_g2t = cPE.tok()

    def g2_r(ve):
        W(ve, p_g2t)
        cDV.inc(ve.tensor_reduce(out=gm2.ap(), in_=pbD[0:1, 0:P], axis=AX.X,
                                 op=ALU.max))
        ve.wait_ge(cDV.h, cDV.n)
        cDV.inc(ve.tensor_scalar_max(gm2.ap(), gm2.ap(), QEPS))
    on("vector", g2_r)
    bankfree["D"] = cDV.tok()
    v_g2 = cDV.tok()

    def g2_ar(ge):
        W(ge, v_g2)
        dma(chCC, ge, ar2_in, gm2.ap())
        ge.wait_ge(chCC.h, chCC.n)
        chCC.pending_epoch = False
        cCC.inc(ge.collective_compute("AllReduce", ALU.max, replica_groups=RG,
                                      ins=[ar2_in], outs=[ar2_out]))
        ge.wait_ge(cCC.h, cCC.n)
        dma(chCC, ge, gm2.ap(), ar2_out)
    on("gpsimd", g2_ar)
    tok_gm2 = use(chCC)

    def g2_rr(ve):
        W(ve, tok_gm2)
        cDV.inc(ve.reciprocal(r2row[0:1, 0:1], gm2.ap()))
        ve.wait_ge(cDV.h, cDV.n)
        cDV.inc(ve.tensor_scalar_mul(r2row[0:1, 0:1], r2row[0:1, 0:1], 255.0))
        cDV.inc(ve.tensor_scalar_mul(r2row[0:1, 1:2], gm2.ap(), 1.0 / 255.0))
    on("vector", g2_rr)
    v_r2r = cDV.tok()

    def g2_mm(te):
        W(te, v_r2r)
        bw(te, "D")
        cPE.inc(te.matmul(pbD[:, 0:2], cb_o1f.ap(), r2row.ap(),
                          start=True, stop=True))
    on("tensor", g2_mm)
    p_r2 = cPE.tok()

    def g2_c(ve):
        W(ve, p_r2, v_ampm, v_mpm)
        cDV.inc(ve.tensor_copy(r2pm.ap(), pbD[:, 0:2]))
        ve.wait_ge(cDV.h, cDV.n)
        cDV.inc(ve.tensor_scalar(out=dq2pm.ap(), in0=am2pm.ap(),
                                 scalar1=r2pm[:, 1:2], scalar2=1.0 / 127.0,
                                 op0=ALU.mult, op1=ALU.mult))
        cDV.inc(ve.reciprocal(t32a[:, 0:CH], am2pm.ap()))
        ve.wait_ge(cDV.h, cDV.n)
        cDV.inc(ve.tensor_scalar(out=t32b[:, 0:CH], in0=cb_b2t.ap(),
                                 scalar1=r2pm[:, 0:1], scalar2=127.0,
                                 op0=ALU.mult, op1=ALU.mult))
        ve.wait_ge(cDV.h, cDV.n)
        cDV.inc(ve.tensor_tensor(out=qb2pm.ap(), in0=t32b[:, 0:CH],
                                 in1=t32a[:, 0:CH], op=ALU.mult))
        ve.wait_ge(cDV.h, cDV.n)
        cDV.inc(ve.tensor_scalar(out=qb2pm.ap(), in0=qb2pm.ap(),
                                 scalar1=MAGIC, scalar2=MAGIC,
                                 op0=ALU.add, op1=ALU.subtract))
        ve.wait_ge(cDV.h, cDV.n)
        cDV.inc(ve.tensor_tensor(out=gbdq2.ap(), in0=gbpm.ap(),
                                 in1=dq2pm.ap()[:, :, None].broadcast_to([P, CH, B]),
                                 op=ALU.mult))
        ve.wait_ge(cDV.h, cDV.n)
    on("vector", g2_c)
    bankfree["D"] = cDV.tok()
    v_q2c = cDV.tok()

    def q2_tr(te):
        W(te, v_q2c)
        bw(te, "D")
        cPE.inc(te.transpose(pbD[0:CH, 0:P], qb2pm.ap(), cb_id.ap()))
    on("tensor", q2_tr)
    p_q2t = cPE.tok()

    def q2_ev(ve):
        W(ve, p_q2t)
        cDV.inc(ve.tensor_copy(sc40[0:CH, :], pbD[0:CH, 0:P]))
    on("vector", q2_ev)
    bankfree["D"] = cDV.tok()
    v_q2rows = cDV.tok()

    def q2_bb(sy):
        W(sy, v_q2rows, f1_pe_all)
        dma(chBC, sy, qb2_dram, sc40[0:CH, :])
        sy.wait_ge(chBC.h, chBC.n)
        chBC.pending_epoch = False
        dma(chBC, sy, vQB2R, qb2_dram.rearrange("r c -> (r c)")[None, :])
    on("sync", q2_bb)
    tok_q2r = use(chBC)

    def q2_hl(ve):
        W(ve, tok_q2r)
        cDV.inc(ve.tensor_copy(qb2hi, vQB2R))
        ve.wait_ge(cDV.h, cDV.n)
        cDV.inc(ve.tensor_tensor(out=vQB2R, in0=vQB2R, in1=qb2hi,
                                 op=ALU.subtract))
        ve.wait_ge(cDV.h, cDV.n)
        cDV.inc(ve.tensor_copy(qb2lo, vQB2R))
    on("vector", q2_hl)
    v_hl = cDV.tok()

    def f_qw2l(seq):
        W(seq, v_ag2, f1_pe_all)
        for j in range(32):
            chW.inc(seq.dma_start_transpose(vQW2[:, j, :],
                                            ago_qw2[:, j * P:(j + 1) * P]))
    on("scalar", f_qw2l)
    tok_qw2 = use(chW)

    f2 = {"uin_free": None, "q2_free": [None, None], "x2_free": [None, None],
          "x3_free": [None, None]}
    for ti in range(16):
        tg, th = ti // 2, ti % 2
        b, jt = tg // 4, tg % 4
        t0 = jt * TT + th * 256
        q2 = vQ2a if ti % 2 == 0 else vQ2b
        x2b = vX2F[:, ti % 2]
        x3b = vX3[:, ti % 2]

        def f2_ul(seq, tg=tg, th=th, prev=f2["uin_free"], first=(ti == 0)):
            W(seq, f1_sp_all, prev)
            if first:
                W(seq, fst["qh_free"][0], fst["qh_free"][1])
                W(seq, cDV.tok())
            for g in range(4):
                dma(chUL, seq, vUIN[:, g * CH:(g + 1) * CH, :],
                    u_dram[tg, g][:, :, th * 256:(th + 1) * 256])
        on("scalar", f2_ul)
        tok_ui = use(chUL)

        def f2_x2l(seq, b=b, t0=t0, x2b=x2b, prev=f2["x2_free"][ti % 2],
                   first=(ti < 2)):
            W(seq, prev)
            if first:
                W(seq, f1_sp_all)
            dma(chUL, seq, x2b, x2_dram[b][:, :, t0:t0 + 256])
        on("scalar", f2_x2l)
        tok_x2 = use(chUL)

        def f2_rd(se):
            W(se, tok_ui, v_q2c)
            uf = vUIN.rearrange("p j t -> p (j t)")
            cAC.inc(se.activation(uf, uf, AF.Copy, bias=MAGIC,
                                  scale=r2pm[:, 0:1]))
            se.wait_ge(cAC.h, cAC.n)
            cAC.inc(se.activation(uf, uf, AF.Relu, bias=-MAGIC))
        on("scalar", f2_rd)
        a_f2r = cAC.tok()

        def f2_cl(ve, q2=q2, prev=f2["q2_free"][ti % 2], first=(ti < 2)):
            W(ve, a_f2r, prev)
            if first:
                W(ve, f1_pe_all, v_hl)
            uf = vUIN.rearrange("p j t -> p (j t)")
            cDV.inc(ve.tensor_scalar_min(q2.rearrange("p j t -> p (j t)"),
                                         uf, 255.0))
        on("vector", f2_cl)
        v_q2t = cDV.tok()
        f2["uin_free"] = v_q2t

        for dc in range(CH):
            bank = "E" if dc % 2 == 0 else "F"
            def f2_mm(te, dc=dc, q2=q2, bank=bank):
                W(te, v_q2t, tok_qw2, v_hl, tok_cc)
                bw(te, bank)
                dst = (pbE.ap() if bank == "E" else pbF.ap())[:, 0:256]
                for fj in range(32):
                    te.matmul(dst, vQW2[:, fj, dc * P:(dc + 1) * P],
                              q2[:, fj, :], start=(fj == 0), stop=False)
                te.matmul(dst, qb2hi[0:1, dc * P:(dc + 1) * P], cb_orb.ap(),
                          start=False, stop=False)
                mm = te.matmul(dst, qb2lo[0:1, dc * P:(dc + 1) * P], cb_orb.ap(),
                               start=False, stop=True)
                cPE.inc(mm)
            on("tensor", f2_mm)
            p_f2 = cPE.tok()

            def f2_ev(ve, dc=dc, b=b, x2b=x2b, x3b=x3b, bank=bank,
                      prev=f2["x3_free"][ti % 2]):
                W(ve, p_f2, tok_x2, v_q2c)
                if dc == 0:
                    W(ve, prev)
                src = (pbE.ap() if bank == "E" else pbF.ap())[:, 0:256]
                cDV.inc(ve.scalar_tensor_tensor(
                    out=x3b[:, dc, :], in0=src,
                    scalar=gbdq2[:, dc, b:b + 1], in1=x2b[:, dc, :],
                    op0=ALU.mult, op1=ALU.add))
            on("vector", f2_ev)
            bankfree[bank] = cDV.tok()
        v_x3 = cDV.tok()
        f2["x2_free"][ti % 2] = v_x3
        f2["q2_free"][ti % 2] = cPE.tok()

        def f2_out(sy, b=b, t0=t0, x3b=x3b):
            W(sy, v_x3)
            dma(chOUT, sy, y_d[b].rearrange("j p t -> p j t")[:, :, t0:t0 + 256],
                x3b)
        on("sync", f2_out)
        f2["x3_free"][ti % 2] = use(chOUT)

    if DEBUG:
        dbg_outs = {}
        def dmk(name, shape, dtype=F32):
            t = nc.dram_tensor("dbg_" + name, list(shape), dtype,
                               kind="ExternalOutput").ap()
            dbg_outs[name] = t
            return t
        d_x2 = dmk("x2", [B, P, CH, TB])
        d_h2 = dmk("h2", [B, P, CH, TB])
        d_u0 = dmk("u0", [4, P, CH, TT])
        smalls = [("gapm", gapm.ap()), ("gbpm", gbpm.ap()), ("capm", capm.ap()),
                  ("sapm", sapm.ap()), ("cbpm", cbpm.ap()), ("sbpm", sbpm.ap()),
                  ("am1pm", am1pm.ap()), ("dq1pm", dq1pm.ap()),
                  ("qb1pm", qb1pm.ap()), ("qbdq1", qbdq1.ap()),
                  ("dq2pm", dq2pm.ap()), ("qb2pm", qb2pm.ap()),
                  ("gbdq2", gbdq2.ap()), ("r1pm", r1pm.ap()), ("r2pm", r2pm.ap()),
                  ("gm1", gm1.ap()), ("gm2", gm2.ap()), ("hmax", hmax.ap()),
                  ("umax", umax.ap()), ("attn1", attn_sb.ap())]
        for nm, ap in smalls:
            dmk(nm, ap.shape, ap.dtype)

        def dbg_dump(sy):
            sy.wait_ge(chOUT.h, chOUT.n)
            dma(chOUT, sy, d_x2, x2_dram)
            dma(chOUT, sy, d_h2, h2_dram)
            dma(chOUT, sy, d_u0, u_dram[0])
            for nm, ap in smalls:
                dma(chOUT, sy, dbg_outs[nm], ap)
        on("sync", dbg_dump)

    def fin(sy):
        sy.wait_ge(chOUT.h, chOUT.n)
    on("sync", fin)

    es.close()
    return nc


def _host_prepare(inputs):
    x = np.asarray(inputs["L0_attn_x"])
    lq = np.asarray(inputs["L0_attn_latent_queries"])
    Wk = np.asarray(inputs["L0_attn_W_k"])
    Wv = np.asarray(inputs["L0_attn_W_v"])
    Wg = np.asarray(inputs["L0_attn_W_g"])
    Wp = np.asarray(inputs["L0_attn_W_p"])
    w1 = np.asarray(inputs["L0_ffn_w1"])
    b1 = np.asarray(inputs["L0_ffn_b1"])
    w2 = np.asarray(inputs["L0_ffn_w2"])
    b2 = np.asarray(inputs["L0_ffn_b2"])
    c = np.asarray(inputs["L0_c"])
    wmod = np.asarray(inputs["L0_wmod"])

    g64 = np.concatenate([np.arange(0, 64, 2), np.arange(1, 64, 2)])
    gidx = (np.arange(16)[:, None] * 64 + g64[None, :]).reshape(-1)

    wkT = np.ascontiguousarray(Wk[gidx, :].T)
    wvT = np.ascontiguousarray(Wv.T)
    wgT = np.ascontiguousarray(Wg.T)
    wpT = np.ascontiguousarray(Wp.T)

    q = lq[0]
    qpk = np.empty((P, JP, 64), np.float32)
    for jp in range(JP):
        qpk[0:64, jp, :] = q[:, 2 * jp, :][:, g64].T
        qpk[64:128, jp, :] = q[:, 2 * jp + 1, :][:, g64].T

    inv = 1.0 / (10000.0 ** (np.arange(0, 64, 2, dtype=np.float64) / 64.0))
    tpos = (np.arange(NCK)[:, None] * 128 + np.arange(64)[None, :]).reshape(-1)
    pfreq = inv[np.arange(P) % 32]
    ang = tpos[None, :] * pfreq[:, None]
    cost = np.cos(ang).astype(np.float32)
    sint = np.sin(ang).astype(np.float32)
    sgn = np.where((np.arange(P) % 64) < 32, -1.0, 1.0).astype(np.float32)
    sint = sint * sgn[:, None]

    mneg = np.where((np.arange(P) % 64)[:, None] <= np.arange(64)[None, :],
                    0.0, NEGBIG).astype(np.float32)
    oneblk = np.zeros((P, 2), np.float32)
    oneblk[0:64, 0] = 1.0
    oneblk[64:128, 1] = 1.0
    blk2 = np.ascontiguousarray(oneblk.T)
    ident = np.eye(P, dtype=np.float32)
    onescol = np.ones((P, 1), np.float32)
    onesrow = np.ones((1, TT), np.float32)
    kconst = np.zeros((P, 4), np.float32)
    kconst[:, 0] = MAGIC
    kconst[:, 1] = LN_EPS
    kconst[:, 2] = QEPS
    kconst[:, 3] = -MAGIC

    b1t = np.ascontiguousarray(b1.reshape(32, P).T)
    b2t = np.ascontiguousarray(b2.reshape(CH, P).T)

    in_maps = []
    for cid in range(NCORES):
        xb = x[cid * B:(cid + 1) * B]
        x_lay = np.ascontiguousarray(xb.transpose(0, 2, 1).reshape(B, CH, P, TB))
        cb = c[cid * B:(cid + 1) * B]
        cT = np.ascontiguousarray(cb.T.reshape(CH, P, B).transpose(1, 0, 2))
        in_maps.append({
            "x": x_lay,
            "wkT": wkT, "wvT": wvT, "wgT": wgT, "wpT": wpT,
            "wmod": wmod, "cT": cT,
            "w1s": np.ascontiguousarray(w1[cid * 512:(cid + 1) * 512]),
            "w2s": np.ascontiguousarray(w2[cid * P:(cid + 1) * P]),
            "b1t": b1t, "b2t": b2t,
            "qpk": qpk, "cost": cost, "sint": sint, "mneg": mneg,
            "oneblk": oneblk, "blk2": blk2, "ident": ident,
            "onescol": onescol, "onesrow": onesrow, "kconst": kconst,
        })
    return in_maps


def kernel(**inputs):
    in_maps = _host_prepare(inputs)
    nc = build_program()
    res = run_bass_kernel_spmd(nc, in_maps, core_ids=list(range(NCORES)))
    outs = []
    for cid in range(NCORES):
        y = np.asarray(res.results[cid]["y"])
        outs.append(y.transpose(0, 3, 1, 2).reshape(B, TB, DM))
    return np.concatenate(outs, axis=0).astype(np.float32)


if __name__ == "__main__":
    nc = build_program()
    print("instructions:", nc.n_instructions())
    print("sbuf remaining:", nc.sbuf_bytes_remaining)
